# version 02
# baseline (speedup 1.0000x reference)
import numpy as np
from contextlib import ExitStack

import concourse.bass as bass
import concourse.mybir as mybir
from concourse.bass_utils import run_bass_kernel_spmd
from concourse.bass_core_rust import IndirectOffsetOnAxis

F32 = mybir.dt.float32
BF16 = mybir.dt.bfloat16
U32 = mybir.dt.uint32
I16 = mybir.dt.int16
AF = mybir.ActivationFunctionType
ALU = mybir.AluOpType

P = 128
D = 1024
KD = D // P
T = 2048
NCORES = 8
SEQ = 8192

ENGS = ("sync", "tensor", "vector", "scalar", "gpsimd")


class Prog:

    def __init__(self, name="prog"):
        self.nc = bass.Bass("TRN2", target_bir_lowering=False, debug=False)
        self.nc.detect_race_conditions = False
        self.es = ExitStack()
        self.ops = {e: [] for e in ENGS}
        self.sems = {}
        self.cnt = {}
        self.name = name
        self._uid = 0
        self._guards = {}

    def uid(self):
        self._uid += 1
        return self._uid

    def sb(self, name, shape, dtype):
        g = self.nc.sbuf_tensor(name, shape, dtype)
        h = g.__enter__()
        self._guards[name] = g
        return h

    def psum(self, name, shape, dtype=F32):
        g = self.nc.psum_tensor(name, shape, dtype)
        h = g.__enter__()
        self._guards[name] = g
        return h

    def free(self, *names):
        for n in names:
            g = self._guards.pop(n)
            g.__exit__(None, None, None)

    def dram(self, name, shape, dtype, kind=None, shared=False):
        kw = {}
        if kind:
            kw["kind"] = kind
        if shared:
            kw["addr_space"] = "Shared"
        return self.nc.dram_tensor(name, shape, dtype, **kw)

    def sem(self, name):
        if name not in self.sems:
            self.sems[name] = self.es.enter_context(self.nc.semaphore(name))
            self.cnt[name] = 0
        return self.sems[name]

    def op(self, eng, fn, inc=None, amt=1):
        if inc is not None:
            sem = self.sem(inc)
            self.cnt[inc] += amt

            def wrapped(e, fn=fn, sem=sem, amt=amt):
                ins = fn(e)
                ins.then_inc(sem, amt)
            self.ops[eng].append(wrapped)
        else:
            self.ops[eng].append(fn)

    def wait(self, eng, semname, val=None):
        sem = self.sem(semname)
        v = self.cnt[semname] if val is None else val
        self.ops[eng].append(lambda e, sem=sem, v=v: e.wait_ge(sem, v))

    def dma(self, eng, out, in_, inc="dma"):
        self.op(eng, lambda e, out=out, in_=in_: e.dma_start(out=out, in_=in_),
                inc=inc, amt=16)

    def fence_dma(self, eng="sync"):
        for s in ("v", "t", "s"):
            if s in self.cnt:
                self.wait(eng, s)

    def constcol(self, value, np_=1):
        key = ("cc", float(value), np_)
        if not hasattr(self, "_ccols"):
            self._ccols = {}
        if key not in self._ccols:
            t = self.sb(f"ccol_{self.uid()}", [np_, 1], F32)
            self.op("vector", lambda e, t=t, v=float(value): e.memset(t[:], v), inc="v")
            self.wait("vector", "v")
            self._ccols[key] = t
        return self._ccols[key]

    def finish(self):
        with self.nc.Block() as block:
            for engname in ENGS:
                ops = self.ops[engname]
                if not ops:
                    continue

                def make(ops):
                    def run(e):
                        for fn in ops:
                            fn(e)
                    return run
                getattr(block, engname)(make(ops))
        for g in reversed(list(self._guards.values())):
            g.__exit__(None, None, None)
        self._guards.clear()
        self.es.close()
        return self.nc


def tok_blocks(total, blk=512):
    out = []
    o = 0
    while o < total:
        out.append((o, min(blk, total - o)))
        o += blk
    return out


def _bf16np():
    import ml_dtypes
    return ml_dtypes.bfloat16



class Common:
    def __init__(self, p: Prog):
        self.p = p
        self.ident_bf = p.sb("c_ident_bf", [P, P], BF16)
        self.ones_row = p.sb("c_ones_row", [1, P], F32)
        self.ones_col = p.sb("c_ones_col", [P, 1], F32)
        self.ident_f = p.sb("c_ident_f", [P, P], F32)
        cd = p.dram("c_ident", [P, P], BF16, kind="ExternalInput").ap()
        p.dma("sync", self.ident_bf[:], cd)
        p.op("vector", lambda e: e.memset(self.ones_row[:], 1.0), inc="v")
        p.op("vector", lambda e: e.memset(self.ones_col[:], 1.0), inc="v")
        p.wait("vector", "dma")
        p.wait("vector", "v")
        p.op("vector", lambda e: e.tensor_copy(out=self.ident_f[:], in_=self.ident_bf[:]), inc="v")
        junk = p.sb("c_accjunk", [P, 2], F32)
        p.wait("vector", "v")
        p.op("vector", lambda e, junk=junk: e.scalar_tensor_tensor(
            out=junk[:, 0:1], in0=self.ones_col[:], scalar=0.0, in1=self.ones_col[:],
            op0=ALU.mult, op1=ALU.mult, accum_out=junk[:, 1:2]), inc="v")

    @staticmethod
    def host_inputs():
        return {"c_ident": np.eye(P, dtype=np.float32).astype(_bf16np())}


def emit_bcast_row(p, cm, src_row, dst, N, ps, scale=1.0):
    for off, ln_ in tok_blocks(N):
        p.op("tensor", lambda e, off=off, ln_=ln_: e.matmul(
            out=ps[:, 0:ln_], lhsT=cm.ones_row[0:1, :], rhs=src_row[0:1, off:off + ln_],
            start=True, stop=True), inc="t")
        p.wait("vector", "t")
        if scale == 1.0:
            p.op("vector", lambda e, off=off, ln_=ln_: e.tensor_copy(
                out=dst[:, off:off + ln_], in_=ps[:, 0:ln_]), inc="v")
        else:
            p.op("vector", lambda e, off=off, ln_=ln_: e.tensor_scalar(
                out=dst[:, off:off + ln_], in0=ps[:, 0:ln_],
                scalar1=float(scale), scalar2=None, op0=ALU.mult), inc="v")
        p.wait("tensor", "v")
    p.wait("vector", "v")


def emit_ln(p, cm, xt, Tlen, tag, eps=1e-5, xstride=None):
    xs_ = xstride if xstride is not None else Tlen
    epsc = p.constcol(eps)
    rho_bc = p.sb(f"ln_rho_{tag}", [P, Tlen], F32)
    mrho_bc = p.sb(f"ln_mrho_{tag}", [P, Tlen], F32)
    pb = p.psum(f"ln_pb_{tag}", [P, 512], F32)
    A = p.sb(f"ln_A_{tag}", [1, Tlen], F32)
    B = p.sb(f"ln_B_{tag}", [1, Tlen], F32)
    Crow = p.sb(f"ln_C_{tag}", [1, Tlen], F32)
    sq = p.sb(f"ln_sq_{tag}", [P, KD * 512], F32)
    pr0 = p.psum(f"ln_pr0_{tag}", [1, 512], F32)
    pr1 = p.psum(f"ln_pr1_{tag}", [1, 512], F32)

    for off, ln_ in tok_blocks(Tlen):
        p.wait("scalar", "t")
        for k in range(KD):
            p.op("scalar", lambda e, k=k, off=off, ln_=ln_: e.activation(
                out=sq[:, k * 512:k * 512 + ln_],
                in_=xt[:, k * xs_ + off: k * xs_ + off + ln_], func=AF.Square),
                inc=("s" if k == KD - 1 else None))
        p.wait("tensor", "s")
        for k in range(KD):
            p.op("tensor", lambda e, k=k, off=off, ln_=ln_: e.matmul(
                out=pr0[0:1, 0:ln_], lhsT=cm.ones_col[:, 0:1],
                rhs=xt[:, k * xs_ + off: k * xs_ + off + ln_],
                start=(k == 0), stop=(k == KD - 1)), inc=None)
        for k in range(KD):
            p.op("tensor", lambda e, k=k, off=off, ln_=ln_: e.matmul(
                out=pr1[0:1, 0:ln_], lhsT=cm.ones_col[:, 0:1],
                rhs=sq[:, k * 512:k * 512 + ln_],
                start=(k == 0), stop=(k == KD - 1)),
                inc=("t" if k == KD - 1 else None))
        p.wait("vector", "t")
        p.op("vector", lambda e, off=off, ln_=ln_: e.tensor_copy(
            out=A[:, off:off + ln_], in_=pr0[0:1, 0:ln_]), inc=None)
        p.op("vector", lambda e, off=off, ln_=ln_: e.tensor_copy(
            out=B[:, off:off + ln_], in_=pr1[0:1, 0:ln_]), inc="v")
        p.wait("tensor", "v")
    p.free(f"ln_pr1_{tag}", f"ln_pr0_{tag}", f"ln_sq_{tag}")

    p.wait("vector", "v")
    p.op("vector", lambda e: e.tensor_scalar(out=A[:], in0=A[0:1, :],
                                             scalar1=1.0 / D, scalar2=None,
                                             op0=ALU.mult), inc="v")
    p.wait("scalar", "v")
    p.op("scalar", lambda e: e.activation(out=Crow[:], in_=A[:], func=AF.Square), inc="s")
    p.wait("vector", "s")
    p.op("vector", lambda e: e.scalar_tensor_tensor(
        out=B[:], in0=B[0:1, :], scalar=1.0 / D, in1=Crow[:],
        op0=ALU.mult, op1=ALU.subtract), inc="v")
    p.wait("scalar", "v")
    p.op("scalar", lambda e, epsc=epsc: e.activation(out=B[:], in_=B[:], func=AF.Ln,
                                                     bias=epsc[0:1, 0:1]), inc="s")
    p.wait("scalar", "s")
    p.op("scalar", lambda e: e.activation(out=B[:], in_=B[:], func=AF.Exp,
                                          scale=-0.5), inc="s")
    p.wait("vector", "s")
    p.op("vector", lambda e: e.tensor_mul(out=Crow[:], in0=A[:], in1=B[:]), inc="v")
    p.wait("tensor", "v")
    emit_bcast_row(p, cm, B, rho_bc, Tlen, pb)
    emit_bcast_row(p, cm, Crow, mrho_bc, Tlen, pb, scale=-1.0)
    p._ln_rows = {"A": A, "B": B, "C": Crow, "pb": pb}
    names = [f"ln_C_{tag}", f"ln_B_{tag}", f"ln_A_{tag}", f"ln_pb_{tag}"]
    return rho_bc, mrho_bc, names


def emit_ln_xnb(p, cm, xt, Tlen, tag):
    xnb = p.sb(f"xnb_{tag}", [P, KD * Tlen], BF16)
    rho_bc, mrho_bc, names = emit_ln(p, cm, xt, Tlen, tag)
    xnf = p.sb(f"xnf_{tag}", [P, Tlen], F32)
    for k in range(KD):
        p.wait("vector", "v")
        p.op("vector", lambda e, k=k: e.scalar_tensor_tensor(
            out=xnf[:], in0=xt[:, k * Tlen:(k + 1) * Tlen],
            scalar=1.0, in1=rho_bc[:], op0=ALU.mult, op1=ALU.mult), inc=None)
        p.op("vector", lambda e, k=k: e.tensor_add(
            out=xnf[:], in0=xnf[:], in1=mrho_bc[:]), inc=None)
        p.op("vector", lambda e, k=k: e.tensor_copy(
            out=xnb[:, k * Tlen:(k + 1) * Tlen], in_=xnf[:]), inc="v")
    p._ln_keep = {"rho_bc": rho_bc, "mrho_bc": mrho_bc}
    order = [f"xnf_{tag}"] + names + [f"ln_mrho_{tag}", f"ln_rho_{tag}"]
    return xnb, order


def emit_load_wbf(p, Wd, Din, Dout, wbuf, stg, woff=0):
    nk = Din // P
    p.wait("sync", "t")
    for k in range(nk):
        p.wait("sync", "v")
        p.dma("sync", stg[:, 0:Dout], Wd[k * P:(k + 1) * P, :], inc="dma")
        p.wait("vector", "dma")
        p.op("vector", lambda e, k=k: e.tensor_copy(
            out=wbuf[:, woff + k * Dout:woff + (k + 1) * Dout], in_=stg[:, 0:Dout]), inc="v")


HALO_L, HALO_R = 32, 31
TE = 2112


def build_l0_attn_a():
    p = Prog("l0aa")
    xd = p.dram("x", [D, TE], F32, kind="ExternalInput").ap()
    Ad = p.dram("Ain", [1, 64], F32, kind="ExternalInput").ap()
    Bd = p.dram("Bin", [64, D], F32, kind="ExternalInput").ap()
    Cd = p.dram("Cin", [D, 64], F32, kind="ExternalInput").ap()
    Dd = p.dram("Dvec", [D, 1], F32, kind="ExternalInput").ap()
    ldt = p.dram("log_dt", [D, 1], F32, kind="ExternalInput").ap()
    yd = p.dram("y", [D, T], F32, kind="ExternalOutput").ap()

    cm = Common(p)
    xt = p.sb("xt", [P, KD * TE], F32)
    un = p.sb("un", [P, TE], F32)
    acc = p.sb("acc", [P, T], F32)

    for k in range(KD):
        p.dma("sync", xt[:, k * TE:(k + 1) * TE], xd[k * P:(k + 1) * P, :])
    p.wait("vector", "dma")
    p.wait("scalar", "dma")
    p.wait("tensor", "dma")

    rho_bc, mrho_bc, _ln_names = emit_ln(p, cm, xt, TE, "aa")

    Bt = p.sb("Bt", [64, D], F32)
    Arow = p.sb("Arow", [1, 64], F32)
    Dcol = p.sb("Dcol", [P, KD], F32)
    dtc = p.sb("dtc", [P, KD], F32)
    Ct = p.sb("Ct", [P, KD * 64], F32)
    p.fence_dma()
    p.dma("sync", Bt[:, :], Bd)
    p.dma("sync", Arow[:, :], Ad)
    for k in range(KD):
        p.dma("sync", Ct[:, k * 64:(k + 1) * 64], Cd[k * P:(k + 1) * P, :])
        p.dma("sync", Dcol[:, k:k + 1], Dd[k * P:(k + 1) * P, :])
        p.dma("sync", dtc[:, k:k + 1], ldt[k * P:(k + 1) * P, :])

    csum_ps = p.psum("csum_ps", [1, 64], F32)
    csum_row = p.sb("csum_row", [1, 64], F32)
    cs_bc = p.sb("cs_bc", [P, 64], F32)
    Arow_bc = p.sb("Arow_bc", [P, 64], F32)
    ps_small = p.psum("ps_small", [P, 512], F32)
    W0T = p.sb("W0T", [P, KD * 64], F32)
    Abar = p.sb("Abar", [P, KD * 64], F32)
    Pl = p.sb("Pl", [P, 64], F32)
    qtmp = p.sb("qtmp", [P, 64], F32)
    Kt = p.sb("Kt", [P, KD * 64], F32)
    bt_ps = p.psum("bt_ps", [P, 64], F32)

    p.wait("scalar", "dma")
    p.op("scalar", lambda e: e.activation(out=dtc[:], in_=dtc[:], func=AF.Exp), inc="s")
    p.wait("tensor", "dma")
    for k in range(KD):
        p.op("tensor", lambda e, k=k: e.matmul(
            out=csum_ps[:, :], lhsT=cm.ones_col[:, 0:1],
            rhs=Ct[:, k * 64:(k + 1) * 64],
            start=(k == 0), stop=(k == KD - 1)), inc=("t" if k == KD - 1 else None))
    p.wait("vector", "t")
    p.op("vector", lambda e: e.tensor_copy(out=csum_row[:], in_=csum_ps[:]), inc="v")
    p.wait("tensor", "v")
    emit_bcast_row(p, cm, csum_row, cs_bc, 64, ps_small)
    emit_bcast_row(p, cm, Arow, Arow_bc, 64, ps_small)
    for k in range(KD):
        p.wait("tensor", "v")
        p.op("tensor", lambda e, k=k: e.transpose(
            out=bt_ps[0:P, 0:64], in_=Bt[:, k * P:(k + 1) * P],
            identity=cm.ident_f[0:64, 0:64]), inc="t")
        p.wait("vector", "t")
        p.op("vector", lambda e, k=k: e.tensor_mul(
            out=W0T[:, k * 64:(k + 1) * 64], in0=bt_ps[0:P, 0:64], in1=cs_bc[:, :]), inc="v")
        p.wait("tensor", "v")
    p.wait("scalar", "v")
    p.wait("scalar", "s")
    for k in range(KD):
        p.op("scalar", lambda e, k=k: e.activation(
            out=Abar[:, k * 64:(k + 1) * 64], in_=Arow_bc[:, :], func=AF.Exp,
            scale=dtc[:, k:k + 1]), inc="s")
    p.wait("vector", "s")
    for k in range(KD):
        p.op("vector", lambda e: e.memset(Pl[:], 1.0), inc="v")
        p.wait("vector", "v")
        for l in range(64):
            p.op("vector", lambda e, k=k, l=l: e.scalar_tensor_tensor(
                out=qtmp[:], in0=Pl[:], scalar=1.0, in1=W0T[:, k * 64:(k + 1) * 64],
                op0=ALU.mult, op1=ALU.mult,
                accum_out=Kt[:, k * 64 + l:k * 64 + l + 1]), inc=None)
            if l < 63:
                p.op("vector", lambda e, k=k: e.tensor_mul(
                    out=Pl[:], in0=Pl[:], in1=Abar[:, k * 64:(k + 1) * 64]), inc=None)
        p.op("vector", lambda e: e.engine_nop(), inc="v")

    for k in range(KD):
        p.wait("vector", "st")
        p.wait("vector", "v")
        p.wait("vector", "s")
        p.op("vector", lambda e, k=k: e.scalar_tensor_tensor(
            out=un[:], in0=xt[:, k * TE:(k + 1) * TE],
            scalar=1.0, in1=rho_bc[:], op0=ALU.mult, op1=ALU.mult), inc=None)
        p.op("vector", lambda e, k=k: e.tensor_add(
            out=un[:], in0=un[:], in1=mrho_bc[:]), inc=None)
        p.op("vector", lambda e, k=k: e.tensor_scalar(
            out=acc[:], in0=un[:, HALO_L:HALO_L + T],
            scalar1=Dcol[:, k:k + 1], scalar2=None, op0=ALU.mult), inc=None)
        for l in range(64):
            off = 63 - l
            p.op("vector", lambda e, k=k, l=l, off=off: e.scalar_tensor_tensor(
                out=acc[:], in0=un[:, off:off + T],
                scalar=Kt[:, k * 64 + l:k * 64 + l + 1],
                in1=acc[:], op0=ALU.mult, op1=ALU.add), inc=None)
        p.op("vector", lambda e, k=k: e.tensor_add(
            out=acc[:], in0=acc[:],
            in1=xt[:, k * TE + HALO_L:k * TE + HALO_L + T]), inc="v")
        p.wait("sync", "v")
        p.dma("sync", yd[k * P:(k + 1) * P, :], acc[:], inc="st")
    p.wait("sync", "st")
    p.finish()
    return p



def build_l1_ffn(dbg=False):
    p = Prog("l1ffn")
    xd = p.dram("x", [D, T], F32, kind="ExternalInput").ap()
    rwd = p.dram("router_w", [8, D], F32, kind="ExternalInput").ap()
    rbd = p.dram("router_bias", [1, 8], F32, kind="ExternalInput").ap()
    sw1d = p.dram("sw1", [D, 2048], F32, kind="ExternalInput").ap()
    sw3d = p.dram("sw3", [D, 2048], F32, kind="ExternalInput").ap()
    sw2d = p.dram("sw2", [2048, D], F32, kind="ExternalInput").ap()
    ew1d = p.dram("ew1", [8 * D, D], F32, kind="ExternalInput").ap()
    ew2d = p.dram("ew2", [8 * D, D], F32, kind="ExternalInput").ap()
    eye8d = p.dram("eye8", [8, 8], F32, kind="ExternalInput").ap()
    yd = p.dram("y", [D, T], F32, kind="ExternalOutput").ap()
    if dbg:
        ycomb = p.dram("ycomb", [8, T], F32, kind="ExternalOutput").ap()
        yglu = p.dram("yglu", [D, T], F32, kind="ExternalOutput").ap()
        ydbg = p.dram("ydbg", [P, 72], F32, kind="ExternalOutput").ap()

    cm = Common(p)
    xt = p.sb("xt", [P, KD * T], F32)
    for k in range(KD):
        p.dma("sync", xt[:, k * T:(k + 1) * T], xd[k * P:(k + 1) * P, :])
    p.wait("vector", "dma")
    p.wait("scalar", "dma")
    p.wait("tensor", "dma")

    eye8 = p.sb("eye8t", [8, 8], F32)
    comb_fm = p.sb("comb_fm", [8, T], F32)
    rps = p.psum("rps", [P, 512], F32)
    lps = p.psum("lps", [8, 512], F32)
    xnb, ln_free = emit_ln_xnb(p, cm, xt, T, "f1")

    rw = p.sb("rw", [8, D], F32)
    rwT = p.sb("rwT", [P, KD * 8], F32)
    rb8 = p.sb("rb8", [1, 8], F32)
    logit = p.sb("logit", [8, T], F32)
    rs8 = p.sb("rs8", [8, 1], F32)
    p.fence_dma()
    p.dma("sync", rw[:, :], rwd)
    p.dma("sync", rb8[:, :], rbd)
    p.dma("sync", eye8[:, :], eye8d)
    p.wait("tensor", "dma")
    p.wait("vector", "dma")
    p.op("vector", lambda e: e.tensor_reduce(out=rs8[:], in_=rw[:, :],
                                             axis=mybir.AxisListType.X,
                                             op=ALU.add), inc="v")
    for k in range(KD):
        p.wait("tensor", "v")
        p.op("tensor", lambda e, k=k: e.transpose(
            out=rps[0:P, 0:8], in_=rw[0:8, k * P:(k + 1) * P],
            identity=cm.ident_f[0:8, 0:8]), inc="t")
        p.wait("vector", "t")
        p.op("vector", lambda e, k=k: e.tensor_copy(
            out=rwT[:, k * 8:(k + 1) * 8], in_=rps[0:P, 0:8]), inc="v")
    rho8 = p._ln_keep["rho_bc"]
    mrho8 = p._ln_keep["mrho_bc"]
    for off, ln_ in tok_blocks(T):
        p.wait("tensor", "v")
        for k in range(KD):
            p.op("tensor", lambda e, k=k, off=off, ln_=ln_: e.matmul(
                out=lps[0:8, 0:ln_], lhsT=rwT[:, k * 8:(k + 1) * 8],
                rhs=xt[:, k * T + off:k * T + off + ln_],
                start=(k == 0), stop=(k == KD - 1)),
                inc=("t" if k == KD - 1 else None))
        p.wait("vector", "t")
        p.op("vector", lambda e, off=off, ln_=ln_: e.tensor_mul(
            out=logit[:, off:off + ln_], in0=lps[0:8, 0:ln_],
            in1=rho8[0:8, off:off + ln_]), inc=None)
        p.op("vector", lambda e, off=off, ln_=ln_: e.scalar_tensor_tensor(
            out=logit[:, off:off + ln_], in0=mrho8[0:8, off:off + ln_],
            scalar=rs8[0:8, 0:1], in1=logit[:, off:off + ln_],
            op0=ALU.mult, op1=ALU.add), inc="v")
        p.wait("scalar", "v")
        p.op("scalar", lambda e, off=off, ln_=ln_: e.activation(
            out=logit[:, off:off + ln_], in_=logit[:, off:off + ln_],
            func=AF.Sigmoid), inc="v")
        p.wait("tensor", "v")
    s_tm = p.sb("s_tm", [P, 8], F32)
    comb_tm = p.sb("comb_tm", [P, 8], F32)
    bias_bc8 = p.sb("bias_bc8", [P, 8], F32)
    sel_tm = p.sb("sel_tm", [P, 8], F32)
    m8r = p.sb("m8r", [P, 8], F32)
    oh1 = p.sb("oh1", [P, 8], F32)
    oh2 = p.sb("oh2", [P, 8], F32)
    scl = p.sb("scl", [P, 8], F32)
    emit_bcast_row(p, cm, rb8, bias_bc8, 8, rps)
    for c in range(16):
        p.wait("tensor", "v")
        p.op("tensor", lambda e, c=c: e.transpose(
            out=rps[0:P, 0:8], in_=logit[0:8, c * P:(c + 1) * P],
            identity=cm.ident_f[0:8, 0:8]), inc="t")
        p.wait("vector", "t")
        p.op("vector", lambda e: e.tensor_copy(out=s_tm[:], in_=rps[0:P, 0:8]), inc="v")
        p.wait("tensor", "v")
        p.wait("vector", "v")
        p.op("vector", lambda e: e.tensor_add(
            out=sel_tm[:], in0=s_tm[:], in1=bias_bc8[:]), inc="v")
        p.wait("vector", "v")
        p.op("vector", lambda e: e.max(out=m8r[:], in_=sel_tm[:]), inc="v")
        p.wait("vector", "v")
        p.op("vector", lambda e: e.tensor_scalar(
            out=oh1[:], in0=sel_tm[:], scalar1=m8r[:, 0:1], scalar2=None,
            op0=ALU.is_equal), inc=None)
        p.op("vector", lambda e: e.tensor_scalar(
            out=oh2[:], in0=sel_tm[:], scalar1=m8r[:, 1:2], scalar2=None,
            op0=ALU.is_equal), inc="v")
        p.wait("vector", "v")
        p.op("vector", lambda e: e.scalar_tensor_tensor(
            out=oh1[:], in0=oh1[:], scalar=1.0, in1=s_tm[:],
            op0=ALU.mult, op1=ALU.mult, accum_out=scl[:, 0:1]), inc="v")
        p.wait("vector", "v")
        p.op("vector", lambda e: e.scalar_tensor_tensor(
            out=oh2[:], in0=oh2[:], scalar=1.0, in1=s_tm[:],
            op0=ALU.mult, op1=ALU.mult, accum_out=scl[:, 1:2]), inc="v")
        p.wait("vector", "v")
        p.op("vector", lambda e: e.tensor_add(out=scl[:, 2:3], in0=scl[:, 0:1],
                                              in1=scl[:, 1:2]), inc="v")
        p.wait("vector", "v")
        p.op("vector", lambda e: e.tensor_scalar(
            out=scl[:, 3:4], in0=scl[:, 2:3], scalar1=1e-9, scalar2=None,
            op0=ALU.add), inc=None)
        p.op("vector", lambda e: e.tensor_scalar(
            out=scl[:, 5:6], in0=scl[:, 2:3], scalar1=1e-9, scalar2=None,
            op0=ALU.is_gt), inc="v")
        p.wait("vector", "v")
        p.op("vector", lambda e: e.reciprocal(out=scl[:, 4:5], in_=scl[:, 3:4]), inc="v")
        p.wait("vector", "v")
        p.op("vector", lambda e: e.tensor_add(out=comb_tm[:], in0=oh1[:], in1=oh2[:]), inc="v")
        p.wait("vector", "v")
        p.op("vector", lambda e: e.tensor_scalar(
            out=comb_tm[:], in0=comb_tm[:], scalar1=scl[:, 4:5],
            scalar2=scl[:, 5:6], op0=ALU.mult, op1=ALU.mult), inc="v")
        if dbg and c == 0:
            p.wait("sync", "v")
            for j, tt in enumerate([s_tm, sel_tm, m8r, oh1, scl, oh2, comb_tm, bias_bc8]):
                pass
            for j, tt in enumerate([s_tm, sel_tm, m8r, oh1, scl, oh2, comb_tm, bias_bc8]):
                p.dma("sync", ydbg[:, j * 8:(j + 1) * 8], tt[:, 0:8], inc="st")
            p.dma("sync", ydbg[0:1, 64:72], rb8[0:1, 0:8], inc="st")
            p.wait("vector", "st")
            p.wait("tensor", "st")
            p.wait("scalar", "st")
        p.wait("tensor", "v")
        p.op("tensor", lambda e, c=c: e.transpose(
            out=rps[0:8, 0:P], in_=comb_tm[:, 0:8], identity=cm.ident_f[:]), inc="t")
        p.wait("vector", "t")
        p.op("vector", lambda e, c=c: e.tensor_copy(
            out=comb_fm[:, c * P:(c + 1) * P], in_=rps[0:8, 0:P]), inc="v")
        p.wait("tensor", "v")
    p.free("scl", "oh2", "oh1", "m8r", "sel_tm", "bias_bc8", "comb_tm", "s_tm",
           "rs8", "logit", "rb8", "rwT", "rw")
    p.free(*ln_free)

    HS = 2048
    NH = HS // P
    wbuf = p.sb("wbuf", [P, 8 * HS], BF16)
    stg = p.sb("stg", [P, HS], F32)
    gsil_q = p.sb("gsil_q", [P, NH * 512], BF16)
    mprod_q = p.sb("mprod_q", [P, NH * 512], BF16)
    ps1 = p.psum("ps1", [P, 512], F32)
    pso = p.psum("pso", [P, 512], F32)

    def glu_pass(off, ln_, which):
        for m in range(NH):
            p.wait("tensor", "v")
            for k in range(KD):
                p.op("tensor", lambda e, m=m, k=k: e.matmul(
                    out=ps1[:, 0:ln_],
                    lhsT=wbuf[:, k * HS + m * P:k * HS + (m + 1) * P],
                    rhs=xnb[:, k * T + off:k * T + off + ln_],
                    start=(k == 0), stop=(k == KD - 1)),
                    inc=("t" if k == KD - 1 else None))
            if which == 1:
                p.wait("scalar", "t")
                p.op("scalar", lambda e, m=m: e.activation(
                    out=gsil_q[:, m * 512:m * 512 + ln_], in_=ps1[:, 0:ln_],
                    func=AF.Silu), inc="v")
            else:
                p.wait("vector", "t")
                p.op("vector", lambda e, m=m: e.tensor_mul(
                    out=mprod_q[:, m * 512:m * 512 + ln_], in0=ps1[:, 0:ln_],
                    in1=gsil_q[:, m * 512:m * 512 + ln_]), inc="v")

    p.fence_dma()
    for off, ln_ in tok_blocks(T, 512):
        p.wait("tensor", "v")
        emit_load_wbf(p, sw1d, D, HS, wbuf, stg)
        glu_pass(off, ln_, 1)
        p.wait("tensor", "v")
        emit_load_wbf(p, sw3d, D, HS, wbuf, stg)
        glu_pass(off, ln_, 3)
        p.wait("tensor", "v")
        emit_load_wbf(p, sw2d, HS, D, wbuf, stg)
        for mo in range(KD):
            p.wait("tensor", "v")
            for k in range(NH):
                p.op("tensor", lambda e, mo=mo, k=k, off=off, ln_=ln_: e.matmul(
                    out=pso[:, 0:ln_],
                    lhsT=wbuf[:, k * D + mo * P:k * D + (mo + 1) * P],
                    rhs=mprod_q[:, k * 512:k * 512 + ln_],
                    start=(k == 0), stop=(k == NH - 1)),
                    inc=("t" if k == NH - 1 else None))
            p.wait("vector", "t")
            p.op("vector", lambda e, mo=mo, off=off, ln_=ln_: e.tensor_add(
                out=xt[:, mo * T + off:mo * T + off + ln_],
                in0=xt[:, mo * T + off:mo * T + off + ln_],
                in1=pso[:, 0:ln_]), inc="v")

    if dbg:
        p.wait("sync", "v")
        p.dma("sync", ycomb, comb_fm[0:8, :], inc="st")
        for k in range(KD):
            p.dma("sync", yglu[k * P:(k + 1) * P, :], xt[:, k * T:(k + 1) * T], inc="st")

    comb_bc = p.sb("comb_bc", [P, T], F32)
    crow = p.sb("crow", [1, T], F32)
    cps = p.psum("cps", [1, 512], F32)
    EW2OFF = 8 * D
    for e_i in range(8):
        for off, ln_ in tok_blocks(T):
            p.wait("tensor", "v")
            p.op("tensor", lambda e, e_i=e_i, off=off, ln_=ln_: e.matmul(
                out=cps[0:1, 0:ln_], lhsT=eye8[0:8, e_i:e_i + 1],
                rhs=comb_fm[0:8, off:off + ln_], start=True, stop=True), inc="t")
            p.wait("vector", "t")
            p.op("vector", lambda e, off=off, ln_=ln_: e.tensor_copy(
                out=crow[:, off:off + ln_], in_=cps[0:1, 0:ln_]), inc="v")
            p.wait("tensor", "v")
        emit_bcast_row(p, cm, crow, comb_bc, T, rps)
        p.wait("tensor", "v")
        emit_load_wbf(p, ew1d[e_i * D:(e_i + 1) * D, :], D, D, wbuf, stg)
        emit_load_wbf(p, ew2d[e_i * D:(e_i + 1) * D, :], D, D, wbuf, stg, woff=EW2OFF)
        for off, ln_ in tok_blocks(T):
            for m in range(KD):
                p.wait("tensor", "v")
                for k in range(KD):
                    p.op("tensor", lambda e, m=m, k=k, off=off, ln_=ln_: e.matmul(
                        out=ps1[:, 0:ln_],
                        lhsT=wbuf[:, k * D + m * P:k * D + (m + 1) * P],
                        rhs=xnb[:, k * T + off:k * T + off + ln_],
                        start=(k == 0), stop=(k == KD - 1)),
                        inc=("t" if k == KD - 1 else None))
                p.wait("scalar", "t")
                p.op("scalar", lambda e, m=m, off=off, ln_=ln_: e.activation(
                    out=gsil_q[:, m * 512:m * 512 + ln_], in_=ps1[:, 0:ln_],
                    func=AF.Gelu), inc="s")
                p.wait("vector", "s")
                p.op("vector", lambda e, m=m, off=off, ln_=ln_: e.tensor_mul(
                    out=gsil_q[:, m * 512:m * 512 + ln_],
                    in0=gsil_q[:, m * 512:m * 512 + ln_],
                    in1=comb_bc[:, off:off + ln_]), inc="v")
            for mo in range(KD):
                p.wait("tensor", "v")
                for k in range(KD):
                    p.op("tensor", lambda e, mo=mo, k=k, off=off, ln_=ln_: e.matmul(
                        out=pso[:, 0:ln_],
                        lhsT=wbuf[:, EW2OFF + k * D + mo * P:EW2OFF + k * D + (mo + 1) * P],
                        rhs=gsil_q[:, k * 512:k * 512 + ln_],
                        start=(k == 0), stop=(k == KD - 1)),
                        inc=("t" if k == KD - 1 else None))
                p.wait("vector", "t")
                p.op("vector", lambda e, mo=mo, off=off, ln_=ln_: e.tensor_add(
                    out=xt[:, mo * T + off:mo * T + off + ln_],
                    in0=xt[:, mo * T + off:mo * T + off + ln_],
                    in1=pso[:, 0:ln_]), inc="v")

    p.wait("sync", "v")
    for k in range(KD):
        p.dma("sync", yd[k * P:(k + 1) * P, :], xt[:, k * T:(k + 1) * T], inc="st")
    p.wait("sync", "st")
    p.finish()
    return p



_cache = {}


def _get_prog(name, builder):
    if name not in _cache:
        _cache[name] = builder()
    return _cache[name]


def _fm_slices(x):
    out = []
    for c in range(NCORES):
        b, s0 = c // 4, (c % 4) * T
        out.append(np.ascontiguousarray(x[b, s0:s0 + T, :].T))
    return out


def _fm_slices_ext(x):
    out = []
    for c in range(NCORES):
        b, s0 = c // 4, (c % 4) * T
        lo, hi = s0 - HALO_L, s0 + T + HALO_R + 1
        ext = np.zeros((D, TE), np.float32)
        src_lo, src_hi = max(lo, 0), min(hi, SEQ)
        ext[:, src_lo - lo: src_hi - lo] = x[b, src_lo:src_hi, :].T
        out.append(ext)
    return out


def _unshard(ys):
    x = np.empty((2, SEQ, D), np.float32)
    for c in range(NCORES):
        b, s0 = c // 4, (c % 4) * T
        x[b, s0:s0 + T, :] = ys[c].T
    return x


def run_l0_attn_a(x, inp):
    pr = _get_prog("l0aa", build_l0_attn_a)
    xs = _fm_slices_ext(x)
    base = {
        "Ain": np.asarray(inp["L0_attn_a_A"], np.float32).reshape(1, 64),
        "Bin": np.asarray(inp["L0_attn_a_B"], np.float32),
        "Cin": np.asarray(inp["L0_attn_a_C"], np.float32),
        "Dvec": np.asarray(inp["L0_attn_a_D"], np.float32).reshape(D, 1),
        "log_dt": np.asarray(inp["L0_attn_a_log_dt"], np.float32).reshape(D, 1),
        **Common.host_inputs(),
    }
    in_maps = [{**base, "x": xs[c]} for c in range(NCORES)]
    res = run_bass_kernel_spmd(pr.nc, in_maps, core_ids=list(range(NCORES)))
    return _unshard([res.results[c]["y"] for c in range(NCORES)])


def run_l1_ffn(x, inp):
    pr = _get_prog("l1ffn", build_l1_ffn)
    xs = _fm_slices(x)
    base = {
        "router_w": np.asarray(inp["L1_ffn_router_w"], np.float32),
        "router_bias": np.asarray(inp["L1_ffn_router_bias"], np.float32).reshape(1, 8),
        "sw1": np.asarray(inp["L1_ffn_sw1"], np.float32),
        "sw3": np.asarray(inp["L1_ffn_sw3"], np.float32),
        "sw2": np.asarray(inp["L1_ffn_sw2"], np.float32),
        "ew1": np.asarray(inp["L1_ffn_ew1"], np.float32).reshape(8 * D, D),
        "ew2": np.asarray(inp["L1_ffn_ew2"], np.float32).reshape(8 * D, D),
        "eye8": np.eye(8, dtype=np.float32),
        **Common.host_inputs(),
    }
    in_maps = [{**base, "x": xs[c]} for c in range(NCORES)]
    res = run_bass_kernel_spmd(pr.nc, in_maps, core_ids=list(range(NCORES)))
    return _unshard([res.results[c]["y"] for c in range(NCORES)])


def kernel(**inputs):
    x = np.asarray(inputs["L0_attn_a_u"], np.float32)
    x = run_l0_attn_a(x, inputs)
    x = run_l0_ffn(x, inputs)
    x = run_l0_attn_b(x, inputs)
    x = run_l0_ffn(x, inputs)
    x = run_l1_attn_a(x, inputs)
    x = run_l1_ffn(x, inputs)
    x = run_l1_attn_b(x, inputs)
    x = run_l1_ffn(x, inputs)
    return x



def build_l0_ffn():
    p = Prog("l0ffn")
    xd = p.dram("x", [D, T], F32, kind="ExternalInput").ap()
    Wgd = p.dram("Wg", [D, 4096], F32, kind="ExternalInput").ap()
    Wud = p.dram("Wu", [D, 4096], F32, kind="ExternalInput").ap()
    Wdd = p.dram("Wd", [4096, D], F32, kind="ExternalInput").ap()
    bgd = p.dram("bg", [4096, 1], F32, kind="ExternalInput").ap()
    bud = p.dram("bu", [4096, 1], F32, kind="ExternalInput").ap()
    bdd = p.dram("bd", [D, 1], F32, kind="ExternalInput").ap()
    Wqd = p.dram("Wq", [D, 512], F32, kind="ExternalInput").ap()
    keysTd = p.dram("keysT", [512, 64], F32, kind="ExternalInput").ap()
    downd = p.dram("down_embed", [4096, D], F32, kind="ExternalInput").ap()
    upd = p.dram("up_embed", [4096, D], F32, kind="ExternalInput").ap()
    yd = p.dram("y", [D, T], F32, kind="ExternalOutput").ap()

    cm = Common(p)
    xt = p.sb("xt", [P, KD * T], F32)
    for k in range(KD):
        p.dma("sync", xt[:, k * T:(k + 1) * T], xd[k * P:(k + 1) * P, :])
    p.wait("vector", "dma")
    p.wait("scalar", "dma")
    p.wait("tensor", "dma")

    xnb, ln_free = emit_ln_xnb(p, cm, xt, T, "f0")
    p.free(*ln_free)

    HH = 2048
    NHH = HH // P
    wbuf = p.sb("wbuf", [P, 8 * HH], BF16)
    stg = p.sb("stg", [P, HH], F32)
    gsil_q = p.sb("gsil_q", [P, NHH * 512], BF16)
    mprod_q = p.sb("mprod_q", [P, NHH * 512], BF16)
    bgc = p.sb("bgc", [P, 32], F32)
    buc = p.sb("buc", [P, 32], F32)
    bdc = p.sb("bdc", [P, KD], F32)
    ps1 = p.psum("ps1", [P, 512], F32)
    pso = p.psum("pso", [P, 512], F32)
    for m in range(32):
        p.dma("sync", bgc[:, m:m + 1], bgd[m * P:(m + 1) * P, :])
        p.dma("sync", buc[:, m:m + 1], bud[m * P:(m + 1) * P, :])
    for k in range(KD):
        p.dma("sync", bdc[:, k:k + 1], bdd[k * P:(k + 1) * P, :])

    p.fence_dma()
    for off, ln_ in tok_blocks(T, 512):
        for half in range(2):
            hof = half * 16
            p.wait("tensor", "v")
            emit_load_wbf(p, Wgd[:, half * HH:(half + 1) * HH], D, HH, wbuf, stg)
            for m in range(NHH):
                p.wait("tensor", "v")
                for k in range(KD):
                    p.op("tensor", lambda e, m=m, k=k, off=off, ln_=ln_: e.matmul(
                        out=ps1[:, 0:ln_],
                        lhsT=wbuf[:, k * HH + m * P:k * HH + (m + 1) * P],
                        rhs=xnb[:, k * T + off:k * T + off + ln_],
                        start=(k == 0), stop=(k == KD - 1)),
                        inc=("t" if k == KD - 1 else None))
                p.wait("scalar", "t")
                p.op("scalar", lambda e, m=m, hof=hof, ln_=ln_: e.activation(
                    out=gsil_q[:, m * 512:m * 512 + ln_], in_=ps1[:, 0:ln_],
                    func=AF.Silu, bias=bgc[:, hof + m:hof + m + 1]), inc="v")
            p.wait("tensor", "v")
            emit_load_wbf(p, Wud[:, half * HH:(half + 1) * HH], D, HH, wbuf, stg)
            for m in range(NHH):
                p.wait("tensor", "v")
                for k in range(KD):
                    p.op("tensor", lambda e, m=m, k=k, off=off, ln_=ln_: e.matmul(
                        out=ps1[:, 0:ln_],
                        lhsT=wbuf[:, k * HH + m * P:k * HH + (m + 1) * P],
                        rhs=xnb[:, k * T + off:k * T + off + ln_],
                        start=(k == 0), stop=(k == KD - 1)),
                        inc=("t" if k == KD - 1 else None))
                p.wait("vector", "t")
                p.op("vector", lambda e, m=m, hof=hof, ln_=ln_: e.scalar_tensor_tensor(
                    out=mprod_q[:, m * 512:m * 512 + ln_], in0=ps1[:, 0:ln_],
                    scalar=buc[:, hof + m:hof + m + 1],
                    in1=gsil_q[:, m * 512:m * 512 + ln_],
                    op0=ALU.add, op1=ALU.mult), inc="v")
            p.wait("tensor", "v")
            emit_load_wbf(p, Wdd[half * HH:(half + 1) * HH, :], HH, D, wbuf, stg)
            for mo in range(KD):
                p.wait("tensor", "v")
                for k in range(NHH):
                    p.op("tensor", lambda e, mo=mo, k=k, off=off, ln_=ln_: e.matmul(
                        out=pso[:, 0:ln_],
                        lhsT=wbuf[:, k * D + mo * P:k * D + (mo + 1) * P],
                        rhs=mprod_q[:, k * 512:k * 512 + ln_],
                        start=(k == 0), stop=(k == NHH - 1)),
                        inc=("t" if k == NHH - 1 else None))
                p.wait("vector", "t")
                if half == 0:
                    p.op("vector", lambda e, mo=mo, off=off, ln_=ln_: e.tensor_add(
                        out=xt[:, mo * T + off:mo * T + off + ln_],
                        in0=xt[:, mo * T + off:mo * T + off + ln_],
                        in1=pso[:, 0:ln_]), inc="v")
                else:
                    p.op("vector", lambda e, mo=mo, off=off, ln_=ln_: e.scalar_tensor_tensor(
                        out=xt[:, mo * T + off:mo * T + off + ln_], in0=pso[:, 0:ln_],
                        scalar=bdc[:, mo:mo + 1],
                        in1=xt[:, mo * T + off:mo * T + off + ln_],
                        op0=ALU.add, op1=ALU.add), inc="v")

    p.free("bdc", "buc", "bgc", "mprod_q", "gsil_q", "stg", "wbuf")

    ps_t0 = p.psum("ps_t0", [P, 128], BF16)
    ps_t64 = p.psum("ps_t64", [P, 128], BF16)
    stg2 = p.sb("stg2", [P, 2048], F32)
    Mbuf = p.sb("Mbuf", [P, KD * 512], BF16)
    SXM = p.sb("SXM", [P, 16 * 4 * 64], F32)
    SYM = p.sb("SYM", [P, 16 * 4 * 64], F32)
    THC = p.sb("THC", [P, 16 * 4], F32)
    NCC = p.sb("NCC", [P, 16 * 4], F32)
    p.fence_dma()

    keysT2 = p.sb("keysT2", [64, 8 * 64], BF16)
    kstg = p.sb("kstg", [64, 64], F32)
    wqb = p.sb("wqb", [P, KD * 512], BF16)
    wqTb = p.sb("wqTb", [64, P], BF16)
    mps = p.psum("mps", [P, 64], F32)
    for j in range(8):
        p.wait("sync", "v")
        p.dma("sync", kstg[:, :], keysTd[j * 64:(j + 1) * 64, :], inc="dma")
        p.wait("vector", "dma")
        p.op("vector", lambda e, j=j: e.tensor_copy(
            out=keysT2[:, j * 64:(j + 1) * 64], in_=kstg[:, :]), inc="v")
    emit_load_wbf(p, Wqd, D, 512, wqb, stg2)
    for j in range(8):
        for kc in range(KD):
            p.wait("tensor", "v")
            p.op("tensor", lambda e, j=j, kc=kc: e.transpose(
                out=ps_t0[0:64, 0:P],
                in_=wqb[:, kc * 512 + j * 64: kc * 512 + (j + 1) * 64],
                identity=cm.ident_bf[:]), inc="t")
            p.wait("vector", "t")
            p.op("vector", lambda e: e.tensor_copy(out=wqTb[:, :], in_=ps_t0[0:64, 0:P]), inc="v")
            p.wait("tensor", "v")
            p.op("tensor", lambda e, j=j: e.matmul(
                out=mps[:, 0:64], lhsT=wqTb[0:64, :], rhs=keysT2[0:64, j * 64:(j + 1) * 64],
                start=True, stop=True), inc="t")
            p.wait("vector", "t")
            p.op("vector", lambda e, j=j, kc=kc: e.tensor_copy(
                out=Mbuf[:, kc * 512 + j * 64: kc * 512 + (j + 1) * 64],
                in_=mps[:, 0:64]), inc="v")
            p.wait("tensor", "v")
    p.free("mps")
    p.free("wqTb", "wqb", "kstg", "keysT2")
    p.fence_dma()

    simf = p.sb("simf", [P, 4 * T], BF16)
    for mc in range(4):
        for off, ln_ in tok_blocks(T, 512):
            p.wait("tensor", "v")
            for k in range(KD):
                p.op("tensor", lambda e, mc=mc, k=k, off=off, ln_=ln_: e.matmul(
                    out=ps1[:, 0:ln_],
                    lhsT=Mbuf[:, k * 512 + mc * P:k * 512 + (mc + 1) * P],
                    rhs=xnb[:, k * T + off:k * T + off + ln_],
                    start=(k == 0), stop=(k == KD - 1)),
                    inc=("t" if k == KD - 1 else None))
            p.wait("vector", "t")
            p.op("vector", lambda e, mc=mc, off=off, ln_=ln_: e.tensor_copy(
                out=simf[:, mc * T + off:mc * T + off + ln_], in_=ps1[:, 0:ln_]), inc="v")

    sim_x = p.sb("sim_x", [P, 64], F32)
    sim_y = p.sb("sim_y", [P, 64], F32)
    sxv = p.sb("sxv", [P, 8], F32)
    syv = p.sb("syv", [P, 8], F32)
    cv8 = p.sb("cv8", [P, 8], F32)
    e8t = p.sb("e8t", [P, 8], F32)
    msk1 = p.sb("msk1", [P, 64], F32)
    mm1 = p.sb("mm1", [P, 64], F32)
    zcol = p.sb("zcol", [P, 4], F32)
    BIGC = 1.0e30

    for c in range(16):
        for h in range(4):
            for pp, dst in ((0, sim_x), (1, sim_y)):
                mc = (pp * 4 + h) // 2
                sub = ((pp * 4 + h) % 2) * 64
                tpx = ps_t0 if sub == 0 else ps_t64
                p.wait("tensor", "v")
                p.op("tensor", lambda e, mc=mc, sub=sub, c=c, tpx=tpx: e.transpose(
                    out=tpx[0:P, 0:64],
                    in_=simf[sub:sub + 64, mc * T + c * P:mc * T + (c + 1) * P],
                    identity=cm.ident_bf[sub:sub + 64, sub:sub + 64]), inc="t")
                p.wait("vector", "t")
                p.op("vector", lambda e, dst=dst, tpx=tpx: e.tensor_copy(
                    out=dst[:], in_=tpx[0:P, 0:64]), inc="v")
                p.wait("tensor", "v")
            p.wait("vector", "v")
            p.op("vector", lambda e: e.max(out=sxv[:], in_=sim_x[:]), inc="v")
            p.wait("vector", "v")
            p.op("vector", lambda e: e.max(out=syv[:], in_=sim_y[:]), inc="v")
            p.wait("vector", "v")
            for (sv, st_, dst) in ((sxv, sim_x, SXM), (syv, sim_y, SYM)):
                p.op("vector", lambda e, sv=sv, st_=st_: e.tensor_scalar(
                    out=msk1[:], in0=st_[:], scalar1=sv[:, 7:8], scalar2=None,
                    op0=ALU.is_ge), inc="v")
                p.wait("vector", "v")
                p.op("vector", lambda e: e.tensor_scalar(
                    out=mm1[:], in0=msk1[:], scalar1=-1.0, scalar2=BIGC,
                    op0=ALU.add, op1=ALU.mult), inc="v")
                p.wait("vector", "v")
                p.op("vector", lambda e, st_=st_, dst=dst, c=c, h=h: e.tensor_add(
                    out=dst[:, (c * 4 + h) * 64:(c * 4 + h + 1) * 64],
                    in0=st_[:], in1=mm1[:]), inc="v")
                p.wait("vector", "v")
            p.op("vector", lambda e: e.tensor_tensor(
                out=msk1[:].rearrange("p (a b) -> p a b", a=8),
                in0=sxv[:].rearrange("p (a u) -> p a u", u=1).broadcast_to([P, 8, 8]),
                in1=syv[:].rearrange("p (u b) -> p u b", u=1).broadcast_to([P, 8, 8]),
                op=ALU.add), inc="v")
            p.wait("vector", "v")
            p.op("vector", lambda e: e.max(out=cv8[:], in_=msk1[:, 0:64]), inc="v")
            p.wait("vector", "v")
            p.op("vector", lambda e, c=c, h=h: e.tensor_copy(
                out=THC[:, c * 4 + h:c * 4 + h + 1], in_=cv8[:, 7:8]), inc=None)
            p.op("vector", lambda e: e.tensor_scalar(
                out=zcol[:, 0:1], in0=cv8[:, 0:1], scalar1=-1.0, scalar2=None,
                op0=ALU.mult), inc="v")
            p.wait("vector", "v")
            p.wait("scalar", "v")
            p.op("scalar", lambda e: e.activation(
                out=e8t[:], in_=cv8[:], func=AF.Exp, bias=zcol[:, 0:1]), inc="s")
            p.wait("vector", "s")
            p.op("vector", lambda e: e.tensor_reduce(
                out=zcol[:, 1:2], in_=e8t[:], axis=mybir.AxisListType.X,
                op=ALU.add), inc="v")
            p.wait("vector", "v")
            p.wait("scalar", "v")
            p.op("scalar", lambda e: e.activation(
                out=zcol[:, 2:3], in_=zcol[:, 1:2], func=AF.Ln), inc="s")
            p.wait("vector", "s")
            p.op("vector", lambda e, c=c, h=h: e.tensor_sub(
                out=NCC[:, c * 4 + h:c * 4 + h + 1], in0=zcol[:, 0:1],
                in1=zcol[:, 2:3]), inc="v")
            p.wait("vector", "v")
    p.free("zcol", "mm1", "msk1", "e8t", "cv8", "syv", "sxv", "sim_y", "sim_x",
           "simf")
    p.fence_dma()

    EQ = 1024
    NEC = EQ // P
    dwnT = p.sb("dwnT", [P, KD * EQ], BF16)
    upb = p.sb("upb", [P, NEC * D], BF16)
    sgrid = p.sb("sgrid", [P, EQ], F32)
    ugrid = p.sb("ugrid", [P, EQ], BF16)
    ggrid = p.sb("ggrid", [P, EQ], BF16)
    gtmp = p.sb("gtmp", [P, EQ], BF16)
    sdots = p.sb("sdots", [P, EQ], BF16)
    ctrT = p.sb("ctrT", [P, NEC * P], BF16)
    dstg = p.sb("dstg", [P, D], F32)
    dps = p.psum("dps", [P, 512], F32)
    ups = p.psum("ups", [P, 128], F32)

    for q in range(4):
        for ec in range(NEC):
            p.wait("sync", "v")
            p.dma("sync", dstg[:, 0:D], downd[(q * NEC + ec) * P:(q * NEC + ec + 1) * P, :], inc="dma")
            p.wait("vector", "dma")
            p.op("vector", lambda e: e.tensor_copy(out=gtmp[:, 0:D], in_=dstg[:, 0:D]), inc="v")
            p.wait("vector", "v")
            for dc in range(KD):
                p.wait("tensor", "v")
                p.op("tensor", lambda e, dc=dc: e.transpose(
                    out=ps_t0[:, 0:P], in_=gtmp[:, dc * P:(dc + 1) * P],
                    identity=cm.ident_bf[:]), inc="t")
                p.wait("vector", "t")
                p.op("vector", lambda e, dc=dc, ec=ec: e.tensor_copy(
                    out=dwnT[:, dc * EQ + ec * P: dc * EQ + (ec + 1) * P],
                    in_=ps_t0[:, 0:P]), inc="v")
                p.wait("tensor", "v")
        emit_load_wbf(p, upd[q * EQ:(q + 1) * EQ, :], EQ, D, upb, dstg)

        for c in range(16):
            for eb in range(2):
                p.wait("tensor", "v")
                for k in range(KD):
                    p.op("tensor", lambda e, c=c, eb=eb, k=k: e.matmul(
                        out=dps[:, 0:512],
                        lhsT=xnb[:, k * T + c * P:k * T + (c + 1) * P],
                        rhs=dwnT[:, k * EQ + eb * 512:k * EQ + (eb + 1) * 512],
                        start=(k == 0), stop=(k == KD - 1)),
                        inc=("t" if k == KD - 1 else None))
                p.wait("scalar", "t")
                p.op("scalar", lambda e, eb=eb: e.activation(
                    out=sdots[:, eb * 512:(eb + 1) * 512], in_=dps[:, 0:512],
                    func=AF.Silu), inc="v")
            for h in range(4):
                cx = (c * 4 + h)
                p.wait("vector", "v")
                p.op("vector", lambda e, cx=cx, q=q: e.tensor_tensor(
                    out=sgrid[:].rearrange("p (a b) -> p a b", a=16),
                    in0=SXM[:, cx * 64 + q * 16: cx * 64 + (q + 1) * 16]
                        .rearrange("p (a u) -> p a u", u=1).broadcast_to([P, 16, 64]),
                    in1=SYM[:, cx * 64:(cx + 1) * 64]
                        .rearrange("p (u b) -> p u b", u=1).broadcast_to([P, 16, 64]),
                    op=ALU.add), inc="v")
                p.wait("scalar", "v")
                p.op("scalar", lambda e, cx=cx: e.activation(
                    out=ugrid[:], in_=sgrid[:], func=AF.Exp,
                    bias=NCC[:, cx:cx + 1]), inc="s")
                p.op("vector", lambda e, cx=cx: e.tensor_scalar(
                    out=gtmp[:], in0=sgrid[:], scalar1=THC[:, cx:cx + 1],
                    scalar2=None, op0=ALU.is_ge), inc="v")
                p.wait("vector", "s")
                p.wait("vector", "v")
                if h == 0:
                    p.op("vector", lambda e: e.tensor_mul(
                        out=ggrid[:], in0=ugrid[:], in1=gtmp[:]), inc="v")
                else:
                    p.op("vector", lambda e: e.tensor_mul(
                        out=gtmp[:], in0=ugrid[:], in1=gtmp[:]), inc="v")
                    p.wait("vector", "v")
                    p.op("vector", lambda e: e.tensor_add(
                        out=ggrid[:], in0=ggrid[:], in1=gtmp[:]), inc="v")
                p.wait("vector", "v")
            p.op("vector", lambda e: e.tensor_mul(
                out=ggrid[:], in0=ggrid[:], in1=sdots[:]), inc="v")
            for ec in range(NEC):
                p.wait("tensor", "v")
                p.op("tensor", lambda e, ec=ec: e.transpose(
                    out=ps_t0[:, 0:P], in_=ggrid[:, ec * P:(ec + 1) * P],
                    identity=cm.ident_bf[:]), inc="t")
                p.wait("vector", "t")
                p.op("vector", lambda e, ec=ec: e.tensor_copy(
                    out=ctrT[:, ec * P:(ec + 1) * P], in_=ps_t0[:, 0:P]), inc="v")
                p.wait("tensor", "v")
            for dc in range(KD):
                p.wait("tensor", "v")
                for ec in range(NEC):
                    p.op("tensor", lambda e, dc=dc, ec=ec: e.matmul(
                        out=ups[:, 0:P],
                        lhsT=upb[:, ec * D + dc * P:ec * D + (dc + 1) * P],
                        rhs=ctrT[:, ec * P:(ec + 1) * P],
                        start=(ec == 0), stop=(ec == NEC - 1)),
                        inc=("t" if ec == NEC - 1 else None))
                p.wait("vector", "t")
                p.op("vector", lambda e, dc=dc, c=c: e.tensor_add(
                    out=xt[:, dc * T + c * P:dc * T + (c + 1) * P],
                    in0=xt[:, dc * T + c * P:dc * T + (c + 1) * P],
                    in1=ups[:, 0:P]), inc="v")

    p.wait("sync", "v")
    for k in range(KD):
        p.dma("sync", yd[k * P:(k + 1) * P, :], xt[:, k * T:(k + 1) * T], inc="st")
    p.wait("sync", "st")
    p.finish()
    return p


def run_l0_ffn(x, inp):
    pr = _get_prog("l0ffn", build_l0_ffn)
    xs = _fm_slices(x)
    keys = np.asarray(inp["L0_ffn_keys"], np.float32)
    keysT = np.ascontiguousarray(keys.transpose(2, 0, 3, 1).reshape(512, 64))
    base = {
        "Wg": np.asarray(inp["L0_ffn_Wg"], np.float32),
        "Wu": np.asarray(inp["L0_ffn_Wu"], np.float32),
        "Wd": np.asarray(inp["L0_ffn_Wd"], np.float32),
        "bg": np.asarray(inp["L0_ffn_bg"], np.float32).reshape(4096, 1),
        "bu": np.asarray(inp["L0_ffn_bu"], np.float32).reshape(4096, 1),
        "bd": np.asarray(inp["L0_ffn_bd"], np.float32).reshape(D, 1),
        "Wq": np.asarray(inp["L0_ffn_Wq"], np.float32),
        "keysT": keysT,
        "down_embed": np.asarray(inp["L0_ffn_down_embed"], np.float32),
        "up_embed": np.asarray(inp["L0_ffn_up_embed"], np.float32),
        **Common.host_inputs(),
    }
    in_maps = [{**base, "x": xs[c]} for c in range(NCORES)]
    res = run_bass_kernel_spmd(pr.nc, in_maps, core_ids=list(range(NCORES)))
    return _unshard([res.results[c]["y"] for c in range(NCORES)])



def emit_allgather(p, sb_in, nfree, tag, npart=P, dtype=F32, readback=True):
    din = p.dram(f"agi_{tag}", [npart, nfree], dtype)
    dout = p.dram(f"ago_{tag}", [8 * npart, nfree], dtype, shared=True)
    p.wait("gpsimd", "v")
    p.wait("gpsimd", "s")
    p.dma("gpsimd", din.ap(), sb_in[0:npart, 0:nfree], inc="ag")
    p.wait("gpsimd", "ag")
    p.op("gpsimd", lambda e: e.collective_compute(
        "AllGather", ALU.bypass, replica_groups=[list(range(8))],
        ins=[din.ap()], outs=[dout.ap()]), inc="cc")
    p.wait("gpsimd", "cc")
    gout = None
    if readback:
        gout = p.sb(f"agg_{tag}", [npart, 8 * nfree], dtype)
        for r in range(8):
            p.dma("gpsimd", gout[0:npart, r * nfree:(r + 1) * nfree],
                  dout.ap()[r * npart:(r + 1) * npart, :], inc="ag")
        p.wait("vector", "ag")
        p.wait("tensor", "ag")
    return gout, dout


TEA = T + 3


def build_l1_attn_a():
    p = Prog("l1aa")
    xd = p.dram("x", [D, TEA], F32, kind="ExternalInput").ap()
    cwd = p.dram("conv_w", [D, 4], F32, kind="ExternalInput").ap()
    cbd = p.dram("conv_b", [D, 1], F32, kind="ExternalInput").ap()
    wrd = p.dram("Wr_w", [D, D], F32, kind="ExternalInput").ap()
    wrbd = p.dram("Wr_b", [D, 1], F32, kind="ExternalInput").ap()
    wid = p.dram("Wi_w", [D, D], F32, kind="ExternalInput").ap()
    wibd = p.dram("Wi_b", [D, 1], F32, kind="ExternalInput").ap()
    lad = p.dram("log_a", [D, 1], F32, kind="ExternalInput").ap()
    owd = p.dram("out_w", [D, D], F32, kind="ExternalInput").ap()
    nwd = p.dram("norm_w", [D, 1], F32, kind="ExternalInput").ap()
    pmd = p.dram("prevmask", [P, 8], F32, kind="ExternalInput").ap()
    yd = p.dram("y", [D, T], F32, kind="ExternalOutput").ap()
    rspill = p.dram("rspill", [D, T], BF16)
    uspill = p.dram("uspill", [D, T], BF16)
    hspill = p.dram("hspill", [D, T], BF16)

    cm = Common(p)
    xt = p.sb("xt", [P, KD * TEA], F32)
    for k in range(KD):
        p.dma("sync", xt[:, k * TEA:(k + 1) * TEA], xd[k * P:(k + 1) * P, :])
    cwc = p.sb("cwc", [P, KD * 4], F32)
    cbc = p.sb("cbc", [P, KD], F32)
    rbc = p.sb("rbc", [P, KD], F32)
    ibc = p.sb("ibc", [P, KD], F32)
    lac = p.sb("lac", [P, KD], F32)
    nwc = p.sb("nwc", [P, KD], F32)
    pmc = p.sb("pmc", [P, 8], F32)
    ones_T = p.sb("ones_T", [P, T], F32)
    for k in range(KD):
        p.dma("sync", cwc[:, k * 4:(k + 1) * 4], cwd[k * P:(k + 1) * P, :])
        p.dma("sync", cbc[:, k:k + 1], cbd[k * P:(k + 1) * P, :])
        p.dma("sync", rbc[:, k:k + 1], wrbd[k * P:(k + 1) * P, :])
        p.dma("sync", ibc[:, k:k + 1], wibd[k * P:(k + 1) * P, :])
        p.dma("sync", lac[:, k:k + 1], lad[k * P:(k + 1) * P, :])
        p.dma("sync", nwc[:, k:k + 1], nwd[k * P:(k + 1) * P, :])
    p.dma("sync", pmc[:, :], pmd)
    p.op("vector", lambda e: e.memset(ones_T[:], 1.0), inc="v")
    p.wait("vector", "dma")
    p.wait("vector", "v")
    p.wait("scalar", "dma")
    p.wait("tensor", "dma")
    p.wait("scalar", "v")
    p.op("scalar", lambda e: e.activation(out=lac[:], in_=lac[:], func=AF.Sigmoid), inc="s")
    p.wait("scalar", "s")
    p.op("scalar", lambda e: e.activation(out=lac[:], in_=lac[:], func=AF.Ln), inc="s")
    p.wait("vector", "s")
    p.op("vector", lambda e: e.tensor_scalar(out=lac[:], in0=lac[:], scalar1=8.0,
                                             scalar2=None, op0=ALU.mult), inc="v")

    xcb = p.sb("xcb", [P, KD * T], BF16)
    rho_bc, mrho_bc, ln_names = emit_ln(p, cm, xt, TEA, "a1")

    un1 = p.sb("un1", [P, TEA], F32)
    sc1 = p.sb("sc1", [P, T], F32)
    for k in range(KD):
        p.wait("vector", "v")
        p.wait("vector", "s")
        p.op("vector", lambda e, k=k: e.scalar_tensor_tensor(
            out=un1[:], in0=xt[:, k * TEA:(k + 1) * TEA],
            scalar=1.0, in1=rho_bc[:], op0=ALU.mult, op1=ALU.mult), inc=None)
        p.op("vector", lambda e, k=k: e.tensor_add(
            out=un1[:], in0=un1[:], in1=mrho_bc[:]), inc=None)
        p.op("vector", lambda e, k=k: e.tensor_scalar(
            out=sc1[:], in0=un1[:, 0:T], scalar1=cwc[:, k * 4:k * 4 + 1],
            scalar2=cbc[:, k:k + 1], op0=ALU.mult, op1=ALU.add), inc=None)
        for kk in range(1, 4):
            p.op("vector", lambda e, k=k, kk=kk: e.scalar_tensor_tensor(
                out=sc1[:], in0=un1[:, kk:kk + T],
                scalar=cwc[:, k * 4 + kk:k * 4 + kk + 1],
                in1=sc1[:], op0=ALU.mult, op1=ALU.add), inc=None)
        p.op("vector", lambda e, k=k: e.tensor_copy(
            out=xcb[:, k * T:(k + 1) * T], in_=sc1[:]), inc="v")
    p.free("sc1", "un1")
    p.free(*ln_names)
    p.free("ln_mrho_a1", "ln_rho_a1")
    p.fence_dma()

    wstg = p.sb("wstg", [P, D], F32)
    wrT = p.sb("wrT", [P, KD * D], BF16)
    wiT = p.sb("wiT", [P, KD * D], BF16)
    wtb = p.sb("wtb", [P, D], BF16)
    wtp = p.psum("wtp", [P, 128], BF16)
    for wdram, wdst in ((wrd, wrT), (wid, wiT)):
        for m in range(KD):
            p.wait("sync", "v")
            p.dma("sync", wstg[:, 0:D], wdram[m * P:(m + 1) * P, :], inc="dma")
            p.wait("vector", "dma")
            p.op("vector", lambda e: e.tensor_copy(out=wtb[:, 0:D], in_=wstg[:, 0:D]), inc="v")
            p.wait("vector", "v")
            for k in range(KD):
                p.wait("tensor", "v")
                p.op("tensor", lambda e, k=k: e.transpose(
                    out=wtp[:, 0:P], in_=wtb[:, k * P:(k + 1) * P],
                    identity=cm.ident_bf[:]), inc="t")
                p.wait("vector", "t")
                p.op("vector", lambda e, wdst=wdst, m=m, k=k: e.tensor_copy(
                    out=wdst[:, k * D + m * P:k * D + (m + 1) * P],
                    in_=wtp[:, 0:P]), inc="v")
                p.wait("tensor", "v")

    ps1 = p.psum("ps1", [P, 512], F32)
    lsum = p.sb("lsum", [P, KD], F32)
    rb4 = p.sb("rb4", [P, T], BF16)
    sca = p.sb("sca", [P, T], F32)
    scb = p.sb("scb", [P, T], F32)
    scc = p.sb("scc", [P, T], F32)
    LN7 = float(np.log(1e-7))
    for m in range(KD):
        p.wait("scalar", "rs")
        for off, ln_ in tok_blocks(T, 512):
            p.wait("tensor", "v")
            for k in range(KD):
                p.op("tensor", lambda e, m=m, k=k, off=off, ln_=ln_: e.matmul(
                    out=ps1[:, 0:ln_],
                    lhsT=wrT[:, k * D + m * P:k * D + (m + 1) * P],
                    rhs=xcb[:, k * T + off:k * T + off + ln_],
                    start=(k == 0), stop=(k == KD - 1)),
                    inc=("t" if k == KD - 1 else None))
            p.wait("scalar", "t")
            p.op("scalar", lambda e, m=m, off=off, ln_=ln_: e.activation(
                out=rb4[:, off:off + ln_], in_=ps1[:, 0:ln_],
                func=AF.Sigmoid, bias=rbc[:, m:m + 1]), inc="v")
        p.wait("sync", "v")
        p.dma("sync", rspill[m * P:(m + 1) * P, :], rb4[:, :], inc="rs")
        p.wait("vector", "v")
        p.op("vector", lambda e, m=m: e.tensor_scalar(
            out=sca[:], in0=rb4[:, :], scalar1=lac[:, m:m + 1],
            scalar2=LN7, op0=ALU.mult, op1=ALU.max), inc="v")
        p.wait("vector", "v")
        p.op("vector", lambda e, m=m: e.tensor_reduce(
            out=lsum[:, m:m + 1], in_=sca[:], axis=mybir.AxisListType.X,
            op=ALU.add), inc="v")
        p.wait("vector", "v")

    gl, _ = emit_allgather(p, lsum, KD, "ls")
    coff = p.sb("coff", [P, KD], F32)
    p.op("vector", lambda e: e.memset(coff[:], 0.0), inc="v")
    p.wait("vector", "v")
    for r in range(8):
        p.op("vector", lambda e, r=r: e.scalar_tensor_tensor(
            out=coff[:], in0=gl[:, r * KD:(r + 1) * KD], scalar=pmc[:, r:r + 1],
            in1=coff[:], op0=ALU.mult, op1=ALU.add), inc="v")
        p.wait("vector", "v")

    usum = p.sb("usum", [P, KD], F32)
    ub4 = p.sb("ub4", [P, T], BF16)
    for m in range(KD):
        p.wait("sync", "s")
        p.wait("sync", "v")
        p.dma("sync", rb4[:, :], rspill[m * P:(m + 1) * P, :], inc="rs")
        p.wait("vector", "rs")
        p.wait("scalar", "rs")
        for off, ln_ in tok_blocks(T, 512):
            p.wait("tensor", "v")
            p.wait("tensor", "s")
            for k in range(KD):
                p.op("tensor", lambda e, m=m, k=k, off=off, ln_=ln_: e.matmul(
                    out=ps1[:, 0:ln_],
                    lhsT=wiT[:, k * D + m * P:k * D + (m + 1) * P],
                    rhs=xcb[:, k * T + off:k * T + off + ln_],
                    start=(k == 0), stop=(k == KD - 1)),
                    inc=("t" if k == KD - 1 else None))
            p.wait("scalar", "t")
            p.op("scalar", lambda e, m=m, off=off, ln_=ln_: e.activation(
                out=sca[:, off:off + ln_], in_=ps1[:, 0:ln_],
                func=AF.Sigmoid, bias=ibc[:, m:m + 1]), inc="s")
        p.wait("scalar", "v")
        p.wait("scalar", "s")
        p.op("scalar", lambda e, m=m: e.activation(
            out=scb[:], in_=rb4[:, :], func=AF.Exp, scale=lac[:, m:m + 1]), inc="s")
        p.wait("vector", "s")
        p.op("vector", lambda e: e.tensor_mul(out=scc[:], in0=scb[:], in1=scb[:]), inc="v")
        p.wait("vector", "v")
        p.op("vector", lambda e: e.tensor_scalar(
            out=scc[:], in0=scc[:], scalar1=-1.0, scalar2=1.0,
            op0=ALU.mult, op1=ALU.add), inc="v")
        p.wait("vector", "v")
        p.op("vector", lambda e: e.tensor_scalar(
            out=scc[:], in0=scc[:], scalar1=1e-6, scalar2=None,
            op0=ALU.max), inc="v")
        p.wait("scalar", "v")
        p.op("scalar", lambda e: e.activation(out=scc[:], in_=scc[:], func=AF.Sqrt), inc="s")
        p.wait("vector", "s")
        p.op("vector", lambda e: e.tensor_mul(out=sca[:], in0=sca[:], in1=scc[:]), inc="v")
        p.wait("vector", "v")
        p.op("vector", lambda e, m=m: e.tensor_mul(
            out=sca[:], in0=sca[:], in1=xcb[:, m * T:(m + 1) * T]), inc="v")
        p.wait("vector", "v")
        p.op("vector", lambda e, m=m: e.tensor_scalar(
            out=scb[:], in0=rb4[:, :], scalar1=lac[:, m:m + 1],
            scalar2=LN7, op0=ALU.mult, op1=ALU.max), inc="v")
        p.wait("vector", "v")
        p.op("vector", lambda e: e.tensor_tensor_scan(
            out=scc[:], data0=ones_T[:], data1=scb[:], initial=0.0,
            op0=ALU.mult, op1=ALU.add), inc="v")
        p.wait("vector", "v")
        p.op("vector", lambda e, m=m: e.tensor_scalar(
            out=scc[:], in0=scc[:], scalar1=coff[:, m:m + 1], scalar2=-80.0,
            op0=ALU.add, op1=ALU.max), inc="v")
        p.wait("scalar", "v")
        p.op("scalar", lambda e: e.activation(out=scb[:], in_=scc[:], func=AF.Exp,
                                              scale=-1.0), inc="s")
        p.wait("vector", "s")
        p.op("vector", lambda e: e.tensor_mul(out=sca[:], in0=sca[:], in1=scb[:]), inc="v")
        p.wait("vector", "v")
        p.wait("vector", "rs")
        p.op("vector", lambda e: e.tensor_copy(out=ub4[:], in_=sca[:]), inc="v")
        p.op("vector", lambda e: e.tensor_tensor_scan(
            out=scb[:], data0=ones_T[:], data1=sca[:], initial=0.0,
            op0=ALU.mult, op1=ALU.add), inc="v")
        p.wait("vector", "v")
        p.op("vector", lambda e, m=m: e.tensor_copy(
            out=usum[:, m:m + 1], in_=scb[:, T - 1:T]), inc="v")
        p.wait("sync", "v")
        p.dma("sync", uspill[m * P:(m + 1) * P, :], ub4[:, :], inc="rs")
        p.wait("vector", "v")

    gu, _ = emit_allgather(p, usum, KD, "us")
    uoff = p.sb("uoff", [P, KD], F32)
    p.op("vector", lambda e: e.memset(uoff[:], 0.0), inc="v")
    p.wait("vector", "v")
    for r in range(8):
        p.op("vector", lambda e, r=r: e.scalar_tensor_tensor(
            out=uoff[:], in0=gu[:, r * KD:(r + 1) * KD], scalar=pmc[:, r:r + 1],
            in1=uoff[:], op0=ALU.mult, op1=ALU.add), inc="v")
        p.wait("vector", "v")

    for m in range(KD):
        p.wait("sync", "s")
        p.wait("sync", "v")
        p.dma("sync", rb4[:, :], rspill[m * P:(m + 1) * P, :], inc="rs")
        p.wait("vector", "rs")
        p.op("vector", lambda e, m=m: e.tensor_scalar(
            out=scb[:], in0=rb4[:, :], scalar1=lac[:, m:m + 1],
            scalar2=LN7, op0=ALU.mult, op1=ALU.max), inc="v")
        p.wait("vector", "v")
        p.op("vector", lambda e: e.tensor_tensor_scan(
            out=scc[:], data0=ones_T[:], data1=scb[:], initial=0.0,
            op0=ALU.mult, op1=ALU.add), inc="v")
        p.wait("vector", "v")
        p.op("vector", lambda e, m=m: e.tensor_scalar(
            out=scc[:], in0=scc[:], scalar1=coff[:, m:m + 1], scalar2=-80.0,
            op0=ALU.add, op1=ALU.max), inc="v")
        p.wait("scalar", "v")
        p.op("scalar", lambda e: e.activation(out=scb[:], in_=scc[:], func=AF.Exp), inc="s")
        p.wait("vector", "s")
        p.wait("sync", "v")
        p.dma("sync", ub4[:, :], uspill[m * P:(m + 1) * P, :], inc="rs")
        p.wait("vector", "rs")
        p.op("vector", lambda e, m=m: e.tensor_tensor_scan(
            out=scc[:], data0=ones_T[:], data1=ub4[:, :],
            initial=0.0, op0=ALU.mult, op1=ALU.add), inc="v")
        p.wait("vector", "v")
        p.op("vector", lambda e, m=m: e.scalar_tensor_tensor(
            out=scc[:], in0=scc[:], scalar=uoff[:, m:m + 1], in1=scb[:],
            op0=ALU.add, op1=ALU.mult), inc="v")
        p.wait("vector", "v")
        p.wait("vector", "rs")
        p.op("vector", lambda e: e.tensor_copy(out=rb4[:, :], in_=scc[:]), inc="v")
        p.wait("vector", "v")
        p.wait("sync", "v")
        p.dma("sync", hspill[m * P:(m + 1) * P, :], rb4[:, :], inc="rs")

    for m in range(KD):
        p.wait("sync", "v")
        p.wait("sync", "t")
        p.dma("sync", wstg[:, 0:D], owd[m * P:(m + 1) * P, :], inc="dma")
        p.wait("vector", "dma")
        p.op("vector", lambda e: e.tensor_copy(out=wtb[:, 0:D], in_=wstg[:, 0:D]), inc="v")
        p.wait("vector", "v")
        for k in range(KD):
            p.wait("tensor", "v")
            p.op("tensor", lambda e, k=k: e.transpose(
                out=wtp[:, 0:P], in_=wtb[:, k * P:(k + 1) * P],
                identity=cm.ident_bf[:]), inc="t")
            p.wait("vector", "t")
            p.op("vector", lambda e, m=m, k=k: e.tensor_copy(
                out=wrT[:, k * D + m * P:k * D + (m + 1) * P],
                in_=wtp[:, 0:P]), inc="v")
            p.wait("tensor", "v")

    hblk = p.sb("hblk", [P, KD * 512], BF16)
    yblk = p.sb("yblk", [P, KD * 512], BF16)
    msps = p.psum("msps", [1, 512], F32)
    msrow = p.sb("msrow", [1, 512], F32)
    sqblk = p.sb("sqblk", [P, 512], F32)
    rstd_bc = p.sb("rstd_bc", [P, 512], F32)
    pbb = p.psum("pbb", [P, 512], F32)
    sc512 = p.sb("sc512", [P, 512], F32)
    for off, ln_ in tok_blocks(T, 512):
        p.wait("sync", "t")
        p.wait("sync", "rs")
        for k in range(KD):
            p.dma("sync", hblk[:, k * 512:k * 512 + ln_],
                  hspill[k * P:(k + 1) * P, off:off + ln_], inc="rs")
        p.wait("tensor", "rs")
        for m in range(KD):
            p.wait("tensor", "v")
            for k in range(KD):
                p.op("tensor", lambda e, m=m, k=k, ln_=ln_: e.matmul(
                    out=ps1[:, 0:ln_],
                    lhsT=wrT[:, k * D + m * P:k * D + (m + 1) * P],
                    rhs=hblk[:, k * 512:k * 512 + ln_],
                    start=(k == 0), stop=(k == KD - 1)),
                    inc=("t" if k == KD - 1 else None))
            p.wait("vector", "t")
            p.op("vector", lambda e, m=m, ln_=ln_: e.tensor_copy(
                out=yblk[:, m * 512:m * 512 + ln_], in_=ps1[:, 0:ln_]), inc="v")
            p.wait("scalar", "v")
            p.op("scalar", lambda e, ln_=ln_: e.activation(
                out=sqblk[:, 0:ln_], in_=ps1[:, 0:ln_], func=AF.Square), inc="s")
            p.wait("tensor", "s")
            p.op("tensor", lambda e, m=m, ln_=ln_: e.matmul(
                out=msps[0:1, 0:ln_], lhsT=cm.ones_col[:, 0:1], rhs=sqblk[:, 0:ln_],
                start=(m == 0), stop=(m == KD - 1)),
                inc=("t" if m == KD - 1 else None))
        p.wait("vector", "t")
        p.op("vector", lambda e, ln_=ln_: e.tensor_scalar(
            out=msrow[:, 0:ln_], in0=msps[0:1, 0:ln_], scalar1=1.0 / D,
            scalar2=1e-6, op0=ALU.mult, op1=ALU.add), inc="v")
        p.wait("scalar", "v")
        p.op("scalar", lambda e, ln_=ln_: e.activation(
            out=msrow[:, 0:ln_], in_=msrow[:, 0:ln_], func=AF.Ln), inc="s")
        p.wait("scalar", "s")
        p.op("scalar", lambda e, ln_=ln_: e.activation(
            out=msrow[:, 0:ln_], in_=msrow[:, 0:ln_], func=AF.Exp,
            scale=-0.5), inc="s")
        p.wait("tensor", "s")
        p.wait("vector", "s")
        emit_bcast_row(p, cm, msrow, rstd_bc, ln_, pbb)
        for m in range(KD):
            p.wait("vector", "v")
            p.wait("vector", "st")
            p.op("vector", lambda e, m=m, ln_=ln_: e.scalar_tensor_tensor(
                out=sc512[:, 0:ln_], in0=yblk[:, m * 512:m * 512 + ln_],
                scalar=nwc[:, m:m + 1], in1=rstd_bc[:, 0:ln_],
                op0=ALU.mult, op1=ALU.mult), inc="v")
            p.wait("vector", "v")
            p.op("vector", lambda e, m=m, off=off, ln_=ln_: e.tensor_add(
                out=sc512[:, 0:ln_], in0=sc512[:, 0:ln_],
                in1=xt[:, m * TEA + 3 + off:m * TEA + 3 + off + ln_]), inc="v")
            p.wait("sync", "v")
            p.dma("sync", yd[m * P:(m + 1) * P, off:off + ln_], sc512[:, 0:ln_], inc="st")
    p.wait("sync", "st")
    p.finish()
    return p


def run_l1_attn_a(x, inp):
    pr = _get_prog("l1aa", build_l1_attn_a)
    xs = []
    for c in range(NCORES):
        b, s0 = c // 4, (c % 4) * T
        ext = np.zeros((D, TEA), np.float32)
        lo = s0 - 3
        src_lo = max(lo, 0)
        ext[:, src_lo - lo:] = x[b, src_lo:s0 + T, :].T
        xs.append(ext)
    base = {
        "conv_w": np.asarray(inp["L1_attn_a_conv_weight"], np.float32).reshape(D, 4),
        "conv_b": np.asarray(inp["L1_attn_a_conv_bias"], np.float32).reshape(D, 1),
        "Wr_w": np.asarray(inp["L1_attn_a_Wr_w"], np.float32),
        "Wr_b": np.asarray(inp["L1_attn_a_Wr_b"], np.float32).reshape(D, 1),
        "Wi_w": np.asarray(inp["L1_attn_a_Wi_w"], np.float32),
        "Wi_b": np.asarray(inp["L1_attn_a_Wi_b"], np.float32).reshape(D, 1),
        "log_a": np.asarray(inp["L1_attn_a_log_a"], np.float32).reshape(D, 1),
        "out_w": np.asarray(inp["L1_attn_a_out_w"], np.float32),
        "norm_w": np.asarray(inp["L1_attn_a_norm_w"], np.float32).reshape(D, 1),
        **Common.host_inputs(),
    }
    in_maps = []
    for c in range(NCORES):
        pm = np.zeros((P, 8), np.float32)
        for r in range(8):
            if r // 4 == c // 4 and r < c:
                pm[:, r] = 1.0
        in_maps.append({**base, "x": xs[c], "prevmask": pm})
    res = run_bass_kernel_spmd(pr.nc, in_maps, core_ids=list(range(NCORES)))
    return _unshard([res.results[c]["y"] for c in range(NCORES)])




def build_l0_attn_b():
    p = Prog("l0ab")
    xd = p.dram("x", [D, T], F32, kind="ExternalInput").ap()
    wqd = p.dram("Wq", [D, D], F32, kind="ExternalInput").ap()
    wkd = p.dram("Wk", [D, D], F32, kind="ExternalInput").ap()
    wvd = p.dram("Wv", [D, D], F32, kind="ExternalInput").ap()
    wbd = p.dram("Wb", [D, 16], F32, kind="ExternalInput").ap()
    wod = p.dram("Wo", [D, D], F32, kind="ExternalInput").ap()
    ln1d = p.dram("ln1", [D, 1], F32, kind="ExternalInput").ap()
    ln2d = p.dram("ln2", [D, 1], F32, kind="ExternalInput").ap()
    wgd = p.dram("Wg", [D, 4096], F32, kind="ExternalInput").ap()
    wud = p.dram("Wu", [D, 4096], F32, kind="ExternalInput").ap()
    wdd = p.dram("Wd", [4096, D], F32, kind="ExternalInput").ap()
    trilSd = p.dram("trilS", [P, P], BF16, kind="ExternalInput").ap()
    triuSNd = p.dram("triuSN", [P, P], BF16, kind="ExternalInput").ap()
    triuId = p.dram("triuI", [P, P], BF16, kind="ExternalInput").ap()
    seld = p.dram("selk", [16, D], F32, kind="ExternalInput").ap()
    pmd = p.dram("prevmask", [P, 8], F32, kind="ExternalInput").ap()
    yd = p.dram("y", [D, T], F32, kind="ExternalOutput").ap()
    qspill = p.dram("qspill", [D, T], BF16)
    kspill = p.dram("kspill", [D, T], BF16)
    vspill = p.dram("vspill", [D, T], BF16)
    qptd = p.dram("qptd", [16 * 64, T], BF16)
    NC = T // P

    cm = Common(p)
    p.constcol(1e-6, 2)
    xt = p.sb("xt", [P, KD * T], F32)
    for k in range(KD):
        p.dma("sync", xt[:, k * T:(k + 1) * T], xd[k * P:(k + 1) * P, :])
    trilS = p.sb("c_trilS", [P, P], BF16)
    triuSN = p.sb("c_triuSN", [P, P], BF16)
    triuI = p.sb("c_triuI", [P, P], BF16)
    selk = p.sb("c_selk", [16, D], F32)
    pmc = p.sb("c_pmc", [P, 8], F32)
    ln1c = p.sb("ln1c", [P, KD], F32)
    ln2c = p.sb("ln2c", [P, KD], F32)
    p.dma("sync", trilS[:], trilSd)
    p.dma("sync", triuSN[:], triuSNd)
    p.dma("sync", triuI[:], triuId)
    p.dma("sync", selk[:], seld)
    p.dma("sync", pmc[:], pmd)
    for k in range(KD):
        p.dma("sync", ln1c[:, k:k + 1], ln1d[k * P:(k + 1) * P, :])
        p.dma("sync", ln2c[:, k:k + 1], ln2d[k * P:(k + 1) * P, :])
    p.wait("vector", "dma")
    p.wait("scalar", "dma")
    p.wait("tensor", "dma")

    rstd2_bc = p.sb("rstd2_bc", [P, T], F32)
    rho_bc, mrho_bc, ln_names = emit_ln(p, cm, xt, T, "ab")
    rhor = p._ln_rows["B"]
    pbps = p._ln_rows["pb"]
    rowR = p.sb("rowR", [1, T], F32)
    p.wait("vector", "v")
    p.op("vector", lambda e: e.tensor_mul(out=rowR[:], in0=rhor[0:1, :], in1=rhor[0:1, :]), inc="v")
    p.wait("vector", "v")
    p.op("vector", lambda e: e.tensor_scalar(
        out=rowR[:], in0=rowR[:], scalar1=-1e-5, scalar2=1.0 + 1e-6,
        op0=ALU.mult, op1=ALU.add), inc="v")
    p.wait("scalar", "v")
    p.op("scalar", lambda e: e.activation(out=rowR[:], in_=rowR[:], func=AF.Ln), inc="s")
    p.wait("scalar", "s")
    p.op("scalar", lambda e: e.activation(out=rowR[:], in_=rowR[:], func=AF.Exp,
                                          scale=-0.5), inc="s")
    p.wait("tensor", "s")
    emit_bcast_row(p, cm, rowR, rstd2_bc, T, pbps)
    p.free("rowR")
    p.free(*ln_names)

    Otm = p.sb("Otm", [P, NC * 16 * 64], BF16)
    btm = p.sb("btm", [P, NC * 16], F32)
    btmN = p.sb("btmN", [P, NC * 16], F32)
    x2b = p.sb("x2b", [P, KD * T], BF16)
    wb_ = p.sb("wb_", [P, KD * P], BF16)
    wstg = p.sb("wstg", [P, P], F32)
    prj = p.sb("prj", [P, T], F32)
    prb = p.sb("prb", [P, T], BF16)
    s2r = p.sb("s2r", [2, T], F32)
    rsqb = p.sb("rsqb", [P, T], F32)
    b16 = p.sb("b16", [16, T], F32)
    ps1 = p.psum("ps1", [P, 512], F32)
    ps2 = p.psum("ps2", [2, 512], F32)
    psb = p.psum("psb", [P, 512], F32)
    E2 = p.sb("E2", [P, 2], F32)
    e2t = p.sb("e2t", [2, P], F32)
    p.op("vector", lambda e: e.memset(E2[:, :], 0.0), inc=None)
    p.op("vector", lambda e: e.memset(e2t[:, :], 0.0), inc="v")
    p.wait("vector", "v")
    p.op("vector", lambda e: e.memset(E2[0:64, 0:1], 1.0), inc=None)
    p.op("vector", lambda e: e.memset(E2[64:128, 1:2], 1.0), inc=None)
    p.op("vector", lambda e: e.memset(e2t[0:1, 0:64], 1.0), inc="v")
    p.wait("sync", "v")
    p.dma("sync", e2t[1:2, 64:128], cm.ones_row[0:1, 0:64], inc="dma")
    p.wait("vector", "dma")
    p.wait("tensor", "dma")
    p.wait("vector", "v")

    for k in range(KD):
        p.wait("vector", "v")
        p.op("vector", lambda e, k=k: e.scalar_tensor_tensor(
            out=prj[:], in0=xt[:, k * T:(k + 1) * T], scalar=1.0,
            in1=rho_bc[:], op0=ALU.mult, op1=ALU.mult), inc=None)
        p.op("vector", lambda e, k=k: e.tensor_add(
            out=prj[:], in0=prj[:], in1=mrho_bc[:]), inc=None)
        p.op("vector", lambda e, k=k: e.scalar_tensor_tensor(
            out=prj[:], in0=prj[:], scalar=ln1c[:, k:k + 1],
            in1=rstd2_bc[:], op0=ALU.mult, op1=ALU.mult), inc=None)
        p.op("vector", lambda e, k=k: e.tensor_copy(
            out=x2b[:, k * T:(k + 1) * T], in_=prj[:]), inc="v")
    p.free("rstd2_bc") if False else None

    def load_w_chunk(Wd_, m):
        for k in range(KD):
            p.wait("sync", "v")
            p.wait("sync", "t")
            p.dma("sync", wstg[:, :], Wd_[k * P:(k + 1) * P, m * P:(m + 1) * P], inc="dma")
            p.wait("vector", "dma")
            p.op("vector", lambda e, k=k: e.tensor_copy(
                out=wb_[:, k * P:(k + 1) * P], in_=wstg[:, :]), inc="v")

    for which, Wd_, spill, do_l2 in (("q", wqd, qspill, True),
                                     ("k", wkd, kspill, True),
                                     ("v", wvd, vspill, False)):
        for m in range(KD):
            load_w_chunk(Wd_, m)
            for off, ln_ in tok_blocks(T, 512):
                p.wait("tensor", "v")
                for k in range(KD):
                    p.op("tensor", lambda e, m=m, k=k, off=off, ln_=ln_: e.matmul(
                        out=ps1[:, 0:ln_],
                        lhsT=wb_[:, k * P:(k + 1) * P],
                        rhs=x2b[:, k * T + off:k * T + off + ln_],
                        start=(k == 0), stop=(k == KD - 1)),
                        inc=("t" if k == KD - 1 else None))
                p.wait("scalar", "t")
                p.op("scalar", lambda e, off=off, ln_=ln_: e.activation(
                    out=prj[:, off:off + ln_], in_=ps1[:, 0:ln_], func=AF.Silu), inc="v")
            if do_l2:
                p.wait("scalar", "v")
                for off, ln_ in tok_blocks(T, 512):
                    p.op("scalar", lambda e, off=off, ln_=ln_: e.activation(
                        out=rsqb[:, off:off + ln_], in_=prj[:, off:off + ln_],
                        func=AF.Square), inc="s")
                p.wait("tensor", "s")
                for off, ln_ in tok_blocks(T, 512):
                    p.op("tensor", lambda e, off=off, ln_=ln_: e.matmul(
                        out=ps2[0:2, 0:ln_], lhsT=E2[:, 0:2],
                        rhs=rsqb[:, off:off + ln_], start=True, stop=True), inc="t")
                    p.wait("vector", "t")
                    p.op("vector", lambda e, off=off, ln_=ln_: e.tensor_copy(
                        out=s2r[0:2, off:off + ln_], in_=ps2[0:2, 0:ln_]), inc="v")
                    p.wait("tensor", "v")
                ec = p.constcol(1e-6, 2)
                p.wait("scalar", "v")
                p.op("scalar", lambda e, ec=ec: e.activation(
                    out=s2r[0:2, :], in_=s2r[0:2, :],
                    func=AF.Ln, bias=ec[0:2, 0:1]), inc="s")
                p.wait("scalar", "s")
                p.op("scalar", lambda e: e.activation(
                    out=s2r[0:2, :], in_=s2r[0:2, :],
                    func=AF.Exp, scale=-0.5), inc="s")
                p.wait("tensor", "s")
                for off, ln_ in tok_blocks(T, 512):
                    p.op("tensor", lambda e, off=off, ln_=ln_: e.matmul(
                        out=psb[:, 0:ln_], lhsT=e2t[0:2, :],
                        rhs=s2r[0:2, off:off + ln_], start=True, stop=True), inc="t")
                    p.wait("vector", "t")
                    p.op("vector", lambda e, off=off, ln_=ln_: e.tensor_mul(
                        out=prb[:, off:off + ln_], in0=prj[:, off:off + ln_],
                        in1=psb[:, 0:ln_]), inc="v")
                    p.wait("tensor", "v")
            else:
                p.wait("vector", "v")
                p.op("vector", lambda e: e.tensor_copy(out=prb[:], in_=prj[:]), inc="v")
            p.wait("sync", "v")
            p.dma("sync", spill[m * P:(m + 1) * P, :], prb[:], inc="sp")
            p.wait("vector", "sp")
            p.wait("scalar", "sp")

    wbb = p.sb("wbb", [P, KD * 16], BF16)
    for k in range(KD):
        p.wait("sync", "v")
        p.dma("sync", wstg[:, 0:16], wbd[k * P:(k + 1) * P, :], inc="dma")
        p.wait("vector", "dma")
        p.op("vector", lambda e, k=k: e.tensor_copy(
            out=wbb[:, k * 16:(k + 1) * 16], in_=wstg[:, 0:16]), inc="v")
    psb16 = p.psum("psb16", [16, 512], F32)
    for off, ln_ in tok_blocks(T, 512):
        p.wait("tensor", "v")
        p.wait("tensor", "s")
        for k in range(KD):
            p.op("tensor", lambda e, k=k, off=off, ln_=ln_: e.matmul(
                out=psb16[0:16, 0:ln_], lhsT=wbb[:, k * 16:(k + 1) * 16],
                rhs=x2b[:, k * T + off:k * T + off + ln_],
                start=(k == 0), stop=(k == KD - 1)),
                inc=("t" if k == KD - 1 else None))
        p.wait("scalar", "t")
        p.op("scalar", lambda e, off=off, ln_=ln_: e.activation(
            out=b16[0:16, off:off + ln_], in_=psb16[0:16, 0:ln_],
            func=AF.Sigmoid), inc="s")
    pstb = p.psum("pstb", [P, 16], F32)
    p.wait("tensor", "s")
    for c in range(NC):
        p.wait("tensor", "v")
        p.op("tensor", lambda e, c=c: e.transpose(
            out=pstb[0:P, 0:16], in_=b16[0:16, c * P:(c + 1) * P],
            identity=cm.ident_f[0:16, 0:16]), inc="t")
        p.wait("vector", "t")
        p.op("vector", lambda e, c=c: e.tensor_copy(
            out=btm[:, c * 16:(c + 1) * 16], in_=pstb[0:P, 0:16]), inc=None)
        p.op("vector", lambda e, c=c: e.tensor_scalar(
            out=btmN[:, c * 16:(c + 1) * 16], in0=pstb[0:P, 0:16],
            scalar1=-1.0, scalar2=None, op0=ALU.mult), inc="v")
        p.wait("tensor", "v")
    p.free("pstb", "psb16", "psb", "ps2")
    p.free("wbb")
    p.free("e2t", "E2")
    p.free("b16", "rsqb", "s2r", "prb", "prj", "wstg", "wb_", "x2b")
    p.fence_dma()

    Mf = p.sb("Mf", [64, 16 * 64], F32)
    Mb = p.sb("Mb", [64, 16 * 64], BF16)
    Pac = p.sb("Pac", [64, 16 * 64], F32)
    Pacb = p.sb("Pacb", [64, 16 * 64], BF16)
    Fac = p.sb("Fac", [64, 16 * 64], F32)
    identf64 = cm.ident_f
    p.op("vector", lambda e: e.memset(Mf[:], 0.0), inc=None)
    p.op("vector", lambda e: e.memset(Mb[:], 0.0), inc=None)
    for h in range(16):
        p.op("vector", lambda e, h=h: e.tensor_copy(
            out=Pac[0:64, h * 64:(h + 1) * 64], in_=identf64[0:64, 0:64]), inc=None)
        p.op("vector", lambda e, h=h: e.tensor_copy(
            out=Pacb[0:64, h * 64:(h + 1) * 64], in_=identf64[0:64, 0:64]), inc=None)
        p.op("vector", lambda e, h=h: e.tensor_copy(
            out=Fac[0:64, h * 64:(h + 1) * 64], in_=identf64[0:64, 0:64]), inc="v")
    p.wait("vector", "v")

    kslc = p.sb("kslc", [64, P], BF16)
    vslc = p.sb("vslc", [64, P], BF16)
    qslc = p.sb("qslc", [64, P], BF16)
    Gsb = p.sb("Gsb", [P, P], BF16)
    bfree = p.sb("bfree", [P, P], BF16)
    Zt = p.sb("Zt_t", [P, P], BF16)
    Zu = p.sb("Zu_t", [P, P], BF16)
    At = p.sb("At_t", [P, P], BF16)
    Xb = p.sb("Xb_t", [P, P], BF16)
    Ktm = p.sb("Ktm", [P, 64], BF16)
    WT = p.sb("WT_t", [64, P], BF16)
    Up = p.sb("Up_t", [P, 64], BF16)
    QeT = p.sb("QeT", [64, P], BF16)
    tmp64 = p.sb("tmp64", [P, 64], F32)
    brow = p.sb("brow", [1, P], F32)
    ft64 = p.sb("ft64", [64, 64], F32)
    pg = p.psum("pg", [P, P], F32)
    pz1 = p.psum("pz1", [P, P], F32)
    pz2 = p.psum("pz2", [P, P], F32)
    pm1 = p.psum("pm1", [P, P], F32)
    ps64 = p.psum("ps64", [64, P], F32)
    pzt = p.psum("pzt", [P, 64], BF16)
    ps64b = p.psum("ps64b", [64, P], BF16)

    for c in range(NC):
        for h in range(16):
            uc = c * 16 + h
            p.wait("sync", "v")
            p.wait("sync", "t")
            p.dma("sync", kslc[:], kspill[h * 64:(h + 1) * 64, c * P:(c + 1) * P], inc="sp")
            p.dma("sync", vslc[:], vspill[h * 64:(h + 1) * 64, c * P:(c + 1) * P], inc="sp")
            p.dma("sync", qslc[:], qspill[h * 64:(h + 1) * 64, c * P:(c + 1) * P], inc="sp")
            p.wait("tensor", "sp")
            p.op("tensor", lambda e: e.matmul(out=pg[:], lhsT=kslc[:], rhs=kslc[:],
                                              start=True, stop=True), inc="t")
            p.wait("vector", "t")
            p.op("vector", lambda e: e.tensor_copy(out=Gsb[:], in_=pg[:]), inc="v")
            p.wait("tensor", "v")
            p.op("tensor", lambda e, uc=uc: e.transpose(
                out=pm1[0:1, 0:P], in_=btm[:, uc:uc + 1], identity=cm.ident_f[:]), inc="t")
            p.wait("vector", "t")
            p.op("vector", lambda e: e.tensor_copy(out=brow[:], in_=pm1[0:1, :]), inc="v")
            p.wait("tensor", "v")
            p.op("tensor", lambda e: e.matmul(out=pm1[:], lhsT=cm.ones_row[0:1, :],
                                              rhs=brow[0:1, :], start=True, stop=True), inc="t")
            p.wait("vector", "t")
            p.op("vector", lambda e: e.tensor_copy(out=bfree[:], in_=pm1[:]), inc="v")
            p.wait("vector", "v")
            p.op("vector", lambda e: e.tensor_mul(out=Zt[:], in0=Gsb[:], in1=bfree[:]), inc="v")
            p.wait("vector", "v")
            p.op("vector", lambda e: e.tensor_mul(out=Zt[:], in0=Zt[:], in1=triuSN[:]), inc="v")
            p.op("vector", lambda e, uc=uc: e.scalar_tensor_tensor(
                out=Zu[:], in0=Gsb[:], scalar=btmN[:, uc:uc + 1], in1=trilS[:],
                op0=ALU.mult, op1=ALU.mult), inc="v")
            p.wait("vector", "v")
            p.op("tensor", lambda e: e.matmul(out=pg[:], lhsT=kslc[:], rhs=qslc[:],
                                              start=True, stop=True), inc="t")
            p.wait("vector", "t")
            p.op("vector", lambda e: e.tensor_mul(out=At[:], in0=pg[:], in1=triuI[:]), inc="v")
            p.wait("tensor", "v")
            p.op("tensor", lambda e: e.transpose(out=pzt[0:P, 0:64], in_=vslc[:],
                                                 identity=cm.ident_bf[0:64, 0:64]), inc="t")
            p.wait("vector", "t")
            p.op("vector", lambda e, uc=uc: e.tensor_scalar(
                out=Xb[:, 0:64], in0=pzt[0:P, 0:64], scalar1=btm[:, uc:uc + 1],
                scalar2=None, op0=ALU.mult), inc="v")
            p.wait("tensor", "v")
            p.op("tensor", lambda e: e.transpose(out=pzt[0:P, 0:64], in_=kslc[:],
                                                 identity=cm.ident_bf[0:64, 0:64]), inc="t")
            p.wait("vector", "t")
            p.op("vector", lambda e: e.tensor_copy(out=Ktm[:], in_=pzt[0:P, 0:64]), inc=None)
            p.op("vector", lambda e, uc=uc: e.tensor_scalar(
                out=Xb[:, 64:128], in0=pzt[0:P, 0:64], scalar1=btm[:, uc:uc + 1],
                scalar2=None, op0=ALU.mult), inc="v")
            p.wait("tensor", "v")
            for it in range(7):
                p.op("tensor", lambda e: e.matmul(out=pz1[:], lhsT=Zt[:], rhs=Xb[:],
                                                  start=True, stop=True), inc="t")
                p.wait("vector", "t")
                p.op("vector", lambda e: e.tensor_add(out=Xb[:], in0=Xb[:], in1=pz1[:]), inc="v")
                if it < 6:
                    p.wait("tensor", "v")
                    p.op("tensor", lambda e: e.matmul(out=pz1[:], lhsT=Zt[:], rhs=Zu[:],
                                                      start=True, stop=True), inc=None)
                    p.op("tensor", lambda e: e.matmul(out=pz2[:], lhsT=Zu[:], rhs=Zt[:],
                                                      start=True, stop=True), inc="t")
                    p.wait("vector", "t")
                    p.op("vector", lambda e: e.tensor_copy(out=Zu[:], in_=pz1[:]), inc=None)
                    p.op("vector", lambda e: e.tensor_copy(out=Zt[:], in_=pz2[:]), inc="v")
                p.wait("tensor", "v")
            p.op("tensor", lambda e: e.transpose(out=ps64b[0:64, 0:P], in_=Xb[:, 64:128],
                                                 identity=cm.ident_bf[:]), inc="t")
            p.wait("vector", "t")
            p.op("vector", lambda e: e.tensor_copy(out=WT[:], in_=ps64b[0:64, :]), inc="v")
            p.wait("tensor", "v")
            p.op("tensor", lambda e, h=h: e.matmul(
                out=pz1[0:P, 0:64], lhsT=WT[:], rhs=Mb[0:64, h * 64:(h + 1) * 64],
                start=True, stop=True), inc="t")
            p.wait("vector", "t")
            p.op("vector", lambda e: e.scalar_tensor_tensor(
                out=Up[:], in0=pz1[0:P, 0:64], scalar=-1.0, in1=Xb[:, 0:64],
                op0=ALU.mult, op1=ALU.add), inc="v")
            p.wait("tensor", "v")
            p.op("tensor", lambda e: e.matmul(out=ps64[0:64, 0:P], lhsT=Xb[:, 64:128],
                                              rhs=At[:], start=True, stop=True), inc="t")
            p.wait("vector", "t")
            p.op("vector", lambda e: e.scalar_tensor_tensor(
                out=QeT[:], in0=ps64[0:64, :], scalar=-1.0, in1=qslc[:],
                op0=ALU.mult, op1=ALU.add), inc="v")
            p.wait("tensor", "v")
            p.op("tensor", lambda e, h=h: e.matmul(
                out=ps64[0:64, 0:P], lhsT=Pacb[0:64, h * 64:(h + 1) * 64],
                rhs=QeT[:], start=True, stop=True), inc="t")
            p.wait("vector", "t")
            p.op("vector", lambda e: e.tensor_copy(out=QeT[:], in_=ps64[0:64, :]), inc="v")
            p.wait("sync", "v")
            p.dma("sync", qptd[h * 64:(h + 1) * 64, c * P:(c + 1) * P], QeT[:], inc="sp")
            p.wait("tensor", "v")
            p.op("tensor", lambda e: e.matmul(out=pz1[0:P, 0:64], lhsT=At[:], rhs=Up[:],
                                              start=True, stop=False), inc=None)
            p.op("tensor", lambda e, h=h: e.matmul(
                out=pz1[0:P, 0:64], lhsT=qslc[:], rhs=Mb[0:64, h * 64:(h + 1) * 64],
                start=False, stop=True), inc="t")
            p.wait("vector", "t")
            p.op("vector", lambda e, uc=uc: e.tensor_copy(
                out=Otm[:, uc * 64:(uc + 1) * 64], in_=pz1[0:P, 0:64]), inc="v")
            p.wait("tensor", "v")
            p.op("tensor", lambda e: e.matmul(out=pm1[0:64, 0:64], lhsT=Xb[:, 64:128],
                                              rhs=Ktm[:], start=True, stop=True), inc="t")
            p.wait("vector", "t")
            p.op("vector", lambda e: e.scalar_tensor_tensor(
                out=ft64[:], in0=pm1[0:64, 0:64], scalar=-1.0,
                in1=cm.ident_f[0:64, 0:64], op0=ALU.mult, op1=ALU.add), inc="v")
            p.wait("tensor", "v")
            p.op("tensor", lambda e: e.matmul(out=pm1[0:64, 0:64], lhsT=Ktm[:], rhs=Up[:],
                                              start=True, stop=True), inc="t")
            p.wait("vector", "t")
            p.op("vector", lambda e, h=h: e.tensor_add(
                out=Mf[0:64, h * 64:(h + 1) * 64], in0=Mf[0:64, h * 64:(h + 1) * 64],
                in1=pm1[0:64, 0:64]), inc="v")
            p.wait("vector", "v")
            p.op("vector", lambda e, h=h: e.tensor_copy(
                out=Mb[0:64, h * 64:(h + 1) * 64], in_=Mf[0:64, h * 64:(h + 1) * 64]), inc="v")
            p.wait("tensor", "v")
            p.op("tensor", lambda e, h=h: e.matmul(
                out=pm1[0:64, 0:64], lhsT=ft64[:, :], rhs=Pac[0:64, h * 64:(h + 1) * 64],
                start=True, stop=True), inc=None)
            p.op("tensor", lambda e, h=h: e.matmul(
                out=pm1[0:64, 64:128], lhsT=ft64[:, :], rhs=Fac[0:64, h * 64:(h + 1) * 64],
                start=True, stop=True), inc="t")
            p.wait("vector", "t")
            p.op("vector", lambda e, h=h: e.tensor_copy(
                out=Pac[0:64, h * 64:(h + 1) * 64], in_=pm1[0:64, 0:64]), inc=None)
            p.op("vector", lambda e, h=h: e.tensor_copy(
                out=Pacb[0:64, h * 64:(h + 1) * 64], in_=pm1[0:64, 0:64]), inc=None)
            p.op("vector", lambda e, h=h: e.tensor_copy(
                out=Fac[0:64, h * 64:(h + 1) * 64], in_=pm1[0:64, 64:128]), inc="v")
            p.wait("tensor", "v")

    FacT = p.sb("FacT", [64, 16 * 64], F32)
    for h in range(16):
        p.wait("tensor", "v")
        p.op("tensor", lambda e, h=h: e.transpose(
            out=pm1[0:64, 0:64], in_=Fac[0:64, h * 64:(h + 1) * 64],
            identity=cm.ident_f[0:64, 0:64]), inc="t")
        p.wait("vector", "t")
        p.op("vector", lambda e, h=h: e.tensor_copy(
            out=FacT[0:64, h * 64:(h + 1) * 64], in_=pm1[0:64, 0:64]), inc="v")
        p.wait("tensor", "v")
    _, dM = emit_allgather(p, Mf, 16 * 64, "dm", npart=64, readback=False)
    _, dF = emit_allgather(p, FacT, 16 * 64, "df", npart=64, readback=False)
    gMr = p.sb("gMr", [64, 16 * 64], F32)
    gFr = p.sb("gFr", [64, 16 * 64], F32)

    Min = p.sb("Min", [64, 16 * 64], F32)
    Minb = p.sb("Minb", [64, 16 * 64], BF16)
    fblend = p.sb("fblend", [64, 64], F32)
    iscaled = p.sb("iscaled", [64, 64], F32)
    p.op("vector", lambda e: e.memset(Min[:], 0.0), inc="v")
    p.wait("vector", "v")
    for r in range(7):
        p.wait("sync", "v")
        p.wait("sync", "cc")
        p.dma("sync", gMr[:], dM.ap()[r * 64:(r + 1) * 64, :], inc="dma")
        p.dma("sync", gFr[:], dF.ap()[r * 64:(r + 1) * 64, :], inc="dma")
        p.wait("vector", "dma")
        p.op("vector", lambda e, r=r: e.tensor_scalar(
            out=fblend[:], in0=pmc[0:64, r:r + 1], scalar1=-1.0, scalar2=1.0,
            op0=ALU.mult, op1=ALU.add), inc="v") if False else None
        p.op("vector", lambda e, r=r: e.scalar_tensor_tensor(
            out=iscaled[:], in0=cm.ident_f[0:64, 0:64], scalar=pmc[0:64, r:r + 1],
            in1=cm.ident_f[0:64, 0:64], op0=ALU.mult, op1=ALU.subtract), inc="v")
        p.wait("vector", "v")
        p.op("vector", lambda e: e.tensor_scalar(
            out=iscaled[:], in0=iscaled[:], scalar1=-1.0, scalar2=None,
            op0=ALU.mult), inc="v")
        p.wait("vector", "v")
        for h in range(16):
            p.op("vector", lambda e, r=r, h=h: e.scalar_tensor_tensor(
                out=fblend[:], in0=gFr[0:64, h * 64:(h + 1) * 64],
                scalar=pmc[0:64, r:r + 1], in1=iscaled[:],
                op0=ALU.mult, op1=ALU.add), inc="v")
            p.wait("tensor", "v")
            p.op("tensor", lambda e, h=h: e.matmul(
                out=pm1[0:64, 0:64], lhsT=fblend[:], rhs=Min[0:64, h * 64:(h + 1) * 64],
                start=True, stop=True), inc="t")
            p.wait("vector", "t")
            p.op("vector", lambda e, r=r, h=h: e.scalar_tensor_tensor(
                out=Min[0:64, h * 64:(h + 1) * 64],
                in0=gMr[0:64, h * 64:(h + 1) * 64],
                scalar=pmc[0:64, r:r + 1], in1=pm1[0:64, 0:64],
                op0=ALU.mult, op1=ALU.add), inc="v")
            p.wait("vector", "v")
    for h in range(16):
        p.op("vector", lambda e, h=h: e.tensor_copy(
            out=Minb[0:64, h * 64:(h + 1) * 64], in_=Min[0:64, h * 64:(h + 1) * 64]),
            inc=("v" if h == 15 else None))
    p.wait("vector", "v")

    for c in range(NC):
        for h in range(16):
            uc = c * 16 + h
            p.wait("sync", "v")
            p.wait("sync", "t")
            p.dma("sync", QeT[:], qptd[h * 64:(h + 1) * 64, c * P:(c + 1) * P], inc="sp")
            p.wait("tensor", "sp")
            p.op("tensor", lambda e, h=h: e.matmul(
                out=pz1[0:P, 0:64], lhsT=QeT[:], rhs=Minb[0:64, h * 64:(h + 1) * 64],
                start=True, stop=True), inc="t")
            p.wait("vector", "t")
            p.op("vector", lambda e, uc=uc: e.tensor_add(
                out=Otm[:, uc * 64:(uc + 1) * 64], in0=Otm[:, uc * 64:(uc + 1) * 64],
                in1=pz1[0:P, 0:64]), inc="v")
            p.wait("tensor", "v")

    p.free("iscaled", "fblend", "Minb", "Min", "gFr", "gMr", "FacT")
    p.free("ps64b", "pzt", "ps64", "pm1", "pz2", "pz1", "pg")
    p.free("ft64", "brow", "tmp64", "QeT", "Up_t", "WT_t", "Ktm", "Xb_t",
           "At_t", "Zu_t", "Zt_t", "bfree", "Gsb", "qslc", "vslc", "kslc")
    p.free("Fac", "Pacb", "Pac", "Mb", "Mf")
    p.fence_dma()

    otb64 = p.sb("otb64", [64, 512], BF16)
    wohb = p.sb("wohb", [64, 16 * P], BF16)
    wstg2 = p.sb("wstg2", [64, P], F32)
    hbb = p.sb("hbb", [P, KD * T], BF16)
    hch = p.sb("hch", [P, 512], F32)
    msr = p.sb("msr", [1, T], F32)
    sqb2 = p.sb("sqb2", [P, 512], F32)
    pst2 = p.psum("pst2", [64, P], BF16)
    msps2 = p.psum("msps2", [1, 512], F32)
    for off, ln_ in tok_blocks(T, 512):
        for m in range(KD):
            for h in range(16):
                p.wait("sync", "v")
                p.wait("sync", "t")
                p.dma("sync", wstg2[:, :], wod[h * 64:(h + 1) * 64, m * P:(m + 1) * P], inc="dma")
                p.wait("vector", "dma")
                p.op("vector", lambda e, h=h: e.tensor_copy(
                    out=wohb[:, h * P:(h + 1) * P], in_=wstg2[:, :]), inc="v")
            p.wait("tensor", "v")
            for h in range(16):
                for cc in range(off // P, (off + ln_) // P):
                    p.wait("tensor", "v")
                    p.op("tensor", lambda e, h=h, cc=cc: e.transpose(
                        out=pst2[0:64, 0:P],
                        in_=Otm[:, (cc * 16 + h) * 64:(cc * 16 + h + 1) * 64],
                        identity=cm.ident_bf[:]), inc="t")
                    p.wait("vector", "t")
                    p.op("vector", lambda e, cc=cc, off=off: e.tensor_copy(
                        out=otb64[:, (cc * P - off):(cc * P - off) + P],
                        in_=pst2[0:64, 0:P]), inc="v")
                p.wait("tensor", "v")
                p.op("tensor", lambda e, h=h, ln_=ln_: e.matmul(
                    out=ps1[:, 0:ln_], lhsT=wohb[:, h * P:(h + 1) * P],
                    rhs=otb64[:, 0:ln_], start=(h == 0), stop=(h == 15)),
                    inc=("t" if h == 15 else None))
            p.wait("vector", "t")
            p.op("vector", lambda e, m=m, off=off, ln_=ln_: e.scalar_tensor_tensor(
                out=hch[:, 0:ln_], in0=xt[:, m * T + off:m * T + off + ln_],
                scalar=1.0, in1=rho_bc[:, off:off + ln_], op0=ALU.mult, op1=ALU.mult), inc=None)
            p.op("vector", lambda e, off=off, ln_=ln_: e.tensor_add(
                out=hch[:, 0:ln_], in0=hch[:, 0:ln_],
                in1=mrho_bc[:, off:off + ln_]), inc=None)
            p.op("vector", lambda e, ln_=ln_: e.tensor_add(
                out=hch[:, 0:ln_], in0=hch[:, 0:ln_], in1=ps1[:, 0:ln_]), inc="v")
            p.wait("vector", "v")
            p.op("vector", lambda e, m=m, off=off, ln_=ln_: e.tensor_scalar(
                out=hbb[:, m * T + off:m * T + off + ln_], in0=hch[:, 0:ln_],
                scalar1=ln2c[:, m:m + 1], scalar2=None, op0=ALU.mult), inc=None)
            p.op("vector", lambda e, m=m, off=off, ln_=ln_: e.tensor_add(
                out=xt[:, m * T + off:m * T + off + ln_],
                in0=xt[:, m * T + off:m * T + off + ln_],
                in1=hch[:, 0:ln_]), inc="v")
            p.wait("scalar", "v")
            p.op("scalar", lambda e, ln_=ln_: e.activation(
                out=sqb2[:, 0:ln_], in_=hch[:, 0:ln_], func=AF.Square), inc="s")
            p.wait("tensor", "s")
            p.op("tensor", lambda e, m=m, ln_=ln_: e.matmul(
                out=msps2[0:1, 0:ln_], lhsT=cm.ones_col[:, 0:1], rhs=sqb2[:, 0:ln_],
                start=(m == 0), stop=(m == KD - 1)),
                inc=("t" if m == KD - 1 else None))
        p.wait("vector", "t")
        p.op("vector", lambda e, off=off, ln_=ln_: e.tensor_copy(
            out=msr[:, off:off + ln_], in_=msps2[0:1, 0:ln_]), inc="v")
        p.wait("vector", "v")
    rstdb = p.sb("rstdb", [P, T], F32)
    pbb2 = p.psum("pbb2", [P, 512], F32)
    p.op("vector", lambda e: e.tensor_scalar(
        out=msr[:], in0=msr[:], scalar1=1.0 / D, scalar2=1e-6,
        op0=ALU.mult, op1=ALU.add), inc="v")
    p.wait("scalar", "v")
    p.op("scalar", lambda e: e.activation(out=msr[:], in_=msr[:], func=AF.Ln), inc="s")
    p.wait("scalar", "s")
    p.op("scalar", lambda e: e.activation(out=msr[:], in_=msr[:], func=AF.Exp,
                                          scale=-0.5), inc="s")
    p.wait("tensor", "s")
    p.wait("vector", "s")
    emit_bcast_row(p, cm, msr, rstdb, T, pbb2)
    for m in range(KD):
        for off, ln_ in tok_blocks(T, 512):
            p.wait("vector", "v")
            p.op("vector", lambda e, m=m, off=off, ln_=ln_: e.tensor_copy(
                out=hch[:, 0:ln_], in_=hbb[:, m * T + off:m * T + off + ln_]), inc="v")
            p.wait("vector", "v")
            p.op("vector", lambda e, m=m, off=off, ln_=ln_: e.tensor_mul(
                out=hbb[:, m * T + off:m * T + off + ln_], in0=hch[:, 0:ln_],
                in1=rstdb[:, off:off + ln_]), inc="v")

    p.free("pbb2")
    p.free("rstdb", "sqb2", "msr", "hch")
    HQ = 1024
    NHQ = HQ // P
    wbuf = p.sb("wbufg", [P, KD * HQ], BF16)
    stg = p.sb("stgg", [P, HQ], F32)
    gsil_q = p.sb("gsil_qg", [P, NHQ * 512], BF16)
    mprod_q = p.sb("mprod_qg", [P, NHQ * 512], BF16)
    psg = p.psum("psg", [P, 512], F32)
    for off, ln_ in tok_blocks(T, 512):
        for quar in range(4):
            p.wait("tensor", "v")
            emit_load_wbf(p, wgd[:, quar * HQ:(quar + 1) * HQ], D, HQ, wbuf, stg)
            for m in range(NHQ):
                p.wait("tensor", "v")
                for k in range(KD):
                    p.op("tensor", lambda e, m=m, k=k, off=off, ln_=ln_: e.matmul(
                        out=psg[:, 0:ln_],
                        lhsT=wbuf[:, k * HQ + m * P:k * HQ + (m + 1) * P],
                        rhs=hbb[:, k * T + off:k * T + off + ln_],
                        start=(k == 0), stop=(k == KD - 1)),
                        inc=("t" if k == KD - 1 else None))
                p.wait("scalar", "t")
                p.op("scalar", lambda e, m=m, ln_=ln_: e.activation(
                    out=gsil_q[:, m * 512:m * 512 + ln_], in_=psg[:, 0:ln_],
                    func=AF.Silu), inc="v")
            p.wait("tensor", "v")
            emit_load_wbf(p, wud[:, quar * HQ:(quar + 1) * HQ], D, HQ, wbuf, stg)
            for m in range(NHQ):
                p.wait("tensor", "v")
                for k in range(KD):
                    p.op("tensor", lambda e, m=m, k=k, off=off, ln_=ln_: e.matmul(
                        out=psg[:, 0:ln_],
                        lhsT=wbuf[:, k * HQ + m * P:k * HQ + (m + 1) * P],
                        rhs=hbb[:, k * T + off:k * T + off + ln_],
                        start=(k == 0), stop=(k == KD - 1)),
                        inc=("t" if k == KD - 1 else None))
                p.wait("vector", "t")
                p.op("vector", lambda e, m=m, ln_=ln_: e.tensor_mul(
                    out=mprod_q[:, m * 512:m * 512 + ln_], in0=psg[:, 0:ln_],
                    in1=gsil_q[:, m * 512:m * 512 + ln_]), inc="v")
            p.wait("tensor", "v")
            emit_load_wbf(p, wdd[quar * HQ:(quar + 1) * HQ, :], HQ, D, wbuf, stg)
            for mo in range(KD):
                p.wait("tensor", "v")
                for k in range(NHQ):
                    p.op("tensor", lambda e, mo=mo, k=k, off=off, ln_=ln_: e.matmul(
                        out=psg[:, 0:ln_],
                        lhsT=wbuf[:, k * D + mo * P:k * D + (mo + 1) * P],
                        rhs=mprod_q[:, k * 512:k * 512 + ln_],
                        start=(k == 0), stop=(k == NHQ - 1)),
                        inc=("t" if k == NHQ - 1 else None))
                p.wait("vector", "t")
                p.op("vector", lambda e, mo=mo, off=off, ln_=ln_: e.tensor_add(
                    out=xt[:, mo * T + off:mo * T + off + ln_],
                    in0=xt[:, mo * T + off:mo * T + off + ln_],
                    in1=psg[:, 0:ln_]), inc="v")

    p.wait("sync", "v")
    for k in range(KD):
        p.dma("sync", yd[k * P:(k + 1) * P, :], xt[:, k * T:(k + 1) * T], inc="st")
    p.wait("sync", "st")
    p.finish()
    return p


def run_l0_attn_b(x, inp):
    pr = _get_prog("l0ab", build_l0_attn_b)
    xs = _fm_slices(x)
    tril = np.tril(np.ones((P, P), np.float32), -1).astype(_bf16np())
    triuSN = (-np.triu(np.ones((P, P), np.float32), 1)).astype(_bf16np())
    triuI = np.triu(np.ones((P, P), np.float32), 0).astype(_bf16np())
    selk = np.zeros((16, D), np.float32)
    for g in range(D):
        selk[g // 64, g] = 1.0
    base = {
        "Wq": np.asarray(inp["L0_attn_b_Wq"], np.float32),
        "Wk": np.asarray(inp["L0_attn_b_Wk"], np.float32),
        "Wv": np.asarray(inp["L0_attn_b_Wv"], np.float32),
        "Wb": np.asarray(inp["L0_attn_b_Wb"], np.float32),
        "Wo": np.asarray(inp["L0_attn_b_Wo"], np.float32),
        "ln1": np.asarray(inp["L0_attn_b_ln1"], np.float32).reshape(D, 1),
        "ln2": np.asarray(inp["L0_attn_b_ln2"], np.float32).reshape(D, 1),
        "Wg": np.asarray(inp["L0_attn_b_Wg"], np.float32),
        "Wu": np.asarray(inp["L0_attn_b_Wu"], np.float32),
        "Wd": np.asarray(inp["L0_attn_b_Wd"], np.float32),
        "trilS": tril, "triuSN": triuSN, "triuI": triuI, "selk": selk,
        **Common.host_inputs(),
    }
    in_maps = []
    for c in range(NCORES):
        pm = np.zeros((P, 8), np.float32)
        for r in range(8):
            if r // 4 == c // 4 and r < c:
                pm[:, r] = 1.0
        in_maps.append({**base, "x": xs[c], "prevmask": pm})
    res = run_bass_kernel_spmd(pr.nc, in_maps, core_ids=list(range(NCORES)))
    return _unshard([res.results[c]["y"] for c in range(NCORES)])


def _bw_list(N=SEQ):
    out = []
    w = 1
    while True:
        den = 2 ** (w - 1)
        num = N - 1
        Bw = (num // den - 1 if num % den == 0 else num // den) - 1
        if Bw < 0:
            break
        out.append(Bw)
        w += 1
    return out


def build_l1_attn_b():
    p = Prog("l1ab")
    xd = p.dram("x", [D, T], F32, kind="ExternalInput").ap()
    wtd = {}
    for nm, dn in (("q", "WqT"), ("k", "WkT"), ("v", "WvT"), ("o", "WoT")):
        wtd[nm] = p.dram(dn, [D, D], F32, kind="ExternalInput").ap()
    coldef = [("bq", "bq"), ("bk", "bk"), ("bv", "bv"), ("bo", "bo"),
              ("gq", "gq"), ("b2q", "b2q"), ("gk", "gk"), ("b2k", "b2k"),
              ("gv", "gv"), ("b2v", "b2v"), ("go", "go"), ("b2o", "b2o")]
    cold = {nm: p.dram(dn, [D, 1], F32, kind="ExternalInput").ap()
            for nm, dn in coldef}
    scd = p.dram("scalec", [P, 1], F32, kind="ExternalInput").ap()
    bmd = p.dram("bmask", [P, 8], F32, kind="ExternalInput").ap()
    vixd = p.dram("vidx", [P, 8], U32, kind="ExternalInput").ap()
    bixd = p.dram("bidx", [P, 8], U32, kind="ExternalInput").ap()
    yd = p.dram("y", [D, T], F32, kind="ExternalOutput").ap()
    NC = T // P
    NT = SEQ

    cm = Common(p)
    p.constcol(1e-5)
    xnb = p.sb("xnb", [P, KD * T], BF16)
    xt = p.sb("xt", [P, KD * T], F32)
    for k in range(KD):
        p.dma("sync", xt[:, k * T:(k + 1) * T], xd[k * P:(k + 1) * P, :])
    p.wait("vector", "dma")
    p.wait("scalar", "dma")
    p.wait("tensor", "dma")

    rho_bc, mrho_bc, lnn = emit_ln(p, cm, xt, T, "ab2")
    xnf = p.sb("xnf_ab2", [P, T], F32)
    for k in range(KD):
        p.wait("vector", "v")
        p.op("vector", lambda e, k=k: e.scalar_tensor_tensor(
            out=xnf[:], in0=xt[:, k * T:(k + 1) * T], scalar=1.0,
            in1=rho_bc[:], op0=ALU.mult, op1=ALU.mult), inc=None)
        p.op("vector", lambda e, k=k: e.tensor_add(
            out=xnf[:], in0=xnf[:], in1=mrho_bc[:]), inc=None)
        p.op("vector", lambda e, k=k: e.tensor_copy(
            out=xnb[:, k * T:(k + 1) * T], in_=xnf[:]), inc="v")
    p.free("xnf_ab2")
    p.free(*lnn)
    p.free("ln_mrho_ab2", "ln_rho_ab2")
    xspill = p.dram("xspill", [D, T], F32)
    p.wait("sync", "v")
    for k in range(KD):
        p.dma("sync", xspill.ap()[k * P:(k + 1) * P, :],
              xt[:, k * T:(k + 1) * T], inc="sp")
    p.wait("sync", "sp")
    p.free("xt")
    x2b = xnb

    tabs = {}
    for nm, _dn in coldef:
        t = p.sb("tb_" + nm, [P, KD], F32)
        for k in range(KD):
            p.dma("sync", t[:, k:k + 1], cold[nm][k * P:(k + 1) * P, :])
        tabs[nm] = t
    scol = p.sb("tb_scale", [P, 1], F32)
    p.dma("sync", scol[:], scd)
    bmc = p.sb("tb_bmask", [P, 8], F32)
    p.dma("sync", bmc[:], bmd)
    vix = p.sb("tb_vidx", [P, 8], U32)
    p.dma("sync", vix[:], vixd)
    bix = p.sb("tb_bidx", [P, 8], U32)
    p.dma("sync", bix[:], bixd)
    p.wait("vector", "dma")
    p.wait("scalar", "dma")
    p.wait("tensor", "dma")

    Qf = p.sb("Qf", [P, KD * T], BF16)
    Ofm = p.sb("Ofm", [P, KD * T], BF16)
    KtVg = p.sb("KtVg", [P, KD * 64], F32)
    colVg = p.sb("colVg", [P, KD], F32)
    c3t = p.sb("c3t", [P, KD], F32)
    combT = p.sb("combT", [P, KD * 64], F32)
    combH = p.sb("combH", [P, KD * 64], BF16)
    combL = p.sb("combL", [P, KD * 64], BF16)
    Kb = p.sb("Kb", [P, KD * T], BF16)
    Vb = p.sb("Vb", [P, KD * T], BF16)

    ps1 = p.psum("ps1", [P, 512], F32)
    pb = p.psum("pb", [P, 512], F32)

    def mk_pw(sfx):
        pw = {"sfx": sfx}
        pw["wb"] = p.sb("pw_wb" + sfx, [P, KD * P], BF16)
        pw["stg"] = p.sb("pw_stg" + sfx, [P, P], F32)
        pw["prj"] = p.sb("pw_prj" + sfx, [P, 512], F32)
        pw["sq"] = p.sb("pw_sq" + sfx, [P, 512], F32)
        pw["srow"] = p.sb("pw_srow" + sfx, [1, T], F32)
        pw["qrow"] = p.sb("pw_qrow" + sfx, [1, T], F32)
        pw["m2r"] = p.sb("pw_m2r" + sfx, [1, T], F32)
        pw["rho2b"] = p.sb("pw_rho2" + sfx, [P, 512], F32)
        pw["mrho2b"] = p.sb("pw_mrho2" + sfx, [P, 512], F32)
        pw["pss"] = p.psum("pss" + sfx, [1, 512], F32)
        pw["psq"] = p.psum("psq" + sfx, [1, 512], F32)
        return pw

    def free_pw(pw):
        sfx = pw["sfx"]
        p.free("psq" + sfx, "pss" + sfx)
        p.free("pw_mrho2" + sfx, "pw_rho2" + sfx, "pw_m2r" + sfx,
               "pw_qrow" + sfx, "pw_srow" + sfx,
               "pw_sq" + sfx, "pw_prj" + sfx, "pw_stg" + sfx, "pw_wb" + sfx)

    def load_wT(pw, Wd_, m):
        for k in range(KD):
            p.wait("sync", "v")
            p.wait("sync", "t")
            p.dma("sync", pw["stg"][:, :],
                  Wd_[k * P:(k + 1) * P, m * P:(m + 1) * P], inc="dma")
            p.wait("vector", "dma")
            p.op("vector", lambda e, k=k: e.tensor_copy(
                out=pw["wb"][:, k * P:(k + 1) * P], in_=pw["stg"][:, :]),
                inc="v")

    def proj_mm(pw, src, m, off, ln_):
        p.wait("tensor", "v")
        for k in range(KD):
            p.op("tensor", lambda e, m=m, k=k, off=off, ln_=ln_: e.matmul(
                out=ps1[:, 0:ln_], lhsT=pw["wb"][:, k * P:(k + 1) * P],
                rhs=src[:, k * T + off:k * T + off + ln_],
                start=(k == 0), stop=(k == KD - 1)),
                inc=("t" if k == KD - 1 else None))

    def proj_stats(pw, Wd_, bt, src):
        for m in range(KD):
            load_wT(pw, Wd_, m)
            for off, ln_ in tok_blocks(T, 512):
                proj_mm(pw, src, m, off, ln_)
                p.wait("vector", "t")
                p.op("vector", lambda e, m=m, ln_=ln_: e.tensor_scalar(
                    out=pw["prj"][:, 0:ln_], in0=ps1[:, 0:ln_],
                    scalar1=bt[:, m:m + 1], scalar2=None, op0=ALU.add), inc="v")
                p.wait("vector", "v")
                p.wait("scalar", "v")
                p.op("scalar", lambda e, ln_=ln_: e.activation(
                    out=pw["sq"][:, 0:ln_], in_=pw["prj"][:, 0:ln_],
                    func=AF.Square), inc="s")
                p.wait("tensor", "s")
                p.op("tensor", lambda e, ln_=ln_: e.matmul(
                    out=pw["pss"][0:1, 0:ln_], lhsT=cm.ones_col[:, 0:1],
                    rhs=pw["prj"][:, 0:ln_], start=True, stop=True), inc=None)
                p.op("tensor", lambda e, ln_=ln_: e.matmul(
                    out=pw["psq"][0:1, 0:ln_], lhsT=cm.ones_col[:, 0:1],
                    rhs=pw["sq"][:, 0:ln_], start=True, stop=True), inc="t")
                p.wait("vector", "t")
                if m == 0:
                    p.op("vector", lambda e, off=off, ln_=ln_: e.tensor_copy(
                        out=pw["srow"][:, off:off + ln_],
                        in_=pw["pss"][0:1, 0:ln_]), inc=None)
                    p.op("vector", lambda e, off=off, ln_=ln_: e.tensor_copy(
                        out=pw["qrow"][:, off:off + ln_],
                        in_=pw["psq"][0:1, 0:ln_]), inc="v")
                else:
                    p.op("vector", lambda e, off=off, ln_=ln_: e.tensor_add(
                        out=pw["srow"][:, off:off + ln_],
                        in0=pw["srow"][:, off:off + ln_],
                        in1=pw["pss"][0:1, 0:ln_]), inc=None)
                    p.op("vector", lambda e, off=off, ln_=ln_: e.tensor_add(
                        out=pw["qrow"][:, off:off + ln_],
                        in0=pw["qrow"][:, off:off + ln_],
                        in1=pw["psq"][0:1, 0:ln_]), inc="v")
                p.wait("tensor", "v")

    def finalize_rows(pw):
        epsc = p.constcol(1e-5)
        p.wait("vector", "t")
        p.wait("vector", "s")
        p.op("vector", lambda e: e.tensor_scalar(
            out=pw["srow"][:], in0=pw["srow"][0:1, :], scalar1=1.0 / D,
            scalar2=None, op0=ALU.mult), inc="v")
        p.wait("scalar", "v")
        p.op("scalar", lambda e: e.activation(out=pw["m2r"][:],
                                              in_=pw["srow"][:],
                                              func=AF.Square), inc="s")
        p.wait("vector", "s")
        p.op("vector", lambda e: e.scalar_tensor_tensor(
            out=pw["qrow"][:], in0=pw["qrow"][0:1, :], scalar=1.0 / D,
            in1=pw["m2r"][:], op0=ALU.mult, op1=ALU.subtract), inc="v")
        p.wait("scalar", "v")
        p.op("scalar", lambda e, epsc=epsc: e.activation(
            out=pw["qrow"][:], in_=pw["qrow"][:], func=AF.Ln,
            bias=epsc[0:1, 0:1]), inc="s")
        p.wait("scalar", "s")
        p.op("scalar", lambda e: e.activation(out=pw["qrow"][:],
                                              in_=pw["qrow"][:],
                                              func=AF.Exp, scale=-0.5), inc="s")
        p.wait("vector", "s")
        p.op("vector", lambda e: e.tensor_mul(out=pw["m2r"][:],
                                              in0=pw["srow"][:],
                                              in1=pw["qrow"][:]), inc="v")
        p.wait("tensor", "v")

    def proj_apply(pw, Wd_, bt, src, gcol, b2col, writer):
        for m in range(KD):
            load_wT(pw, Wd_, m)
            for off, ln_ in tok_blocks(T, 512):
                proj_mm(pw, src, m, off, ln_)
                p.wait("vector", "t")
                p.op("vector", lambda e, m=m, ln_=ln_: e.tensor_scalar(
                    out=pw["prj"][:, 0:ln_], in0=ps1[:, 0:ln_],
                    scalar1=bt[:, m:m + 1], scalar2=None, op0=ALU.add), inc="v")
                p.wait("vector", "v")
                p.wait("tensor", "v")
                p.op("tensor", lambda e, off=off, ln_=ln_: e.matmul(
                    out=pb[:, 0:ln_], lhsT=cm.ones_row[0:1, :],
                    rhs=pw["qrow"][0:1, off:off + ln_],
                    start=True, stop=True), inc="t")
                p.wait("vector", "t")
                p.op("vector", lambda e, ln_=ln_: e.tensor_copy(
                    out=pw["rho2b"][:, 0:ln_], in_=pb[:, 0:ln_]), inc="v")
                p.wait("tensor", "v")
                p.op("tensor", lambda e, off=off, ln_=ln_: e.matmul(
                    out=pb[:, 0:ln_], lhsT=cm.ones_row[0:1, :],
                    rhs=pw["m2r"][0:1, off:off + ln_],
                    start=True, stop=True), inc="t")
                p.wait("vector", "t")
                p.op("vector", lambda e, ln_=ln_: e.tensor_scalar(
                    out=pw["mrho2b"][:, 0:ln_], in0=pb[:, 0:ln_],
                    scalar1=-1.0, scalar2=None, op0=ALU.mult), inc="v")
                p.wait("vector", "v")
                p.op("vector", lambda e, ln_=ln_: e.tensor_mul(
                    out=pw["prj"][:, 0:ln_], in0=pw["prj"][:, 0:ln_],
                    in1=pw["rho2b"][:, 0:ln_]), inc=None)
                p.op("vector", lambda e, ln_=ln_: e.tensor_add(
                    out=pw["prj"][:, 0:ln_], in0=pw["prj"][:, 0:ln_],
                    in1=pw["mrho2b"][:, 0:ln_]), inc=None)
                p.op("vector", lambda e, m=m, ln_=ln_: e.tensor_scalar(
                    out=pw["prj"][:, 0:ln_], in0=pw["prj"][:, 0:ln_],
                    scalar1=gcol[:, m:m + 1], scalar2=None, op0=ALU.mult),
                    inc=None)
                p.op("vector", lambda e, m=m, ln_=ln_: e.tensor_scalar(
                    out=pw["prj"][:, 0:ln_], in0=pw["prj"][:, 0:ln_],
                    scalar1=b2col[:, m:m + 1], scalar2=None, op0=ALU.add),
                    inc="v")
                p.wait("vector", "v")
                writer(m, off, ln_)

    def mk_bin_writer(pw, dst):
        def w(m, off, ln_):
            p.op("vector", lambda e, m=m, off=off, ln_=ln_: e.tensor_scalar(
                out=dst[:, m * T + off:m * T + off + ln_],
                in0=pw["prj"][:, 0:ln_], scalar1=1.0, scalar2=None,
                op0=ALU.is_ge), inc="v")
            p.wait("tensor", "v")
        return w

    pw1 = mk_pw("")
    proj_stats(pw1, wtd["q"], tabs["bq"], x2b)
    finalize_rows(pw1)
    proj_apply(pw1, wtd["q"], tabs["bq"], x2b, tabs["gq"], tabs["b2q"],
               mk_bin_writer(pw1, Qf))
    proj_stats(pw1, wtd["k"], tabs["bk"], x2b)
    finalize_rows(pw1)
    proj_apply(pw1, wtd["k"], tabs["bk"], x2b, tabs["gk"], tabs["b2k"],
               mk_bin_writer(pw1, Kb))
    proj_stats(pw1, wtd["v"], tabs["bv"], x2b)
    finalize_rows(pw1)
    proj_apply(pw1, wtd["v"], tabs["bv"], x2b, tabs["gv"], tabs["b2v"],
               mk_bin_writer(pw1, Vb))

    free_pw(pw1)
    p.fence_dma()

    _, dV = emit_allgather(p, Vb, KD * T, "vv", npart=P, dtype=BF16,
                           readback=False)

    colVl = p.sb("pt_colVl", [P, KD], F32)
    vscr = p.sb("pt_vscr", [P, T], BF16)
    KtVl = p.sb("pt_KtVl", [64, 16 * 64], F32)
    KtVs = p.sb("pt_KtVs", [64, 16 * 64], F32)
    gstg = p.sb("pt_gstg", [64, 16 * 64], F32)
    cstg = p.sb("pt_cstg", [P, KD], F32)
    ktm = p.sb("pt_ktm", [P, 64], BF16)
    vtm = p.sb("pt_vtm", [P, 64], BF16)
    rowc = p.sb("pt_rowc", [1, 16 * 64], F32)
    rbc = p.sb("pt_rbc", [P, 64], F32)
    ptA = p.psum("ptA", [P, 64], BF16)
    ptB = p.psum("ptB", [P, 64], BF16)
    pkv = p.psum("pkv", [P, 64], F32)
    pko = p.psum("pko", [P, 64], F32)
    psrA = p.psum("psrA", [1, 64], F32)
    psrB = p.psum("psrB", [1, 64], F32)

    for k in range(KD):
        p.wait("vector", "v")
        p.op("vector", lambda e, k=k: e.scalar_tensor_tensor(
            out=vscr[:], in0=Vb[:, k * T:(k + 1) * T], scalar=1.0,
            in1=Vb[:, k * T:(k + 1) * T], op0=ALU.mult, op1=ALU.mult,
            accum_out=colVl[:, k:k + 1]), inc="v")

    for h in range(16):
        base, k = (h % 2) * 64, h // 2
        ptX = ptA if h % 2 == 0 else ptB
        for cc in range(NC):
            p.wait("tensor", "v")
            p.op("tensor", lambda e, base=base, k=k, cc=cc, ptX=ptX: e.transpose(
                out=ptX[0:P, 0:64], in_=Kb[base:base + 64, k * T + cc * P:k * T + (cc + 1) * P],
                identity=cm.ident_bf[base:base + 64, base:base + 64]), inc="t")
            p.wait("vector", "t")
            p.op("vector", lambda e, ptX=ptX: e.tensor_copy(
                out=ktm[:, 0:64], in_=ptX[0:P, 0:64]), inc="v")
            p.wait("tensor", "v")
            p.op("tensor", lambda e, base=base, k=k, cc=cc, ptX=ptX: e.transpose(
                out=ptX[0:P, 0:64], in_=Vb[base:base + 64, k * T + cc * P:k * T + (cc + 1) * P],
                identity=cm.ident_bf[base:base + 64, base:base + 64]), inc="t")
            p.wait("vector", "t")
            p.op("vector", lambda e, ptX=ptX: e.tensor_copy(
                out=vtm[:, 0:64], in_=ptX[0:P, 0:64]), inc="v")
            p.wait("tensor", "v")
            p.op("tensor", lambda e, cc=cc: e.matmul(
                out=pkv[0:64, 0:64], lhsT=ktm[:, 0:64], rhs=vtm[:, 0:64],
                start=(cc == 0), stop=(cc == NC - 1)),
                inc=("t" if cc == NC - 1 else None))
        p.wait("vector", "t")
        p.op("vector", lambda e, h=h: e.tensor_copy(
            out=KtVl[0:64, h * 64:(h + 1) * 64], in_=pkv[0:64, 0:64]), inc="v")
        p.wait("tensor", "v")

    _, dC = emit_allgather(p, colVl, KD, "cv", npart=P, readback=False)
    _, dK = emit_allgather(p, KtVl, 16 * 64, "kv", npart=64, readback=False)
    p.op("vector", lambda e: e.memset(colVg[:], 0.0), inc="v")
    p.op("vector", lambda e: e.memset(KtVs[:], 0.0), inc="v")
    p.wait("vector", "v")
    for r in range(8):
        p.wait("sync", "v")
        p.wait("sync", "cc")
        p.dma("sync", cstg[:, :], dC.ap()[r * P:(r + 1) * P, :], inc="dma")
        p.dma("sync", gstg[:, :], dK.ap()[r * 64:(r + 1) * 64, :], inc="dma")
        p.wait("vector", "dma")
        p.op("vector", lambda e, r=r: e.scalar_tensor_tensor(
            out=colVg[:], in0=cstg[:], scalar=bmc[:, r:r + 1], in1=colVg[:],
            op0=ALU.mult, op1=ALU.add), inc=None)
        p.op("vector", lambda e, r=r: e.scalar_tensor_tensor(
            out=KtVs[:], in0=gstg[:], scalar=bmc[0:64, r:r + 1], in1=KtVs[:],
            op0=ALU.mult, op1=ALU.add), inc="v")
        p.wait("vector", "v")

    for h in range(16):
        base, k = (h % 2) * 64, h // 2
        if base == 0:
            p.op("vector", lambda e, h=h, k=k: e.tensor_copy(
                out=KtVg[0:64, k * 64:(k + 1) * 64],
                in_=KtVs[0:64, h * 64:(h + 1) * 64]), inc="v")
        else:
            p.wait("sync", "v")
            p.dma("sync", KtVg[64:128, k * 64:(k + 1) * 64],
                  KtVs[0:64, h * 64:(h + 1) * 64], inc="dma")
    p.wait("vector", "dma")
    p.wait("tensor", "dma")
    p.wait("tensor", "v")
    for h in range(16):
        base, k = (h % 2) * 64, h // 2
        psk = pkv if base == 0 else pko
        p.op("tensor", lambda e, base=base, k=k, psk=psk: e.matmul(
            out=psk[base:base + 64, 0:1],
            lhsT=KtVg[base:base + 64, k * 64:(k + 1) * 64],
            rhs=cm.ones_col[base:base + 64, 0:1], start=True, stop=True), inc="t")
        p.wait("vector", "t")
        p.op("vector", lambda e, base=base, k=k, psk=psk: e.tensor_copy(
            out=c3t[base:base + 64, k:k + 1], in_=psk[base:base + 64, 0:1]),
            inc="v")
        p.wait("tensor", "v")
    p.wait("vector", "v")
    p.op("vector", lambda e: e.scalar_tensor_tensor(
        out=c3t[:], in0=colVg[:], scalar=64.0, in1=c3t[:],
        op0=ALU.mult, op1=ALU.subtract), inc="v")
    p.wait("tensor", "v")
    for h in range(16):
        base, k = (h % 2) * 64, h // 2
        psr = psrA if base == 0 else psrB
        p.op("tensor", lambda e, base=base, k=k, psr=psr: e.transpose(
            out=psr[0:1, 0:64], in_=colVg[base:base + 64, k:k + 1],
            identity=cm.ident_f[base:base + 64, base:base + 64]), inc="t")
        p.wait("vector", "t")
        p.op("vector", lambda e, h=h, psr=psr: e.tensor_copy(
            out=rowc[0:1, h * 64:(h + 1) * 64], in_=psr[0:1, 0:64]), inc="v")
        p.wait("tensor", "v")
    for h in range(16):
        base, k = (h % 2) * 64, h // 2
        p.op("tensor", lambda e, h=h: e.matmul(
            out=pb[:, 0:64], lhsT=cm.ones_row[0:1, :],
            rhs=rowc[0:1, h * 64:(h + 1) * 64], start=True, stop=True), inc="t")
        p.wait("vector", "t")
        p.op("vector", lambda e: e.tensor_copy(out=rbc[:, 0:64],
                                               in_=pb[:, 0:64]), inc="v")
        p.wait("vector", "v")
        p.op("vector", lambda e, base=base, k=k: e.scalar_tensor_tensor(
            out=combT[base:base + 64, k * 64:(k + 1) * 64],
            in0=KtVg[base:base + 64, k * 64:(k + 1) * 64], scalar=2.0,
            in1=rbc[base:base + 64, 0:64], op0=ALU.mult, op1=ALU.subtract),
            inc="v")
        p.wait("tensor", "v")

    csp1 = p.sb("pt_csp1", [P, KD * 64], F32)
    p.wait("vector", "v")
    p.op("vector", lambda e: e.tensor_scalar(
        out=csp1[:], in0=combT[:], scalar1=1.0 / 256.0, scalar2=8388608.0,
        op0=ALU.mult, op1=ALU.add), inc="v")
    p.wait("vector", "v")
    p.op("vector", lambda e: e.tensor_scalar(
        out=csp1[:], in0=csp1[:], scalar1=-8388608.0, scalar2=256.0,
        op0=ALU.add, op1=ALU.mult), inc="v")
    p.wait("vector", "v")
    p.op("vector", lambda e: e.tensor_copy(out=combH[:], in_=csp1[:]), inc=None)
    p.op("vector", lambda e: e.tensor_sub(
        out=combL[:], in0=combT[:], in1=csp1[:]), inc="v")
    p.wait("vector", "v")

    p.free("psrB", "psrA", "pko", "pkv", "ptB", "ptA")
    p.free("pt_csp1")
    p.free("pt_rbc", "pt_rowc", "pt_vtm", "pt_ktm", "pt_cstg", "pt_gstg",
           "pt_KtVs", "pt_KtVl", "pt_vscr", "pt_colVl")
    p.free("Vb", "Kb")
    p.fence_dma()
    p.wait("vector", "cc")
    p.wait("scalar", "cc")
    p.wait("tensor", "cc")

    onesT = p.sb("bv_ones", [P, T], F32)
    Pfull = p.sb("bv_P", [P, NT], F32)
    biasVj = p.sb("bv_b", [P, NT], F32)
    vstg = p.sb("bv_vstg", [P, T], BF16)
    vf = p.sb("bv_vf", [P, T], F32)
    offc = p.sb("bv_off", [P, 1], F32)
    p.op("vector", lambda e: e.memset(onesT[:], 1.0), inc="v")
    p.wait("vector", "v")
    dB = []
    for j in range(2):
        if j == 1:
            p.wait("vector", "ag")
        for blk in range(4):
            col = j * 4 + blk
            p.wait("gpsimd", "cc")
            p.wait("gpsimd", "v")
            p.op("gpsimd", lambda e, col=col: e.indirect_dma_start(
                out=vstg[:, :], out_offset=None, in_=dV.ap(),
                in_offset=IndirectOffsetOnAxis(ap=vix[:, col:col + 1], axis=1)),
                inc="ig", amt=16)
            p.wait("vector", "ig")
            p.op("vector", lambda e: e.tensor_copy(out=vf[:], in_=vstg[:]),
                 inc="v")
            p.wait("vector", "v")
            p.op("vector", lambda e, blk=blk: e.tensor_tensor_scan(
                out=Pfull[:, blk * T:(blk + 1) * T], data0=onesT[:],
                data1=vf[:], initial=0.0, op0=ALU.mult, op1=ALU.add), inc="v")
            p.wait("vector", "v")
            if blk > 0:
                p.op("vector", lambda e, blk=blk: e.tensor_scalar(
                    out=Pfull[:, blk * T:(blk + 1) * T],
                    in0=Pfull[:, blk * T:(blk + 1) * T], scalar1=offc[:, 0:1],
                    scalar2=None, op0=ALU.add), inc="v")
                p.wait("vector", "v")
            if blk < 3:
                p.op("vector", lambda e, blk=blk: e.tensor_copy(
                    out=offc[:, 0:1],
                    in_=Pfull[:, (blk + 1) * T - 1:(blk + 1) * T]), inc="v")
                p.wait("vector", "v")
        p.op("vector", lambda e: e.memset(biasVj[:], 0.0), inc="v")
        p.wait("vector", "v")
        for Bw in _bw_list(NT):
            hi_len = NT - 1 - Bw
            if hi_len > 0:
                p.op("vector", lambda e, Bw=Bw, hi_len=hi_len: e.tensor_add(
                    out=biasVj[:, 0:hi_len], in0=biasVj[:, 0:hi_len],
                    in1=Pfull[:, Bw:Bw + hi_len]), inc="v")
                p.wait("vector", "v")
            p.op("vector", lambda e, hi_len=hi_len: e.tensor_scalar(
                out=biasVj[:, hi_len:NT], in0=biasVj[:, hi_len:NT],
                scalar1=Pfull[:, NT - 1:NT], scalar2=None, op0=ALU.add), inc="v")
            p.wait("vector", "v")
            lo = Bw + 1
            p.op("vector", lambda e, lo=lo: e.tensor_sub(
                out=biasVj[:, lo:NT], in0=biasVj[:, lo:NT],
                in1=Pfull[:, 0:NT - lo]), inc="v")
            p.wait("vector", "v")
        _, dj = emit_allgather(p, biasVj, NT, f"b{j}", npart=P, readback=False)
        dB.append(dj)

    p.free("bv_off", "bv_vf", "bv_vstg", "bv_b", "bv_P", "bv_ones")
    p.fence_dma()
    p.wait("vector", "cc")
    p.wait("tensor", "cc")

    bVk = p.sb("pa_bVk", [P, T], F32)
    otmp = p.sb("pa_otmp", [P, 512], F32)
    psE = p.psum("psE", [P, 512], F32)
    for k in range(KD):
        p.wait("gpsimd", "v")
        p.wait("gpsimd", "cc")
        p.op("gpsimd", lambda e, k=k: e.indirect_dma_start(
            out=bVk[:, :], out_offset=None, in_=dB[k % 2].ap(),
            in_offset=IndirectOffsetOnAxis(ap=bix[:, k:k + 1], axis=1)),
            inc="ig", amt=16)
        p.wait("vector", "ig")
        p.wait("tensor", "ig")
        for sub in (0, 1):
            base = sub * 64
            ps = ps1 if sub == 0 else psE
            for off, ln_ in tok_blocks(T, 512):
                p.wait("tensor", "v")
                p.op("tensor", lambda e, base=base, k=k, off=off, ln_=ln_, ps=ps: e.matmul(
                    out=ps[base:base + 64, 0:ln_],
                    lhsT=combH[base:base + 64, k * 64:(k + 1) * 64],
                    rhs=Qf[base:base + 64, k * T + off:k * T + off + ln_],
                    start=True, stop=False), inc=None)
                p.op("tensor", lambda e, base=base, k=k, off=off, ln_=ln_, ps=ps: e.matmul(
                    out=ps[base:base + 64, 0:ln_],
                    lhsT=combL[base:base + 64, k * 64:(k + 1) * 64],
                    rhs=Qf[base:base + 64, k * T + off:k * T + off + ln_],
                    start=False, stop=True), inc="t")
                p.wait("vector", "t")
                p.op("vector", lambda e, base=base, k=k, ln_=ln_, ps=ps: e.tensor_scalar(
                    out=otmp[base:base + 64, 0:ln_], in0=ps[base:base + 64, 0:ln_],
                    scalar1=c3t[base:base + 64, k:k + 1], scalar2=None,
                    op0=ALU.add), inc="v")
                p.wait("vector", "v")
                p.op("vector", lambda e, base=base, off=off, ln_=ln_: e.tensor_add(
                    out=otmp[base:base + 64, 0:ln_],
                    in0=otmp[base:base + 64, 0:ln_],
                    in1=bVk[base:base + 64, off:off + ln_]), inc="v")
                p.wait("vector", "v")
                p.op("vector", lambda e, base=base, k=k, off=off, ln_=ln_: e.tensor_scalar(
                    out=Ofm[base:base + 64, k * T + off:k * T + off + ln_],
                    in0=otmp[base:base + 64, 0:ln_],
                    scalar1=scol[base:base + 64, 0:1], scalar2=0.5,
                    op0=ALU.mult, op1=ALU.is_ge), inc="v")
                p.wait("tensor", "v")

    p.free("psE")
    p.free("pa_otmp", "pa_bVk")
    p.free("combL", "combH", "combT", "c3t", "colVg", "KtVg")

    Yb = p.sb("Yb", [P, KD * T], BF16)
    pw2 = mk_pw("2")

    def o_writer(m, off, ln_):
        p.op("vector", lambda e, m=m, off=off, ln_=ln_: e.tensor_scalar(
            out=Yb[:, m * T + off:m * T + off + ln_], in0=pw2["prj"][:, 0:ln_],
            scalar1=1.0, scalar2=None, op0=ALU.is_ge), inc="v")
        p.wait("tensor", "v")

    proj_stats(pw2, wtd["o"], tabs["bo"], Ofm)
    finalize_rows(pw2)
    proj_apply(pw2, wtd["o"], tabs["bo"], Ofm, tabs["go"], tabs["b2o"],
               o_writer)

    xld = p.sb("xld", [P, T], F32)
    yout = p.sb("yout", [P, T], F32)
    for k in range(KD):
        p.wait("sync", "v")
        p.dma("sync", xld[:, :], xspill.ap()[k * P:(k + 1) * P, :], inc="sp")
        p.wait("vector", "sp")
        p.wait("vector", "st")
        p.op("vector", lambda e, k=k: e.tensor_add(
            out=yout[:], in0=xld[:], in1=Yb[:, k * T:(k + 1) * T]), inc="v")
        p.wait("sync", "v")
        p.dma("sync", yd[k * P:(k + 1) * P, :], yout[:], inc="st")
    p.wait("sync", "st")
    p.finish()
    return p


def _l1ab_base(inp):
    def wT(nm):
        return np.ascontiguousarray(
            np.asarray(inp[f"L1_attn_b_{nm}"], np.float32).T)

    def col(nm):
        return np.asarray(inp[f"L1_attn_b_{nm}"], np.float32).reshape(D, 1)
    base = {
        "WqT": wT("Wq"), "WkT": wT("Wk"), "WvT": wT("Wv"), "WoT": wT("Wo"),
        "bq": col("bq"), "bk": col("bk"), "bv": col("bv"), "bo": col("bo"),
        "gq": col("gq"), "b2q": col("betaq"), "gk": col("gk"),
        "b2k": col("betak"), "gv": col("gv"), "b2v": col("betav"),
        "go": col("go"), "b2o": col("betao"),
        "scalec": np.full((P, 1), np.float32(inp["L1_attn_b_scale"]),
                          np.float32),
        **Common.host_inputs(),
    }
    return base


def _l1ab_maps(inp, x):
    xs = _fm_slices(x)
    base = _l1ab_base(inp)
    maps = []
    pcol = np.arange(P, dtype=np.uint64)
    for c in range(NCORES):
        b, co = c // 4, c % 4
        bm = np.zeros((P, 8), np.float32)
        for r in range(8):
            if r // 4 == b:
                bm[:, r] = 1.0
        vidx = np.zeros((P, 8), np.uint32)
        for j in range(2):
            for rr in range(4):
                vidx[:, j * 4 + rr] = (((4 * b + rr) * 128 + pcol) * (KD * T)
                                       + (2 * co + j) * T).astype(np.uint32)
        bidx = np.zeros((P, 8), np.uint32)
        for k in range(KD):
            bidx[:, k] = (((4 * b + k // 2) * 128 + pcol) * SEQ
                          + co * T).astype(np.uint32)
        maps.append({**base, "x": xs[c], "bmask": bm, "vidx": vidx,
                     "bidx": bidx})
    return maps


def run_l1_attn_b(x, inp):
    pr = _get_prog("l1ab", build_l1_attn_b)
    in_maps = _l1ab_maps(inp, x)
    res = run_bass_kernel_spmd(pr.nc, in_maps, core_ids=list(range(NCORES)))
    return _unshard([res.results[c]["y"] for c in range(NCORES)])


# version 03
# speedup vs baseline: 1.0007x; 1.0007x over previous
import numpy as np
from contextlib import ExitStack

import concourse.bass as bass
import concourse.mybir as mybir
from concourse.bass_utils import run_bass_kernel_spmd
from concourse.bass_core_rust import IndirectOffsetOnAxis

F32 = mybir.dt.float32
BF16 = mybir.dt.bfloat16
U32 = mybir.dt.uint32
I16 = mybir.dt.int16
AF = mybir.ActivationFunctionType
ALU = mybir.AluOpType

P = 128
D = 1024
KD = D // P
T = 2048
NCORES = 8
SEQ = 8192

ENGS = ("sync", "tensor", "vector", "scalar", "gpsimd")


class Prog:

    def __init__(self, name="prog"):
        self.nc = bass.Bass("TRN2", target_bir_lowering=False, debug=False)
        self.nc.detect_race_conditions = False
        self.es = ExitStack()
        self.ops = {e: [] for e in ENGS}
        self.sems = {}
        self.cnt = {}
        self.name = name
        self._uid = 0
        self._guards = {}
        self._pre = ""

    def uid(self):
        self._uid += 1
        return self._uid

    def sb(self, name, shape, dtype):
        name = self._pre + name
        g = self.nc.sbuf_tensor(name, shape, dtype)
        h = g.__enter__()
        self._guards[name] = g
        return h

    def psum(self, name, shape, dtype=F32):
        name = self._pre + name
        g = self.nc.psum_tensor(name, shape, dtype)
        h = g.__enter__()
        self._guards[name] = g
        return h

    def free(self, *names):
        for n in names:
            n = self._pre + n if (self._pre + n) in self._guards else n
            g = self._guards.pop(n)
            g.__exit__(None, None, None)

    def dram(self, name, shape, dtype, kind=None, shared=False):
        name = self._pre + name
        kw = {}
        if kind:
            kw["kind"] = kind
        if shared:
            kw["addr_space"] = "Shared"
        return self.nc.dram_tensor(name, shape, dtype, **kw)

    def begin_epoch(self, pre):
        self._pre = pre
        return set(self._guards.keys())

    def end_epoch(self, before):
        for n in [k for k in reversed(list(self._guards.keys()))
                  if k not in before]:
            g = self._guards.pop(n)
            g.__exit__(None, None, None)
        if hasattr(self, "_ccols"):
            self._ccols = {k: v for k, v in self._ccols.items()
                           if v[1] in self._guards}
        self._pre = ""

    def sem(self, name):
        if name not in self.sems:
            self.sems[name] = self.es.enter_context(self.nc.semaphore(name))
            self.cnt[name] = 0
        return self.sems[name]

    def op(self, eng, fn, inc=None, amt=1):
        if inc is not None:
            sem = self.sem(inc)
            self.cnt[inc] += amt

            def wrapped(e, fn=fn, sem=sem, amt=amt):
                ins = fn(e)
                ins.then_inc(sem, amt)
            self.ops[eng].append(wrapped)
        else:
            self.ops[eng].append(fn)

    def wait(self, eng, semname, val=None):
        sem = self.sem(semname)
        v = self.cnt[semname] if val is None else val
        self.ops[eng].append(lambda e, sem=sem, v=v: e.wait_ge(sem, v))

    def dma(self, eng, out, in_, inc="dma"):
        self.op(eng, lambda e, out=out, in_=in_: e.dma_start(out=out, in_=in_),
                inc=inc, amt=16)

    def fence_dma(self, eng="sync"):
        for s in ("v", "t", "s"):
            if s in self.cnt:
                self.wait(eng, s)

    def constcol(self, value, np_=1):
        key = ("cc", float(value), np_)
        if not hasattr(self, "_ccols"):
            self._ccols = {}
        if key not in self._ccols:
            nm = f"ccol_{self.uid()}"
            t = self.sb(nm, [np_, 1], F32)
            self.op("vector", lambda e, t=t, v=float(value): e.memset(t[:], v), inc="v")
            self.wait("vector", "v")
            self._ccols[key] = (t, self._pre + nm)
        return self._ccols[key][0]

    def finish(self):
        with self.nc.Block() as block:
            for engname in ENGS:
                ops = self.ops[engname]
                if not ops:
                    continue

                def make(ops):
                    def run(e):
                        for fn in ops:
                            fn(e)
                    return run
                getattr(block, engname)(make(ops))
        for g in reversed(list(self._guards.values())):
            g.__exit__(None, None, None)
        self._guards.clear()
        self.es.close()
        return self.nc


def tok_blocks(total, blk=512):
    out = []
    o = 0
    while o < total:
        out.append((o, min(blk, total - o)))
        o += blk
    return out


def _bf16np():
    import ml_dtypes
    return ml_dtypes.bfloat16



class Common:
    def __init__(self, p: Prog):
        self.p = p
        self.ident_bf = p.sb("c_ident_bf", [P, P], BF16)
        self.ones_row = p.sb("c_ones_row", [1, P], F32)
        self.ones_col = p.sb("c_ones_col", [P, 1], F32)
        self.ident_f = p.sb("c_ident_f", [P, P], F32)
        cd = p.dram("c_ident", [P, P], BF16, kind="ExternalInput").ap()
        p.dma("sync", self.ident_bf[:], cd)
        p.op("vector", lambda e: e.memset(self.ones_row[:], 1.0), inc="v")
        p.op("vector", lambda e: e.memset(self.ones_col[:], 1.0), inc="v")
        p.wait("vector", "dma")
        p.wait("vector", "v")
        p.op("vector", lambda e: e.tensor_copy(out=self.ident_f[:], in_=self.ident_bf[:]), inc="v")
        junk = p.sb("c_accjunk", [P, 2], F32)
        p.wait("vector", "v")
        p.op("vector", lambda e, junk=junk: e.scalar_tensor_tensor(
            out=junk[:, 0:1], in0=self.ones_col[:], scalar=0.0, in1=self.ones_col[:],
            op0=ALU.mult, op1=ALU.mult, accum_out=junk[:, 1:2]), inc="v")

    @staticmethod
    def host_inputs():
        return {"c_ident": np.eye(P, dtype=np.float32).astype(_bf16np())}


def emit_bcast_row(p, cm, src_row, dst, N, ps, scale=1.0):
    for off, ln_ in tok_blocks(N):
        p.op("tensor", lambda e, off=off, ln_=ln_: e.matmul(
            out=ps[:, 0:ln_], lhsT=cm.ones_row[0:1, :], rhs=src_row[0:1, off:off + ln_],
            start=True, stop=True), inc="t")
        p.wait("vector", "t")
        if scale == 1.0:
            p.op("vector", lambda e, off=off, ln_=ln_: e.tensor_copy(
                out=dst[:, off:off + ln_], in_=ps[:, 0:ln_]), inc="v")
        else:
            p.op("vector", lambda e, off=off, ln_=ln_: e.tensor_scalar(
                out=dst[:, off:off + ln_], in0=ps[:, 0:ln_],
                scalar1=float(scale), scalar2=None, op0=ALU.mult), inc="v")
        p.wait("tensor", "v")
    p.wait("vector", "v")


def emit_ln(p, cm, xt, Tlen, tag, eps=1e-5, xstride=None):
    xs_ = xstride if xstride is not None else Tlen
    epsc = p.constcol(eps)
    rho_bc = p.sb(f"ln_rho_{tag}", [P, Tlen], F32)
    mrho_bc = p.sb(f"ln_mrho_{tag}", [P, Tlen], F32)
    pb = p.psum(f"ln_pb_{tag}", [P, 512], F32)
    A = p.sb(f"ln_A_{tag}", [1, Tlen], F32)
    B = p.sb(f"ln_B_{tag}", [1, Tlen], F32)
    Crow = p.sb(f"ln_C_{tag}", [1, Tlen], F32)
    sq = p.sb(f"ln_sq_{tag}", [P, KD * 512], F32)
    pr0 = p.psum(f"ln_pr0_{tag}", [1, 512], F32)
    pr1 = p.psum(f"ln_pr1_{tag}", [1, 512], F32)

    for off, ln_ in tok_blocks(Tlen):
        p.wait("scalar", "t")
        for k in range(KD):
            p.op("scalar", lambda e, k=k, off=off, ln_=ln_: e.activation(
                out=sq[:, k * 512:k * 512 + ln_],
                in_=xt[:, k * xs_ + off: k * xs_ + off + ln_], func=AF.Square),
                inc=("s" if k == KD - 1 else None))
        p.wait("tensor", "s")
        for k in range(KD):
            p.op("tensor", lambda e, k=k, off=off, ln_=ln_: e.matmul(
                out=pr0[0:1, 0:ln_], lhsT=cm.ones_col[:, 0:1],
                rhs=xt[:, k * xs_ + off: k * xs_ + off + ln_],
                start=(k == 0), stop=(k == KD - 1)), inc=None)
        for k in range(KD):
            p.op("tensor", lambda e, k=k, off=off, ln_=ln_: e.matmul(
                out=pr1[0:1, 0:ln_], lhsT=cm.ones_col[:, 0:1],
                rhs=sq[:, k * 512:k * 512 + ln_],
                start=(k == 0), stop=(k == KD - 1)),
                inc=("t" if k == KD - 1 else None))
        p.wait("vector", "t")
        p.op("vector", lambda e, off=off, ln_=ln_: e.tensor_copy(
            out=A[:, off:off + ln_], in_=pr0[0:1, 0:ln_]), inc=None)
        p.op("vector", lambda e, off=off, ln_=ln_: e.tensor_copy(
            out=B[:, off:off + ln_], in_=pr1[0:1, 0:ln_]), inc="v")
        p.wait("tensor", "v")
    p.free(f"ln_pr1_{tag}", f"ln_pr0_{tag}", f"ln_sq_{tag}")

    p.wait("vector", "v")
    p.op("vector", lambda e: e.tensor_scalar(out=A[:], in0=A[0:1, :],
                                             scalar1=1.0 / D, scalar2=None,
                                             op0=ALU.mult), inc="v")
    p.wait("scalar", "v")
    p.op("scalar", lambda e: e.activation(out=Crow[:], in_=A[:], func=AF.Square), inc="s")
    p.wait("vector", "s")
    p.op("vector", lambda e: e.scalar_tensor_tensor(
        out=B[:], in0=B[0:1, :], scalar=1.0 / D, in1=Crow[:],
        op0=ALU.mult, op1=ALU.subtract), inc="v")
    p.wait("scalar", "v")
    p.op("scalar", lambda e, epsc=epsc: e.activation(out=B[:], in_=B[:], func=AF.Ln,
                                                     bias=epsc[0:1, 0:1]), inc="s")
    p.wait("scalar", "s")
    p.op("scalar", lambda e: e.activation(out=B[:], in_=B[:], func=AF.Exp,
                                          scale=-0.5), inc="s")
    p.wait("vector", "s")
    p.op("vector", lambda e: e.tensor_mul(out=Crow[:], in0=A[:], in1=B[:]), inc="v")
    p.wait("tensor", "v")
    emit_bcast_row(p, cm, B, rho_bc, Tlen, pb)
    emit_bcast_row(p, cm, Crow, mrho_bc, Tlen, pb, scale=-1.0)
    p._ln_rows = {"A": A, "B": B, "C": Crow, "pb": pb}
    names = [f"ln_C_{tag}", f"ln_B_{tag}", f"ln_A_{tag}", f"ln_pb_{tag}"]
    return rho_bc, mrho_bc, names


def emit_ln_xnb(p, cm, xt, Tlen, tag):
    xnb = p.sb(f"xnb_{tag}", [P, KD * Tlen], BF16)
    rho_bc, mrho_bc, names = emit_ln(p, cm, xt, Tlen, tag)
    xnf = p.sb(f"xnf_{tag}", [P, Tlen], F32)
    for k in range(KD):
        p.wait("vector", "v")
        p.op("vector", lambda e, k=k: e.scalar_tensor_tensor(
            out=xnf[:], in0=xt[:, k * Tlen:(k + 1) * Tlen],
            scalar=1.0, in1=rho_bc[:], op0=ALU.mult, op1=ALU.mult), inc=None)
        p.op("vector", lambda e, k=k: e.tensor_add(
            out=xnf[:], in0=xnf[:], in1=mrho_bc[:]), inc=None)
        p.op("vector", lambda e, k=k: e.tensor_copy(
            out=xnb[:, k * Tlen:(k + 1) * Tlen], in_=xnf[:]), inc="v")
    p._ln_keep = {"rho_bc": rho_bc, "mrho_bc": mrho_bc}
    order = [f"xnf_{tag}"] + names + [f"ln_mrho_{tag}", f"ln_rho_{tag}"]
    return xnb, order


def emit_load_wbf(p, Wd, Din, Dout, wbuf, stg, woff=0):
    nk = Din // P
    p.wait("sync", "t")
    for k in range(nk):
        p.wait("sync", "v")
        p.dma("sync", stg[:, 0:Dout], Wd[k * P:(k + 1) * P, :], inc="dma")
        p.wait("vector", "dma")
        p.op("vector", lambda e, k=k: e.tensor_copy(
            out=wbuf[:, woff + k * Dout:woff + (k + 1) * Dout], in_=stg[:, 0:Dout]), inc="v")


HALO_L, HALO_R = 32, 31
TE = 2112


def build_l0_attn_a():
    p = Prog("l0aa")
    cm = Common(p)
    yd = p.dram("y", [D, T], F32, kind="ExternalOutput").ap()
    emit_l0aa(p, cm, yd)
    p.finish()
    return p


def emit_l0aa(p, cm, yd, xin=None):
    xd = p.dram("x", [D, TE], F32, kind="ExternalInput").ap()
    Ad = p.dram("Ain", [1, 64], F32, kind="ExternalInput").ap()
    Bd = p.dram("Bin", [64, D], F32, kind="ExternalInput").ap()
    Cd = p.dram("Cin", [D, 64], F32, kind="ExternalInput").ap()
    Dd = p.dram("Dvec", [D, 1], F32, kind="ExternalInput").ap()
    ldt = p.dram("log_dt", [D, 1], F32, kind="ExternalInput").ap()

    xt = p.sb("xt", [P, KD * TE], F32)
    un = p.sb("un", [P, TE], F32)
    acc = p.sb("acc", [P, T], F32)

    for k in range(KD):
        p.dma("sync", xt[:, k * TE:(k + 1) * TE], xd[k * P:(k + 1) * P, :])
    p.wait("vector", "dma")
    p.wait("scalar", "dma")
    p.wait("tensor", "dma")

    rho_bc, mrho_bc, _ln_names = emit_ln(p, cm, xt, TE, "aa")

    Bt = p.sb("Bt", [64, D], F32)
    Arow = p.sb("Arow", [1, 64], F32)
    Dcol = p.sb("Dcol", [P, KD], F32)
    dtc = p.sb("dtc", [P, KD], F32)
    Ct = p.sb("Ct", [P, KD * 64], F32)
    p.fence_dma()
    p.dma("sync", Bt[:, :], Bd)
    p.dma("sync", Arow[:, :], Ad)
    for k in range(KD):
        p.dma("sync", Ct[:, k * 64:(k + 1) * 64], Cd[k * P:(k + 1) * P, :])
        p.dma("sync", Dcol[:, k:k + 1], Dd[k * P:(k + 1) * P, :])
        p.dma("sync", dtc[:, k:k + 1], ldt[k * P:(k + 1) * P, :])

    csum_ps = p.psum("csum_ps", [1, 64], F32)
    csum_row = p.sb("csum_row", [1, 64], F32)
    cs_bc = p.sb("cs_bc", [P, 64], F32)
    Arow_bc = p.sb("Arow_bc", [P, 64], F32)
    ps_small = p.psum("ps_small", [P, 512], F32)
    W0T = p.sb("W0T", [P, KD * 64], F32)
    Abar = p.sb("Abar", [P, KD * 64], F32)
    Pl = p.sb("Pl", [P, 64], F32)
    qtmp = p.sb("qtmp", [P, 64], F32)
    Kt = p.sb("Kt", [P, KD * 64], F32)
    bt_ps = p.psum("bt_ps", [P, 64], F32)

    p.wait("scalar", "dma")
    p.op("scalar", lambda e: e.activation(out=dtc[:], in_=dtc[:], func=AF.Exp), inc="s")
    p.wait("tensor", "dma")
    for k in range(KD):
        p.op("tensor", lambda e, k=k: e.matmul(
            out=csum_ps[:, :], lhsT=cm.ones_col[:, 0:1],
            rhs=Ct[:, k * 64:(k + 1) * 64],
            start=(k == 0), stop=(k == KD - 1)), inc=("t" if k == KD - 1 else None))
    p.wait("vector", "t")
    p.op("vector", lambda e: e.tensor_copy(out=csum_row[:], in_=csum_ps[:]), inc="v")
    p.wait("tensor", "v")
    emit_bcast_row(p, cm, csum_row, cs_bc, 64, ps_small)
    emit_bcast_row(p, cm, Arow, Arow_bc, 64, ps_small)
    for k in range(KD):
        p.wait("tensor", "v")
        p.op("tensor", lambda e, k=k: e.transpose(
            out=bt_ps[0:P, 0:64], in_=Bt[:, k * P:(k + 1) * P],
            identity=cm.ident_f[0:64, 0:64]), inc="t")
        p.wait("vector", "t")
        p.op("vector", lambda e, k=k: e.tensor_mul(
            out=W0T[:, k * 64:(k + 1) * 64], in0=bt_ps[0:P, 0:64], in1=cs_bc[:, :]), inc="v")
        p.wait("tensor", "v")
    p.wait("scalar", "v")
    p.wait("scalar", "s")
    for k in range(KD):
        p.op("scalar", lambda e, k=k: e.activation(
            out=Abar[:, k * 64:(k + 1) * 64], in_=Arow_bc[:, :], func=AF.Exp,
            scale=dtc[:, k:k + 1]), inc="s")
    p.wait("vector", "s")
    for k in range(KD):
        p.op("vector", lambda e: e.memset(Pl[:], 1.0), inc="v")
        p.wait("vector", "v")
        for l in range(64):
            p.op("vector", lambda e, k=k, l=l: e.scalar_tensor_tensor(
                out=qtmp[:], in0=Pl[:], scalar=1.0, in1=W0T[:, k * 64:(k + 1) * 64],
                op0=ALU.mult, op1=ALU.mult,
                accum_out=Kt[:, k * 64 + l:k * 64 + l + 1]), inc=None)
            if l < 63:
                p.op("vector", lambda e, k=k: e.tensor_mul(
                    out=Pl[:], in0=Pl[:], in1=Abar[:, k * 64:(k + 1) * 64]), inc=None)
        p.op("vector", lambda e: e.engine_nop(), inc="v")

    for k in range(KD):
        p.wait("vector", "st")
        p.wait("vector", "v")
        p.wait("vector", "s")
        p.op("vector", lambda e, k=k: e.scalar_tensor_tensor(
            out=un[:], in0=xt[:, k * TE:(k + 1) * TE],
            scalar=1.0, in1=rho_bc[:], op0=ALU.mult, op1=ALU.mult), inc=None)
        p.op("vector", lambda e, k=k: e.tensor_add(
            out=un[:], in0=un[:], in1=mrho_bc[:]), inc=None)
        p.op("vector", lambda e, k=k: e.tensor_scalar(
            out=acc[:], in0=un[:, HALO_L:HALO_L + T],
            scalar1=Dcol[:, k:k + 1], scalar2=None, op0=ALU.mult), inc=None)
        for l in range(64):
            off = 63 - l
            p.op("vector", lambda e, k=k, l=l, off=off: e.scalar_tensor_tensor(
                out=acc[:], in0=un[:, off:off + T],
                scalar=Kt[:, k * 64 + l:k * 64 + l + 1],
                in1=acc[:], op0=ALU.mult, op1=ALU.add), inc=None)
        p.op("vector", lambda e, k=k: e.tensor_add(
            out=acc[:], in0=acc[:],
            in1=xt[:, k * TE + HALO_L:k * TE + HALO_L + T]), inc="v")
        p.wait("sync", "v")
        p.dma("sync", yd[k * P:(k + 1) * P, :], acc[:], inc="st")
    p.wait("sync", "st")



def build_l1_ffn(dbg=False):
    p = Prog("l1ffn")
    cm = Common(p)
    yd = p.dram("y", [D, T], F32, kind="ExternalOutput").ap()
    emit_l1ffn(p, cm, yd, dbg=dbg)
    p.finish()
    return p


def emit_l1ffn(p, cm, yd, xin=None, dbg=False):
    xd = (xin if xin is not None
          else p.dram("x", [D, T], F32, kind="ExternalInput").ap())
    rwd = p.dram("router_w", [8, D], F32, kind="ExternalInput").ap()
    rbd = p.dram("router_bias", [1, 8], F32, kind="ExternalInput").ap()
    sw1d = p.dram("sw1", [D, 2048], F32, kind="ExternalInput").ap()
    sw3d = p.dram("sw3", [D, 2048], F32, kind="ExternalInput").ap()
    sw2d = p.dram("sw2", [2048, D], F32, kind="ExternalInput").ap()
    ew1d = p.dram("ew1", [8 * D, D], F32, kind="ExternalInput").ap()
    ew2d = p.dram("ew2", [8 * D, D], F32, kind="ExternalInput").ap()
    eye8d = p.dram("eye8", [8, 8], F32, kind="ExternalInput").ap()
    if dbg:
        ycomb = p.dram("ycomb", [8, T], F32, kind="ExternalOutput").ap()
        yglu = p.dram("yglu", [D, T], F32, kind="ExternalOutput").ap()
        ydbg = p.dram("ydbg", [P, 72], F32, kind="ExternalOutput").ap()

    xt = p.sb("xt", [P, KD * T], F32)
    for k in range(KD):
        p.dma("sync", xt[:, k * T:(k + 1) * T], xd[k * P:(k + 1) * P, :])
    p.wait("vector", "dma")
    p.wait("scalar", "dma")
    p.wait("tensor", "dma")

    eye8 = p.sb("eye8t", [8, 8], F32)
    comb_fm = p.sb("comb_fm", [8, T], F32)
    rps = p.psum("rps", [P, 512], F32)
    lps = p.psum("lps", [8, 512], F32)
    xnb, ln_free = emit_ln_xnb(p, cm, xt, T, "f1")

    rw = p.sb("rw", [8, D], F32)
    rwT = p.sb("rwT", [P, KD * 8], F32)
    rb8 = p.sb("rb8", [1, 8], F32)
    logit = p.sb("logit", [8, T], F32)
    rs8 = p.sb("rs8", [8, 1], F32)
    p.fence_dma()
    p.dma("sync", rw[:, :], rwd)
    p.dma("sync", rb8[:, :], rbd)
    p.dma("sync", eye8[:, :], eye8d)
    p.wait("tensor", "dma")
    p.wait("vector", "dma")
    p.op("vector", lambda e: e.tensor_reduce(out=rs8[:], in_=rw[:, :],
                                             axis=mybir.AxisListType.X,
                                             op=ALU.add), inc="v")
    for k in range(KD):
        p.wait("tensor", "v")
        p.op("tensor", lambda e, k=k: e.transpose(
            out=rps[0:P, 0:8], in_=rw[0:8, k * P:(k + 1) * P],
            identity=cm.ident_f[0:8, 0:8]), inc="t")
        p.wait("vector", "t")
        p.op("vector", lambda e, k=k: e.tensor_copy(
            out=rwT[:, k * 8:(k + 1) * 8], in_=rps[0:P, 0:8]), inc="v")
    rho8 = p._ln_keep["rho_bc"]
    mrho8 = p._ln_keep["mrho_bc"]
    for off, ln_ in tok_blocks(T):
        p.wait("tensor", "v")
        for k in range(KD):
            p.op("tensor", lambda e, k=k, off=off, ln_=ln_: e.matmul(
                out=lps[0:8, 0:ln_], lhsT=rwT[:, k * 8:(k + 1) * 8],
                rhs=xt[:, k * T + off:k * T + off + ln_],
                start=(k == 0), stop=(k == KD - 1)),
                inc=("t" if k == KD - 1 else None))
        p.wait("vector", "t")
        p.op("vector", lambda e, off=off, ln_=ln_: e.tensor_mul(
            out=logit[:, off:off + ln_], in0=lps[0:8, 0:ln_],
            in1=rho8[0:8, off:off + ln_]), inc=None)
        p.op("vector", lambda e, off=off, ln_=ln_: e.scalar_tensor_tensor(
            out=logit[:, off:off + ln_], in0=mrho8[0:8, off:off + ln_],
            scalar=rs8[0:8, 0:1], in1=logit[:, off:off + ln_],
            op0=ALU.mult, op1=ALU.add), inc="v")
        p.wait("scalar", "v")
        p.op("scalar", lambda e, off=off, ln_=ln_: e.activation(
            out=logit[:, off:off + ln_], in_=logit[:, off:off + ln_],
            func=AF.Sigmoid), inc="v")
        p.wait("tensor", "v")
    s_tm = p.sb("s_tm", [P, 8], F32)
    comb_tm = p.sb("comb_tm", [P, 8], F32)
    bias_bc8 = p.sb("bias_bc8", [P, 8], F32)
    sel_tm = p.sb("sel_tm", [P, 8], F32)
    m8r = p.sb("m8r", [P, 8], F32)
    oh1 = p.sb("oh1", [P, 8], F32)
    oh2 = p.sb("oh2", [P, 8], F32)
    scl = p.sb("scl", [P, 8], F32)
    emit_bcast_row(p, cm, rb8, bias_bc8, 8, rps)
    for c in range(16):
        p.wait("tensor", "v")
        p.op("tensor", lambda e, c=c: e.transpose(
            out=rps[0:P, 0:8], in_=logit[0:8, c * P:(c + 1) * P],
            identity=cm.ident_f[0:8, 0:8]), inc="t")
        p.wait("vector", "t")
        p.op("vector", lambda e: e.tensor_copy(out=s_tm[:], in_=rps[0:P, 0:8]), inc="v")
        p.wait("tensor", "v")
        p.wait("vector", "v")
        p.op("vector", lambda e: e.tensor_add(
            out=sel_tm[:], in0=s_tm[:], in1=bias_bc8[:]), inc="v")
        p.wait("vector", "v")
        p.op("vector", lambda e: e.max(out=m8r[:], in_=sel_tm[:]), inc="v")
        p.wait("vector", "v")
        p.op("vector", lambda e: e.tensor_scalar(
            out=oh1[:], in0=sel_tm[:], scalar1=m8r[:, 0:1], scalar2=None,
            op0=ALU.is_equal), inc=None)
        p.op("vector", lambda e: e.tensor_scalar(
            out=oh2[:], in0=sel_tm[:], scalar1=m8r[:, 1:2], scalar2=None,
            op0=ALU.is_equal), inc="v")
        p.wait("vector", "v")
        p.op("vector", lambda e: e.scalar_tensor_tensor(
            out=oh1[:], in0=oh1[:], scalar=1.0, in1=s_tm[:],
            op0=ALU.mult, op1=ALU.mult, accum_out=scl[:, 0:1]), inc="v")
        p.wait("vector", "v")
        p.op("vector", lambda e: e.scalar_tensor_tensor(
            out=oh2[:], in0=oh2[:], scalar=1.0, in1=s_tm[:],
            op0=ALU.mult, op1=ALU.mult, accum_out=scl[:, 1:2]), inc="v")
        p.wait("vector", "v")
        p.op("vector", lambda e: e.tensor_add(out=scl[:, 2:3], in0=scl[:, 0:1],
                                              in1=scl[:, 1:2]), inc="v")
        p.wait("vector", "v")
        p.op("vector", lambda e: e.tensor_scalar(
            out=scl[:, 3:4], in0=scl[:, 2:3], scalar1=1e-9, scalar2=None,
            op0=ALU.add), inc=None)
        p.op("vector", lambda e: e.tensor_scalar(
            out=scl[:, 5:6], in0=scl[:, 2:3], scalar1=1e-9, scalar2=None,
            op0=ALU.is_gt), inc="v")
        p.wait("vector", "v")
        p.op("vector", lambda e: e.reciprocal(out=scl[:, 4:5], in_=scl[:, 3:4]), inc="v")
        p.wait("vector", "v")
        p.op("vector", lambda e: e.tensor_add(out=comb_tm[:], in0=oh1[:], in1=oh2[:]), inc="v")
        p.wait("vector", "v")
        p.op("vector", lambda e: e.tensor_scalar(
            out=comb_tm[:], in0=comb_tm[:], scalar1=scl[:, 4:5],
            scalar2=scl[:, 5:6], op0=ALU.mult, op1=ALU.mult), inc="v")
        if dbg and c == 0:
            p.wait("sync", "v")
            for j, tt in enumerate([s_tm, sel_tm, m8r, oh1, scl, oh2, comb_tm, bias_bc8]):
                pass
            for j, tt in enumerate([s_tm, sel_tm, m8r, oh1, scl, oh2, comb_tm, bias_bc8]):
                p.dma("sync", ydbg[:, j * 8:(j + 1) * 8], tt[:, 0:8], inc="st")
            p.dma("sync", ydbg[0:1, 64:72], rb8[0:1, 0:8], inc="st")
            p.wait("vector", "st")
            p.wait("tensor", "st")
            p.wait("scalar", "st")
        p.wait("tensor", "v")
        p.op("tensor", lambda e, c=c: e.transpose(
            out=rps[0:8, 0:P], in_=comb_tm[:, 0:8], identity=cm.ident_f[:]), inc="t")
        p.wait("vector", "t")
        p.op("vector", lambda e, c=c: e.tensor_copy(
            out=comb_fm[:, c * P:(c + 1) * P], in_=rps[0:8, 0:P]), inc="v")
        p.wait("tensor", "v")
    p.free("scl", "oh2", "oh1", "m8r", "sel_tm", "bias_bc8", "comb_tm", "s_tm",
           "rs8", "logit", "rb8", "rwT", "rw")
    p.free(*ln_free)

    HS = 2048
    NH = HS // P
    wbuf = p.sb("wbuf", [P, 8 * HS], BF16)
    stg = p.sb("stg", [P, HS], F32)
    gsil_q = p.sb("gsil_q", [P, NH * 512], BF16)
    mprod_q = p.sb("mprod_q", [P, NH * 512], BF16)
    ps1 = p.psum("ps1", [P, 512], F32)
    pso = p.psum("pso", [P, 512], F32)

    def glu_pass(off, ln_, which):
        for m in range(NH):
            p.wait("tensor", "v")
            for k in range(KD):
                p.op("tensor", lambda e, m=m, k=k: e.matmul(
                    out=ps1[:, 0:ln_],
                    lhsT=wbuf[:, k * HS + m * P:k * HS + (m + 1) * P],
                    rhs=xnb[:, k * T + off:k * T + off + ln_],
                    start=(k == 0), stop=(k == KD - 1)),
                    inc=("t" if k == KD - 1 else None))
            if which == 1:
                p.wait("scalar", "t")
                p.op("scalar", lambda e, m=m: e.activation(
                    out=gsil_q[:, m * 512:m * 512 + ln_], in_=ps1[:, 0:ln_],
                    func=AF.Silu), inc="v")
            else:
                p.wait("vector", "t")
                p.op("vector", lambda e, m=m: e.tensor_mul(
                    out=mprod_q[:, m * 512:m * 512 + ln_], in0=ps1[:, 0:ln_],
                    in1=gsil_q[:, m * 512:m * 512 + ln_]), inc="v")

    p.fence_dma()
    for off, ln_ in tok_blocks(T, 512):
        p.wait("tensor", "v")
        emit_load_wbf(p, sw1d, D, HS, wbuf, stg)
        glu_pass(off, ln_, 1)
        p.wait("tensor", "v")
        emit_load_wbf(p, sw3d, D, HS, wbuf, stg)
        glu_pass(off, ln_, 3)
        p.wait("tensor", "v")
        emit_load_wbf(p, sw2d, HS, D, wbuf, stg)
        for mo in range(KD):
            p.wait("tensor", "v")
            for k in range(NH):
                p.op("tensor", lambda e, mo=mo, k=k, off=off, ln_=ln_: e.matmul(
                    out=pso[:, 0:ln_],
                    lhsT=wbuf[:, k * D + mo * P:k * D + (mo + 1) * P],
                    rhs=mprod_q[:, k * 512:k * 512 + ln_],
                    start=(k == 0), stop=(k == NH - 1)),
                    inc=("t" if k == NH - 1 else None))
            p.wait("vector", "t")
            p.op("vector", lambda e, mo=mo, off=off, ln_=ln_: e.tensor_add(
                out=xt[:, mo * T + off:mo * T + off + ln_],
                in0=xt[:, mo * T + off:mo * T + off + ln_],
                in1=pso[:, 0:ln_]), inc="v")

    if dbg:
        p.wait("sync", "v")
        p.dma("sync", ycomb, comb_fm[0:8, :], inc="st")
        for k in range(KD):
            p.dma("sync", yglu[k * P:(k + 1) * P, :], xt[:, k * T:(k + 1) * T], inc="st")

    comb_bc = p.sb("comb_bc", [P, T], F32)
    crow = p.sb("crow", [1, T], F32)
    cps = p.psum("cps", [1, 512], F32)
    EW2OFF = 8 * D
    for e_i in range(8):
        for off, ln_ in tok_blocks(T):
            p.wait("tensor", "v")
            p.op("tensor", lambda e, e_i=e_i, off=off, ln_=ln_: e.matmul(
                out=cps[0:1, 0:ln_], lhsT=eye8[0:8, e_i:e_i + 1],
                rhs=comb_fm[0:8, off:off + ln_], start=True, stop=True), inc="t")
            p.wait("vector", "t")
            p.op("vector", lambda e, off=off, ln_=ln_: e.tensor_copy(
                out=crow[:, off:off + ln_], in_=cps[0:1, 0:ln_]), inc="v")
            p.wait("tensor", "v")
        emit_bcast_row(p, cm, crow, comb_bc, T, rps)
        p.wait("tensor", "v")
        emit_load_wbf(p, ew1d[e_i * D:(e_i + 1) * D, :], D, D, wbuf, stg)
        emit_load_wbf(p, ew2d[e_i * D:(e_i + 1) * D, :], D, D, wbuf, stg, woff=EW2OFF)
        for off, ln_ in tok_blocks(T):
            for m in range(KD):
                p.wait("tensor", "v")
                for k in range(KD):
                    p.op("tensor", lambda e, m=m, k=k, off=off, ln_=ln_: e.matmul(
                        out=ps1[:, 0:ln_],
                        lhsT=wbuf[:, k * D + m * P:k * D + (m + 1) * P],
                        rhs=xnb[:, k * T + off:k * T + off + ln_],
                        start=(k == 0), stop=(k == KD - 1)),
                        inc=("t" if k == KD - 1 else None))
                p.wait("scalar", "t")
                p.op("scalar", lambda e, m=m, off=off, ln_=ln_: e.activation(
                    out=gsil_q[:, m * 512:m * 512 + ln_], in_=ps1[:, 0:ln_],
                    func=AF.Gelu), inc="s")
                p.wait("vector", "s")
                p.op("vector", lambda e, m=m, off=off, ln_=ln_: e.tensor_mul(
                    out=gsil_q[:, m * 512:m * 512 + ln_],
                    in0=gsil_q[:, m * 512:m * 512 + ln_],
                    in1=comb_bc[:, off:off + ln_]), inc="v")
            for mo in range(KD):
                p.wait("tensor", "v")
                for k in range(KD):
                    p.op("tensor", lambda e, mo=mo, k=k, off=off, ln_=ln_: e.matmul(
                        out=pso[:, 0:ln_],
                        lhsT=wbuf[:, EW2OFF + k * D + mo * P:EW2OFF + k * D + (mo + 1) * P],
                        rhs=gsil_q[:, k * 512:k * 512 + ln_],
                        start=(k == 0), stop=(k == KD - 1)),
                        inc=("t" if k == KD - 1 else None))
                p.wait("vector", "t")
                p.op("vector", lambda e, mo=mo, off=off, ln_=ln_: e.tensor_add(
                    out=xt[:, mo * T + off:mo * T + off + ln_],
                    in0=xt[:, mo * T + off:mo * T + off + ln_],
                    in1=pso[:, 0:ln_]), inc="v")

    p.wait("sync", "v")
    for k in range(KD):
        p.dma("sync", yd[k * P:(k + 1) * P, :], xt[:, k * T:(k + 1) * T], inc="st")
    p.wait("sync", "st")



_cache = {}


def _get_prog(name, builder):
    if name not in _cache:
        _cache[name] = builder()
    return _cache[name]


def _fm_slices(x):
    out = []
    for c in range(NCORES):
        b, s0 = c // 4, (c % 4) * T
        out.append(np.ascontiguousarray(x[b, s0:s0 + T, :].T))
    return out


def _fm_slices_ext(x):
    out = []
    for c in range(NCORES):
        b, s0 = c // 4, (c % 4) * T
        lo, hi = s0 - HALO_L, s0 + T + HALO_R + 1
        ext = np.zeros((D, TE), np.float32)
        src_lo, src_hi = max(lo, 0), min(hi, SEQ)
        ext[:, src_lo - lo: src_hi - lo] = x[b, src_lo:src_hi, :].T
        out.append(ext)
    return out


def _unshard(ys):
    x = np.empty((2, SEQ, D), np.float32)
    for c in range(NCORES):
        b, s0 = c // 4, (c % 4) * T
        x[b, s0:s0 + T, :] = ys[c].T
    return x


def run_l0_attn_a(x, inp):
    pr = _get_prog("l0aa", build_l0_attn_a)
    xs = _fm_slices_ext(x)
    base = {
        "Ain": np.asarray(inp["L0_attn_a_A"], np.float32).reshape(1, 64),
        "Bin": np.asarray(inp["L0_attn_a_B"], np.float32),
        "Cin": np.asarray(inp["L0_attn_a_C"], np.float32),
        "Dvec": np.asarray(inp["L0_attn_a_D"], np.float32).reshape(D, 1),
        "log_dt": np.asarray(inp["L0_attn_a_log_dt"], np.float32).reshape(D, 1),
        **Common.host_inputs(),
    }
    in_maps = [{**base, "x": xs[c]} for c in range(NCORES)]
    res = run_bass_kernel_spmd(pr.nc, in_maps, core_ids=list(range(NCORES)))
    return _unshard([res.results[c]["y"] for c in range(NCORES)])


def run_l1_ffn(x, inp):
    pr = _get_prog("l1ffn", build_l1_ffn)
    xs = _fm_slices(x)
    base = {
        "router_w": np.asarray(inp["L1_ffn_router_w"], np.float32),
        "router_bias": np.asarray(inp["L1_ffn_router_bias"], np.float32).reshape(1, 8),
        "sw1": np.asarray(inp["L1_ffn_sw1"], np.float32),
        "sw3": np.asarray(inp["L1_ffn_sw3"], np.float32),
        "sw2": np.asarray(inp["L1_ffn_sw2"], np.float32),
        "ew1": np.asarray(inp["L1_ffn_ew1"], np.float32).reshape(8 * D, D),
        "ew2": np.asarray(inp["L1_ffn_ew2"], np.float32).reshape(8 * D, D),
        "eye8": np.eye(8, dtype=np.float32),
        **Common.host_inputs(),
    }
    in_maps = [{**base, "x": xs[c]} for c in range(NCORES)]
    res = run_bass_kernel_spmd(pr.nc, in_maps, core_ids=list(range(NCORES)))
    return _unshard([res.results[c]["y"] for c in range(NCORES)])


def kernel(**inputs):
    return kernel_fused(**inputs)



def build_l0_ffn():
    p = Prog("l0ffn")
    cm = Common(p)
    yd = p.dram("y", [D, T], F32, kind="ExternalOutput").ap()
    emit_l0ffn(p, cm, yd)
    p.finish()
    return p


def emit_l0ffn(p, cm, yd, xin=None):
    xd = (xin if xin is not None
          else p.dram("x", [D, T], F32, kind="ExternalInput").ap())
    Wgd = p.dram("Wg", [D, 4096], F32, kind="ExternalInput").ap()
    Wud = p.dram("Wu", [D, 4096], F32, kind="ExternalInput").ap()
    Wdd = p.dram("Wd", [4096, D], F32, kind="ExternalInput").ap()
    bgd = p.dram("bg", [4096, 1], F32, kind="ExternalInput").ap()
    bud = p.dram("bu", [4096, 1], F32, kind="ExternalInput").ap()
    bdd = p.dram("bd", [D, 1], F32, kind="ExternalInput").ap()
    Wqd = p.dram("Wq", [D, 512], F32, kind="ExternalInput").ap()
    keysTd = p.dram("keysT", [512, 64], F32, kind="ExternalInput").ap()
    downd = p.dram("down_embed", [4096, D], F32, kind="ExternalInput").ap()
    upd = p.dram("up_embed", [4096, D], F32, kind="ExternalInput").ap()

    xt = p.sb("xt", [P, KD * T], F32)
    for k in range(KD):
        p.dma("sync", xt[:, k * T:(k + 1) * T], xd[k * P:(k + 1) * P, :])
    p.wait("vector", "dma")
    p.wait("scalar", "dma")
    p.wait("tensor", "dma")

    xnb, ln_free = emit_ln_xnb(p, cm, xt, T, "f0")
    p.free(*ln_free)

    HH = 2048
    NHH = HH // P
    wbuf = p.sb("wbuf", [P, 8 * HH], BF16)
    stg = p.sb("stg", [P, HH], F32)
    gsil_q = p.sb("gsil_q", [P, NHH * 512], BF16)
    mprod_q = p.sb("mprod_q", [P, NHH * 512], BF16)
    bgc = p.sb("bgc", [P, 32], F32)
    buc = p.sb("buc", [P, 32], F32)
    bdc = p.sb("bdc", [P, KD], F32)
    ps1 = p.psum("ps1", [P, 512], F32)
    pso = p.psum("pso", [P, 512], F32)
    for m in range(32):
        p.dma("sync", bgc[:, m:m + 1], bgd[m * P:(m + 1) * P, :])
        p.dma("sync", buc[:, m:m + 1], bud[m * P:(m + 1) * P, :])
    for k in range(KD):
        p.dma("sync", bdc[:, k:k + 1], bdd[k * P:(k + 1) * P, :])

    p.fence_dma()
    for off, ln_ in tok_blocks(T, 512):
        for half in range(2):
            hof = half * 16
            p.wait("tensor", "v")
            emit_load_wbf(p, Wgd[:, half * HH:(half + 1) * HH], D, HH, wbuf, stg)
            for m in range(NHH):
                p.wait("tensor", "v")
                for k in range(KD):
                    p.op("tensor", lambda e, m=m, k=k, off=off, ln_=ln_: e.matmul(
                        out=ps1[:, 0:ln_],
                        lhsT=wbuf[:, k * HH + m * P:k * HH + (m + 1) * P],
                        rhs=xnb[:, k * T + off:k * T + off + ln_],
                        start=(k == 0), stop=(k == KD - 1)),
                        inc=("t" if k == KD - 1 else None))
                p.wait("scalar", "t")
                p.op("scalar", lambda e, m=m, hof=hof, ln_=ln_: e.activation(
                    out=gsil_q[:, m * 512:m * 512 + ln_], in_=ps1[:, 0:ln_],
                    func=AF.Silu, bias=bgc[:, hof + m:hof + m + 1]), inc="v")
            p.wait("tensor", "v")
            emit_load_wbf(p, Wud[:, half * HH:(half + 1) * HH], D, HH, wbuf, stg)
            for m in range(NHH):
                p.wait("tensor", "v")
                for k in range(KD):
                    p.op("tensor", lambda e, m=m, k=k, off=off, ln_=ln_: e.matmul(
                        out=ps1[:, 0:ln_],
                        lhsT=wbuf[:, k * HH + m * P:k * HH + (m + 1) * P],
                        rhs=xnb[:, k * T + off:k * T + off + ln_],
                        start=(k == 0), stop=(k == KD - 1)),
                        inc=("t" if k == KD - 1 else None))
                p.wait("vector", "t")
                p.op("vector", lambda e, m=m, hof=hof, ln_=ln_: e.scalar_tensor_tensor(
                    out=mprod_q[:, m * 512:m * 512 + ln_], in0=ps1[:, 0:ln_],
                    scalar=buc[:, hof + m:hof + m + 1],
                    in1=gsil_q[:, m * 512:m * 512 + ln_],
                    op0=ALU.add, op1=ALU.mult), inc="v")
            p.wait("tensor", "v")
            emit_load_wbf(p, Wdd[half * HH:(half + 1) * HH, :], HH, D, wbuf, stg)
            for mo in range(KD):
                p.wait("tensor", "v")
                for k in range(NHH):
                    p.op("tensor", lambda e, mo=mo, k=k, off=off, ln_=ln_: e.matmul(
                        out=pso[:, 0:ln_],
                        lhsT=wbuf[:, k * D + mo * P:k * D + (mo + 1) * P],
                        rhs=mprod_q[:, k * 512:k * 512 + ln_],
                        start=(k == 0), stop=(k == NHH - 1)),
                        inc=("t" if k == NHH - 1 else None))
                p.wait("vector", "t")
                if half == 0:
                    p.op("vector", lambda e, mo=mo, off=off, ln_=ln_: e.tensor_add(
                        out=xt[:, mo * T + off:mo * T + off + ln_],
                        in0=xt[:, mo * T + off:mo * T + off + ln_],
                        in1=pso[:, 0:ln_]), inc="v")
                else:
                    p.op("vector", lambda e, mo=mo, off=off, ln_=ln_: e.scalar_tensor_tensor(
                        out=xt[:, mo * T + off:mo * T + off + ln_], in0=pso[:, 0:ln_],
                        scalar=bdc[:, mo:mo + 1],
                        in1=xt[:, mo * T + off:mo * T + off + ln_],
                        op0=ALU.add, op1=ALU.add), inc="v")

    p.free("bdc", "buc", "bgc", "mprod_q", "gsil_q", "stg", "wbuf")

    ps_t0 = p.psum("ps_t0", [P, 128], BF16)
    ps_t64 = p.psum("ps_t64", [P, 128], BF16)
    stg2 = p.sb("stg2", [P, 2048], F32)
    Mbuf = p.sb("Mbuf", [P, KD * 512], BF16)
    SXM = p.sb("SXM", [P, 16 * 4 * 64], F32)
    SYM = p.sb("SYM", [P, 16 * 4 * 64], F32)
    THC = p.sb("THC", [P, 16 * 4], F32)
    NCC = p.sb("NCC", [P, 16 * 4], F32)
    p.fence_dma()

    keysT2 = p.sb("keysT2", [64, 8 * 64], BF16)
    kstg = p.sb("kstg", [64, 64], F32)
    wqb = p.sb("wqb", [P, KD * 512], BF16)
    wqTb = p.sb("wqTb", [64, P], BF16)
    mps = p.psum("mps", [P, 64], F32)
    for j in range(8):
        p.wait("sync", "v")
        p.dma("sync", kstg[:, :], keysTd[j * 64:(j + 1) * 64, :], inc="dma")
        p.wait("vector", "dma")
        p.op("vector", lambda e, j=j: e.tensor_copy(
            out=keysT2[:, j * 64:(j + 1) * 64], in_=kstg[:, :]), inc="v")
    emit_load_wbf(p, Wqd, D, 512, wqb, stg2)
    for j in range(8):
        for kc in range(KD):
            p.wait("tensor", "v")
            p.op("tensor", lambda e, j=j, kc=kc: e.transpose(
                out=ps_t0[0:64, 0:P],
                in_=wqb[:, kc * 512 + j * 64: kc * 512 + (j + 1) * 64],
                identity=cm.ident_bf[:]), inc="t")
            p.wait("vector", "t")
            p.op("vector", lambda e: e.tensor_copy(out=wqTb[:, :], in_=ps_t0[0:64, 0:P]), inc="v")
            p.wait("tensor", "v")
            p.op("tensor", lambda e, j=j: e.matmul(
                out=mps[:, 0:64], lhsT=wqTb[0:64, :], rhs=keysT2[0:64, j * 64:(j + 1) * 64],
                start=True, stop=True), inc="t")
            p.wait("vector", "t")
            p.op("vector", lambda e, j=j, kc=kc: e.tensor_copy(
                out=Mbuf[:, kc * 512 + j * 64: kc * 512 + (j + 1) * 64],
                in_=mps[:, 0:64]), inc="v")
            p.wait("tensor", "v")
    p.free("mps")
    p.free("wqTb", "wqb", "kstg", "keysT2")
    p.fence_dma()

    simf = p.sb("simf", [P, 4 * T], BF16)
    for mc in range(4):
        for off, ln_ in tok_blocks(T, 512):
            p.wait("tensor", "v")
            for k in range(KD):
                p.op("tensor", lambda e, mc=mc, k=k, off=off, ln_=ln_: e.matmul(
                    out=ps1[:, 0:ln_],
                    lhsT=Mbuf[:, k * 512 + mc * P:k * 512 + (mc + 1) * P],
                    rhs=xnb[:, k * T + off:k * T + off + ln_],
                    start=(k == 0), stop=(k == KD - 1)),
                    inc=("t" if k == KD - 1 else None))
            p.wait("vector", "t")
            p.op("vector", lambda e, mc=mc, off=off, ln_=ln_: e.tensor_copy(
                out=simf[:, mc * T + off:mc * T + off + ln_], in_=ps1[:, 0:ln_]), inc="v")

    sim_x = p.sb("sim_x", [P, 64], F32)
    sim_y = p.sb("sim_y", [P, 64], F32)
    sxv = p.sb("sxv", [P, 8], F32)
    syv = p.sb("syv", [P, 8], F32)
    cv8 = p.sb("cv8", [P, 8], F32)
    e8t = p.sb("e8t", [P, 8], F32)
    msk1 = p.sb("msk1", [P, 64], F32)
    mm1 = p.sb("mm1", [P, 64], F32)
    zcol = p.sb("zcol", [P, 4], F32)
    BIGC = 1.0e30

    for c in range(16):
        for h in range(4):
            for pp, dst in ((0, sim_x), (1, sim_y)):
                mc = (pp * 4 + h) // 2
                sub = ((pp * 4 + h) % 2) * 64
                tpx = ps_t0 if sub == 0 else ps_t64
                p.wait("tensor", "v")
                p.op("tensor", lambda e, mc=mc, sub=sub, c=c, tpx=tpx: e.transpose(
                    out=tpx[0:P, 0:64],
                    in_=simf[sub:sub + 64, mc * T + c * P:mc * T + (c + 1) * P],
                    identity=cm.ident_bf[sub:sub + 64, sub:sub + 64]), inc="t")
                p.wait("vector", "t")
                p.op("vector", lambda e, dst=dst, tpx=tpx: e.tensor_copy(
                    out=dst[:], in_=tpx[0:P, 0:64]), inc="v")
                p.wait("tensor", "v")
            p.wait("vector", "v")
            p.op("vector", lambda e: e.max(out=sxv[:], in_=sim_x[:]), inc="v")
            p.wait("vector", "v")
            p.op("vector", lambda e: e.max(out=syv[:], in_=sim_y[:]), inc="v")
            p.wait("vector", "v")
            for (sv, st_, dst) in ((sxv, sim_x, SXM), (syv, sim_y, SYM)):
                p.op("vector", lambda e, sv=sv, st_=st_: e.tensor_scalar(
                    out=msk1[:], in0=st_[:], scalar1=sv[:, 7:8], scalar2=None,
                    op0=ALU.is_ge), inc="v")
                p.wait("vector", "v")
                p.op("vector", lambda e: e.tensor_scalar(
                    out=mm1[:], in0=msk1[:], scalar1=-1.0, scalar2=BIGC,
                    op0=ALU.add, op1=ALU.mult), inc="v")
                p.wait("vector", "v")
                p.op("vector", lambda e, st_=st_, dst=dst, c=c, h=h: e.tensor_add(
                    out=dst[:, (c * 4 + h) * 64:(c * 4 + h + 1) * 64],
                    in0=st_[:], in1=mm1[:]), inc="v")
                p.wait("vector", "v")
            p.op("vector", lambda e: e.tensor_tensor(
                out=msk1[:].rearrange("p (a b) -> p a b", a=8),
                in0=sxv[:].rearrange("p (a u) -> p a u", u=1).broadcast_to([P, 8, 8]),
                in1=syv[:].rearrange("p (u b) -> p u b", u=1).broadcast_to([P, 8, 8]),
                op=ALU.add), inc="v")
            p.wait("vector", "v")
            p.op("vector", lambda e: e.max(out=cv8[:], in_=msk1[:, 0:64]), inc="v")
            p.wait("vector", "v")
            p.op("vector", lambda e, c=c, h=h: e.tensor_copy(
                out=THC[:, c * 4 + h:c * 4 + h + 1], in_=cv8[:, 7:8]), inc=None)
            p.op("vector", lambda e: e.tensor_scalar(
                out=zcol[:, 0:1], in0=cv8[:, 0:1], scalar1=-1.0, scalar2=None,
                op0=ALU.mult), inc="v")
            p.wait("vector", "v")
            p.wait("scalar", "v")
            p.op("scalar", lambda e: e.activation(
                out=e8t[:], in_=cv8[:], func=AF.Exp, bias=zcol[:, 0:1]), inc="s")
            p.wait("vector", "s")
            p.op("vector", lambda e: e.tensor_reduce(
                out=zcol[:, 1:2], in_=e8t[:], axis=mybir.AxisListType.X,
                op=ALU.add), inc="v")
            p.wait("vector", "v")
            p.wait("scalar", "v")
            p.op("scalar", lambda e: e.activation(
                out=zcol[:, 2:3], in_=zcol[:, 1:2], func=AF.Ln), inc="s")
            p.wait("vector", "s")
            p.op("vector", lambda e, c=c, h=h: e.tensor_sub(
                out=NCC[:, c * 4 + h:c * 4 + h + 1], in0=zcol[:, 0:1],
                in1=zcol[:, 2:3]), inc="v")
            p.wait("vector", "v")
    p.free("zcol", "mm1", "msk1", "e8t", "cv8", "syv", "sxv", "sim_y", "sim_x",
           "simf")
    p.fence_dma()

    EQ = 1024
    NEC = EQ // P
    dwnT = p.sb("dwnT", [P, KD * EQ], BF16)
    upb = p.sb("upb", [P, NEC * D], BF16)
    sgrid = p.sb("sgrid", [P, EQ], F32)
    ugrid = p.sb("ugrid", [P, EQ], BF16)
    ggrid = p.sb("ggrid", [P, EQ], BF16)
    gtmp = p.sb("gtmp", [P, EQ], BF16)
    sdots = p.sb("sdots", [P, EQ], BF16)
    ctrT = p.sb("ctrT", [P, NEC * P], BF16)
    dstg = p.sb("dstg", [P, D], F32)
    dps = p.psum("dps", [P, 512], F32)
    ups = p.psum("ups", [P, 128], F32)

    for q in range(4):
        for ec in range(NEC):
            p.wait("sync", "v")
            p.dma("sync", dstg[:, 0:D], downd[(q * NEC + ec) * P:(q * NEC + ec + 1) * P, :], inc="dma")
            p.wait("vector", "dma")
            p.op("vector", lambda e: e.tensor_copy(out=gtmp[:, 0:D], in_=dstg[:, 0:D]), inc="v")
            p.wait("vector", "v")
            for dc in range(KD):
                p.wait("tensor", "v")
                p.op("tensor", lambda e, dc=dc: e.transpose(
                    out=ps_t0[:, 0:P], in_=gtmp[:, dc * P:(dc + 1) * P],
                    identity=cm.ident_bf[:]), inc="t")
                p.wait("vector", "t")
                p.op("vector", lambda e, dc=dc, ec=ec: e.tensor_copy(
                    out=dwnT[:, dc * EQ + ec * P: dc * EQ + (ec + 1) * P],
                    in_=ps_t0[:, 0:P]), inc="v")
                p.wait("tensor", "v")
        emit_load_wbf(p, upd[q * EQ:(q + 1) * EQ, :], EQ, D, upb, dstg)

        for c in range(16):
            for eb in range(2):
                p.wait("tensor", "v")
                for k in range(KD):
                    p.op("tensor", lambda e, c=c, eb=eb, k=k: e.matmul(
                        out=dps[:, 0:512],
                        lhsT=xnb[:, k * T + c * P:k * T + (c + 1) * P],
                        rhs=dwnT[:, k * EQ + eb * 512:k * EQ + (eb + 1) * 512],
                        start=(k == 0), stop=(k == KD - 1)),
                        inc=("t" if k == KD - 1 else None))
                p.wait("scalar", "t")
                p.op("scalar", lambda e, eb=eb: e.activation(
                    out=sdots[:, eb * 512:(eb + 1) * 512], in_=dps[:, 0:512],
                    func=AF.Silu), inc="v")
            for h in range(4):
                cx = (c * 4 + h)
                p.wait("vector", "v")
                p.op("vector", lambda e, cx=cx, q=q: e.tensor_tensor(
                    out=sgrid[:].rearrange("p (a b) -> p a b", a=16),
                    in0=SXM[:, cx * 64 + q * 16: cx * 64 + (q + 1) * 16]
                        .rearrange("p (a u) -> p a u", u=1).broadcast_to([P, 16, 64]),
                    in1=SYM[:, cx * 64:(cx + 1) * 64]
                        .rearrange("p (u b) -> p u b", u=1).broadcast_to([P, 16, 64]),
                    op=ALU.add), inc="v")
                p.wait("scalar", "v")
                p.op("scalar", lambda e, cx=cx: e.activation(
                    out=ugrid[:], in_=sgrid[:], func=AF.Exp,
                    bias=NCC[:, cx:cx + 1]), inc="s")
                p.op("vector", lambda e, cx=cx: e.tensor_scalar(
                    out=gtmp[:], in0=sgrid[:], scalar1=THC[:, cx:cx + 1],
                    scalar2=None, op0=ALU.is_ge), inc="v")
                p.wait("vector", "s")
                p.wait("vector", "v")
                if h == 0:
                    p.op("vector", lambda e: e.tensor_mul(
                        out=ggrid[:], in0=ugrid[:], in1=gtmp[:]), inc="v")
                else:
                    p.op("vector", lambda e: e.tensor_mul(
                        out=gtmp[:], in0=ugrid[:], in1=gtmp[:]), inc="v")
                    p.wait("vector", "v")
                    p.op("vector", lambda e: e.tensor_add(
                        out=ggrid[:], in0=ggrid[:], in1=gtmp[:]), inc="v")
                p.wait("vector", "v")
            p.op("vector", lambda e: e.tensor_mul(
                out=ggrid[:], in0=ggrid[:], in1=sdots[:]), inc="v")
            for ec in range(NEC):
                p.wait("tensor", "v")
                p.op("tensor", lambda e, ec=ec: e.transpose(
                    out=ps_t0[:, 0:P], in_=ggrid[:, ec * P:(ec + 1) * P],
                    identity=cm.ident_bf[:]), inc="t")
                p.wait("vector", "t")
                p.op("vector", lambda e, ec=ec: e.tensor_copy(
                    out=ctrT[:, ec * P:(ec + 1) * P], in_=ps_t0[:, 0:P]), inc="v")
                p.wait("tensor", "v")
            for dc in range(KD):
                p.wait("tensor", "v")
                for ec in range(NEC):
                    p.op("tensor", lambda e, dc=dc, ec=ec: e.matmul(
                        out=ups[:, 0:P],
                        lhsT=upb[:, ec * D + dc * P:ec * D + (dc + 1) * P],
                        rhs=ctrT[:, ec * P:(ec + 1) * P],
                        start=(ec == 0), stop=(ec == NEC - 1)),
                        inc=("t" if ec == NEC - 1 else None))
                p.wait("vector", "t")
                p.op("vector", lambda e, dc=dc, c=c: e.tensor_add(
                    out=xt[:, dc * T + c * P:dc * T + (c + 1) * P],
                    in0=xt[:, dc * T + c * P:dc * T + (c + 1) * P],
                    in1=ups[:, 0:P]), inc="v")

    p.wait("sync", "v")
    for k in range(KD):
        p.dma("sync", yd[k * P:(k + 1) * P, :], xt[:, k * T:(k + 1) * T], inc="st")
    p.wait("sync", "st")


def run_l0_ffn(x, inp):
    pr = _get_prog("l0ffn", build_l0_ffn)
    xs = _fm_slices(x)
    keys = np.asarray(inp["L0_ffn_keys"], np.float32)
    keysT = np.ascontiguousarray(keys.transpose(2, 0, 3, 1).reshape(512, 64))
    base = {
        "Wg": np.asarray(inp["L0_ffn_Wg"], np.float32),
        "Wu": np.asarray(inp["L0_ffn_Wu"], np.float32),
        "Wd": np.asarray(inp["L0_ffn_Wd"], np.float32),
        "bg": np.asarray(inp["L0_ffn_bg"], np.float32).reshape(4096, 1),
        "bu": np.asarray(inp["L0_ffn_bu"], np.float32).reshape(4096, 1),
        "bd": np.asarray(inp["L0_ffn_bd"], np.float32).reshape(D, 1),
        "Wq": np.asarray(inp["L0_ffn_Wq"], np.float32),
        "keysT": keysT,
        "down_embed": np.asarray(inp["L0_ffn_down_embed"], np.float32),
        "up_embed": np.asarray(inp["L0_ffn_up_embed"], np.float32),
        **Common.host_inputs(),
    }
    in_maps = [{**base, "x": xs[c]} for c in range(NCORES)]
    res = run_bass_kernel_spmd(pr.nc, in_maps, core_ids=list(range(NCORES)))
    return _unshard([res.results[c]["y"] for c in range(NCORES)])



def emit_allgather(p, sb_in, nfree, tag, npart=P, dtype=F32, readback=True):
    din = p.dram(f"agi_{tag}", [npart, nfree], dtype)
    dout = p.dram(f"ago_{tag}", [8 * npart, nfree], dtype, shared=True)
    p.wait("gpsimd", "v")
    p.wait("gpsimd", "s")
    p.dma("gpsimd", din.ap(), sb_in[0:npart, 0:nfree], inc="ag")
    p.wait("gpsimd", "ag")
    p.op("gpsimd", lambda e: e.collective_compute(
        "AllGather", ALU.bypass, replica_groups=[list(range(8))],
        ins=[din.ap()], outs=[dout.ap()]), inc="cc")
    p.wait("gpsimd", "cc")
    gout = None
    if readback:
        gout = p.sb(f"agg_{tag}", [npart, 8 * nfree], dtype)
        for r in range(8):
            p.dma("gpsimd", gout[0:npart, r * nfree:(r + 1) * nfree],
                  dout.ap()[r * npart:(r + 1) * npart, :], inc="ag")
        p.wait("vector", "ag")
        p.wait("tensor", "ag")
    return gout, dout


TEA = T + 3


def build_l1_attn_a():
    p = Prog("l1aa")
    cm = Common(p)
    yd = p.dram("y", [D, T], F32, kind="ExternalOutput").ap()
    emit_l1aa(p, cm, yd)
    p.finish()
    return p


def emit_l1aa(p, cm, yd, xin=None):
    xd = (None if xin is not None
          else p.dram("x", [D, TEA], F32, kind="ExternalInput").ap())
    cwd = p.dram("conv_w", [D, 4], F32, kind="ExternalInput").ap()
    cbd = p.dram("conv_b", [D, 1], F32, kind="ExternalInput").ap()
    wrd = p.dram("Wr_w", [D, D], F32, kind="ExternalInput").ap()
    wrbd = p.dram("Wr_b", [D, 1], F32, kind="ExternalInput").ap()
    wid = p.dram("Wi_w", [D, D], F32, kind="ExternalInput").ap()
    wibd = p.dram("Wi_b", [D, 1], F32, kind="ExternalInput").ap()
    lad = p.dram("log_a", [D, 1], F32, kind="ExternalInput").ap()
    owd = p.dram("out_w", [D, D], F32, kind="ExternalInput").ap()
    nwd = p.dram("norm_w", [D, 1], F32, kind="ExternalInput").ap()
    pmd = p.dram("prevmask", [P, 8], F32, kind="ExternalInput").ap()
    rspill = p.dram("rspill", [D, T], BF16)
    uspill = p.dram("uspill", [D, T], BF16)
    hspill = p.dram("hspill", [D, T], BF16)

    xt = p.sb("xt", [P, KD * TEA], F32)
    if xin is None:
        for k in range(KD):
            p.dma("sync", xt[:, k * TEA:(k + 1) * TEA], xd[k * P:(k + 1) * P, :])
    else:
        for k in range(KD):
            p.dma("sync", xt[:, k * TEA + 3:(k + 1) * TEA],
                  xin[k * P:(k + 1) * P, :])
        pm3d = p.dram("pm3", [P, 8], F32, kind="ExternalInput").ap()
        pm3 = p.sb("hx_pm3", [P, 8], F32)
        bnd = p.sb("hx_bnd", [P, KD * 3], F32)
        hstg = p.sb("hx_stg", [P, KD * 3], F32)
        halo = p.sb("hx_halo", [P, KD * 3], F32)
        p.dma("sync", pm3[:, :], pm3d)
        for k in range(KD):
            p.dma("sync", bnd[:, k * 3:(k + 1) * 3],
                  xin[k * P:(k + 1) * P, T - 3:T])
        p.wait("gpsimd", "dma")
        _, dH = emit_allgather(p, bnd, KD * 3, "hx", npart=P, readback=False)
        p.op("vector", lambda e: e.memset(halo[:], 0.0), inc="v")
        p.wait("vector", "v")
        for r in range(8):
            p.wait("sync", "v")
            p.wait("sync", "cc")
            p.dma("sync", hstg[:, :], dH.ap()[r * P:(r + 1) * P, :], inc="dma")
            p.wait("vector", "dma")
            p.op("vector", lambda e, r=r: e.scalar_tensor_tensor(
                out=halo[:], in0=hstg[:], scalar=pm3[:, r:r + 1], in1=halo[:],
                op0=ALU.mult, op1=ALU.add), inc="v")
        for k in range(KD):
            p.op("vector", lambda e, k=k: e.tensor_copy(
                out=xt[:, k * TEA:k * TEA + 3], in_=halo[:, k * 3:(k + 1) * 3]),
                inc="v")
        p.free("hx_halo", "hx_stg", "hx_bnd", "hx_pm3")
    cwc = p.sb("cwc", [P, KD * 4], F32)
    cbc = p.sb("cbc", [P, KD], F32)
    rbc = p.sb("rbc", [P, KD], F32)
    ibc = p.sb("ibc", [P, KD], F32)
    lac = p.sb("lac", [P, KD], F32)
    nwc = p.sb("nwc", [P, KD], F32)
    pmc = p.sb("pmc", [P, 8], F32)
    ones_T = p.sb("ones_T", [P, T], F32)
    for k in range(KD):
        p.dma("sync", cwc[:, k * 4:(k + 1) * 4], cwd[k * P:(k + 1) * P, :])
        p.dma("sync", cbc[:, k:k + 1], cbd[k * P:(k + 1) * P, :])
        p.dma("sync", rbc[:, k:k + 1], wrbd[k * P:(k + 1) * P, :])
        p.dma("sync", ibc[:, k:k + 1], wibd[k * P:(k + 1) * P, :])
        p.dma("sync", lac[:, k:k + 1], lad[k * P:(k + 1) * P, :])
        p.dma("sync", nwc[:, k:k + 1], nwd[k * P:(k + 1) * P, :])
    p.dma("sync", pmc[:, :], pmd)
    p.op("vector", lambda e: e.memset(ones_T[:], 1.0), inc="v")
    p.wait("vector", "dma")
    p.wait("vector", "v")
    p.wait("scalar", "dma")
    p.wait("tensor", "dma")
    p.wait("scalar", "v")
    p.op("scalar", lambda e: e.activation(out=lac[:], in_=lac[:], func=AF.Sigmoid), inc="s")
    p.wait("scalar", "s")
    p.op("scalar", lambda e: e.activation(out=lac[:], in_=lac[:], func=AF.Ln), inc="s")
    p.wait("vector", "s")
    p.op("vector", lambda e: e.tensor_scalar(out=lac[:], in0=lac[:], scalar1=8.0,
                                             scalar2=None, op0=ALU.mult), inc="v")

    xcb = p.sb("xcb", [P, KD * T], BF16)
    rho_bc, mrho_bc, ln_names = emit_ln(p, cm, xt, TEA, "a1")

    un1 = p.sb("un1", [P, TEA], F32)
    sc1 = p.sb("sc1", [P, T], F32)
    for k in range(KD):
        p.wait("vector", "v")
        p.wait("vector", "s")
        p.op("vector", lambda e, k=k: e.scalar_tensor_tensor(
            out=un1[:], in0=xt[:, k * TEA:(k + 1) * TEA],
            scalar=1.0, in1=rho_bc[:], op0=ALU.mult, op1=ALU.mult), inc=None)
        p.op("vector", lambda e, k=k: e.tensor_add(
            out=un1[:], in0=un1[:], in1=mrho_bc[:]), inc=None)
        p.op("vector", lambda e, k=k: e.tensor_scalar(
            out=sc1[:], in0=un1[:, 0:T], scalar1=cwc[:, k * 4:k * 4 + 1],
            scalar2=cbc[:, k:k + 1], op0=ALU.mult, op1=ALU.add), inc=None)
        for kk in range(1, 4):
            p.op("vector", lambda e, k=k, kk=kk: e.scalar_tensor_tensor(
                out=sc1[:], in0=un1[:, kk:kk + T],
                scalar=cwc[:, k * 4 + kk:k * 4 + kk + 1],
                in1=sc1[:], op0=ALU.mult, op1=ALU.add), inc=None)
        p.op("vector", lambda e, k=k: e.tensor_copy(
            out=xcb[:, k * T:(k + 1) * T], in_=sc1[:]), inc="v")
    p.free("sc1", "un1")
    p.free(*ln_names)
    p.free("ln_mrho_a1", "ln_rho_a1")
    p.fence_dma()

    wstg = p.sb("wstg", [P, D], F32)
    wrT = p.sb("wrT", [P, KD * D], BF16)
    wiT = p.sb("wiT", [P, KD * D], BF16)
    wtb = p.sb("wtb", [P, D], BF16)
    wtp = p.psum("wtp", [P, 128], BF16)
    for wdram, wdst in ((wrd, wrT), (wid, wiT)):
        for m in range(KD):
            p.wait("sync", "v")
            p.dma("sync", wstg[:, 0:D], wdram[m * P:(m + 1) * P, :], inc="dma")
            p.wait("vector", "dma")
            p.op("vector", lambda e: e.tensor_copy(out=wtb[:, 0:D], in_=wstg[:, 0:D]), inc="v")
            p.wait("vector", "v")
            for k in range(KD):
                p.wait("tensor", "v")
                p.op("tensor", lambda e, k=k: e.transpose(
                    out=wtp[:, 0:P], in_=wtb[:, k * P:(k + 1) * P],
                    identity=cm.ident_bf[:]), inc="t")
                p.wait("vector", "t")
                p.op("vector", lambda e, wdst=wdst, m=m, k=k: e.tensor_copy(
                    out=wdst[:, k * D + m * P:k * D + (m + 1) * P],
                    in_=wtp[:, 0:P]), inc="v")
                p.wait("tensor", "v")

    ps1 = p.psum("ps1", [P, 512], F32)
    lsum = p.sb("lsum", [P, KD], F32)
    rb4 = p.sb("rb4", [P, T], BF16)
    sca = p.sb("sca", [P, T], F32)
    scb = p.sb("scb", [P, T], F32)
    scc = p.sb("scc", [P, T], F32)
    LN7 = float(np.log(1e-7))
    for m in range(KD):
        p.wait("scalar", "rs")
        for off, ln_ in tok_blocks(T, 512):
            p.wait("tensor", "v")
            for k in range(KD):
                p.op("tensor", lambda e, m=m, k=k, off=off, ln_=ln_: e.matmul(
                    out=ps1[:, 0:ln_],
                    lhsT=wrT[:, k * D + m * P:k * D + (m + 1) * P],
                    rhs=xcb[:, k * T + off:k * T + off + ln_],
                    start=(k == 0), stop=(k == KD - 1)),
                    inc=("t" if k == KD - 1 else None))
            p.wait("scalar", "t")
            p.op("scalar", lambda e, m=m, off=off, ln_=ln_: e.activation(
                out=rb4[:, off:off + ln_], in_=ps1[:, 0:ln_],
                func=AF.Sigmoid, bias=rbc[:, m:m + 1]), inc="v")
        p.wait("sync", "v")
        p.dma("sync", rspill[m * P:(m + 1) * P, :], rb4[:, :], inc="rs")
        p.wait("vector", "v")
        p.op("vector", lambda e, m=m: e.tensor_scalar(
            out=sca[:], in0=rb4[:, :], scalar1=lac[:, m:m + 1],
            scalar2=LN7, op0=ALU.mult, op1=ALU.max), inc="v")
        p.wait("vector", "v")
        p.op("vector", lambda e, m=m: e.tensor_reduce(
            out=lsum[:, m:m + 1], in_=sca[:], axis=mybir.AxisListType.X,
            op=ALU.add), inc="v")
        p.wait("vector", "v")

    gl, _ = emit_allgather(p, lsum, KD, "ls")
    coff = p.sb("coff", [P, KD], F32)
    p.op("vector", lambda e: e.memset(coff[:], 0.0), inc="v")
    p.wait("vector", "v")
    for r in range(8):
        p.op("vector", lambda e, r=r: e.scalar_tensor_tensor(
            out=coff[:], in0=gl[:, r * KD:(r + 1) * KD], scalar=pmc[:, r:r + 1],
            in1=coff[:], op0=ALU.mult, op1=ALU.add), inc="v")
        p.wait("vector", "v")

    usum = p.sb("usum", [P, KD], F32)
    ub4 = p.sb("ub4", [P, T], BF16)
    for m in range(KD):
        p.wait("sync", "s")
        p.wait("sync", "v")
        p.dma("sync", rb4[:, :], rspill[m * P:(m + 1) * P, :], inc="rs")
        p.wait("vector", "rs")
        p.wait("scalar", "rs")
        for off, ln_ in tok_blocks(T, 512):
            p.wait("tensor", "v")
            p.wait("tensor", "s")
            for k in range(KD):
                p.op("tensor", lambda e, m=m, k=k, off=off, ln_=ln_: e.matmul(
                    out=ps1[:, 0:ln_],
                    lhsT=wiT[:, k * D + m * P:k * D + (m + 1) * P],
                    rhs=xcb[:, k * T + off:k * T + off + ln_],
                    start=(k == 0), stop=(k == KD - 1)),
                    inc=("t" if k == KD - 1 else None))
            p.wait("scalar", "t")
            p.op("scalar", lambda e, m=m, off=off, ln_=ln_: e.activation(
                out=sca[:, off:off + ln_], in_=ps1[:, 0:ln_],
                func=AF.Sigmoid, bias=ibc[:, m:m + 1]), inc="s")
        p.wait("scalar", "v")
        p.wait("scalar", "s")
        p.op("scalar", lambda e, m=m: e.activation(
            out=scb[:], in_=rb4[:, :], func=AF.Exp, scale=lac[:, m:m + 1]), inc="s")
        p.wait("vector", "s")
        p.op("vector", lambda e: e.tensor_mul(out=scc[:], in0=scb[:], in1=scb[:]), inc="v")
        p.wait("vector", "v")
        p.op("vector", lambda e: e.tensor_scalar(
            out=scc[:], in0=scc[:], scalar1=-1.0, scalar2=1.0,
            op0=ALU.mult, op1=ALU.add), inc="v")
        p.wait("vector", "v")
        p.op("vector", lambda e: e.tensor_scalar(
            out=scc[:], in0=scc[:], scalar1=1e-6, scalar2=None,
            op0=ALU.max), inc="v")
        p.wait("scalar", "v")
        p.op("scalar", lambda e: e.activation(out=scc[:], in_=scc[:], func=AF.Sqrt), inc="s")
        p.wait("vector", "s")
        p.op("vector", lambda e: e.tensor_mul(out=sca[:], in0=sca[:], in1=scc[:]), inc="v")
        p.wait("vector", "v")
        p.op("vector", lambda e, m=m: e.tensor_mul(
            out=sca[:], in0=sca[:], in1=xcb[:, m * T:(m + 1) * T]), inc="v")
        p.wait("vector", "v")
        p.op("vector", lambda e, m=m: e.tensor_scalar(
            out=scb[:], in0=rb4[:, :], scalar1=lac[:, m:m + 1],
            scalar2=LN7, op0=ALU.mult, op1=ALU.max), inc="v")
        p.wait("vector", "v")
        p.op("vector", lambda e: e.tensor_tensor_scan(
            out=scc[:], data0=ones_T[:], data1=scb[:], initial=0.0,
            op0=ALU.mult, op1=ALU.add), inc="v")
        p.wait("vector", "v")
        p.op("vector", lambda e, m=m: e.tensor_scalar(
            out=scc[:], in0=scc[:], scalar1=coff[:, m:m + 1], scalar2=-80.0,
            op0=ALU.add, op1=ALU.max), inc="v")
        p.wait("scalar", "v")
        p.op("scalar", lambda e: e.activation(out=scb[:], in_=scc[:], func=AF.Exp,
                                              scale=-1.0), inc="s")
        p.wait("vector", "s")
        p.op("vector", lambda e: e.tensor_mul(out=sca[:], in0=sca[:], in1=scb[:]), inc="v")
        p.wait("vector", "v")
        p.wait("vector", "rs")
        p.op("vector", lambda e: e.tensor_copy(out=ub4[:], in_=sca[:]), inc="v")
        p.op("vector", lambda e: e.tensor_tensor_scan(
            out=scb[:], data0=ones_T[:], data1=sca[:], initial=0.0,
            op0=ALU.mult, op1=ALU.add), inc="v")
        p.wait("vector", "v")
        p.op("vector", lambda e, m=m: e.tensor_copy(
            out=usum[:, m:m + 1], in_=scb[:, T - 1:T]), inc="v")
        p.wait("sync", "v")
        p.dma("sync", uspill[m * P:(m + 1) * P, :], ub4[:, :], inc="rs")
        p.wait("vector", "v")

    gu, _ = emit_allgather(p, usum, KD, "us")
    uoff = p.sb("uoff", [P, KD], F32)
    p.op("vector", lambda e: e.memset(uoff[:], 0.0), inc="v")
    p.wait("vector", "v")
    for r in range(8):
        p.op("vector", lambda e, r=r: e.scalar_tensor_tensor(
            out=uoff[:], in0=gu[:, r * KD:(r + 1) * KD], scalar=pmc[:, r:r + 1],
            in1=uoff[:], op0=ALU.mult, op1=ALU.add), inc="v")
        p.wait("vector", "v")

    for m in range(KD):
        p.wait("sync", "s")
        p.wait("sync", "v")
        p.dma("sync", rb4[:, :], rspill[m * P:(m + 1) * P, :], inc="rs")
        p.wait("vector", "rs")
        p.op("vector", lambda e, m=m: e.tensor_scalar(
            out=scb[:], in0=rb4[:, :], scalar1=lac[:, m:m + 1],
            scalar2=LN7, op0=ALU.mult, op1=ALU.max), inc="v")
        p.wait("vector", "v")
        p.op("vector", lambda e: e.tensor_tensor_scan(
            out=scc[:], data0=ones_T[:], data1=scb[:], initial=0.0,
            op0=ALU.mult, op1=ALU.add), inc="v")
        p.wait("vector", "v")
        p.op("vector", lambda e, m=m: e.tensor_scalar(
            out=scc[:], in0=scc[:], scalar1=coff[:, m:m + 1], scalar2=-80.0,
            op0=ALU.add, op1=ALU.max), inc="v")
        p.wait("scalar", "v")
        p.op("scalar", lambda e: e.activation(out=scb[:], in_=scc[:], func=AF.Exp), inc="s")
        p.wait("vector", "s")
        p.wait("sync", "v")
        p.dma("sync", ub4[:, :], uspill[m * P:(m + 1) * P, :], inc="rs")
        p.wait("vector", "rs")
        p.op("vector", lambda e, m=m: e.tensor_tensor_scan(
            out=scc[:], data0=ones_T[:], data1=ub4[:, :],
            initial=0.0, op0=ALU.mult, op1=ALU.add), inc="v")
        p.wait("vector", "v")
        p.op("vector", lambda e, m=m: e.scalar_tensor_tensor(
            out=scc[:], in0=scc[:], scalar=uoff[:, m:m + 1], in1=scb[:],
            op0=ALU.add, op1=ALU.mult), inc="v")
        p.wait("vector", "v")
        p.wait("vector", "rs")
        p.op("vector", lambda e: e.tensor_copy(out=rb4[:, :], in_=scc[:]), inc="v")
        p.wait("vector", "v")
        p.wait("sync", "v")
        p.dma("sync", hspill[m * P:(m + 1) * P, :], rb4[:, :], inc="rs")

    for m in range(KD):
        p.wait("sync", "v")
        p.wait("sync", "t")
        p.dma("sync", wstg[:, 0:D], owd[m * P:(m + 1) * P, :], inc="dma")
        p.wait("vector", "dma")
        p.op("vector", lambda e: e.tensor_copy(out=wtb[:, 0:D], in_=wstg[:, 0:D]), inc="v")
        p.wait("vector", "v")
        for k in range(KD):
            p.wait("tensor", "v")
            p.op("tensor", lambda e, k=k: e.transpose(
                out=wtp[:, 0:P], in_=wtb[:, k * P:(k + 1) * P],
                identity=cm.ident_bf[:]), inc="t")
            p.wait("vector", "t")
            p.op("vector", lambda e, m=m, k=k: e.tensor_copy(
                out=wrT[:, k * D + m * P:k * D + (m + 1) * P],
                in_=wtp[:, 0:P]), inc="v")
            p.wait("tensor", "v")

    hblk = p.sb("hblk", [P, KD * 512], BF16)
    yblk = p.sb("yblk", [P, KD * 512], BF16)
    msps = p.psum("msps", [1, 512], F32)
    msrow = p.sb("msrow", [1, 512], F32)
    sqblk = p.sb("sqblk", [P, 512], F32)
    rstd_bc = p.sb("rstd_bc", [P, 512], F32)
    pbb = p.psum("pbb", [P, 512], F32)
    sc512 = p.sb("sc512", [P, 512], F32)
    for off, ln_ in tok_blocks(T, 512):
        p.wait("sync", "t")
        p.wait("sync", "rs")
        for k in range(KD):
            p.dma("sync", hblk[:, k * 512:k * 512 + ln_],
                  hspill[k * P:(k + 1) * P, off:off + ln_], inc="rs")
        p.wait("tensor", "rs")
        for m in range(KD):
            p.wait("tensor", "v")
            for k in range(KD):
                p.op("tensor", lambda e, m=m, k=k, ln_=ln_: e.matmul(
                    out=ps1[:, 0:ln_],
                    lhsT=wrT[:, k * D + m * P:k * D + (m + 1) * P],
                    rhs=hblk[:, k * 512:k * 512 + ln_],
                    start=(k == 0), stop=(k == KD - 1)),
                    inc=("t" if k == KD - 1 else None))
            p.wait("vector", "t")
            p.op("vector", lambda e, m=m, ln_=ln_: e.tensor_copy(
                out=yblk[:, m * 512:m * 512 + ln_], in_=ps1[:, 0:ln_]), inc="v")
            p.wait("scalar", "v")
            p.op("scalar", lambda e, ln_=ln_: e.activation(
                out=sqblk[:, 0:ln_], in_=ps1[:, 0:ln_], func=AF.Square), inc="s")
            p.wait("tensor", "s")
            p.op("tensor", lambda e, m=m, ln_=ln_: e.matmul(
                out=msps[0:1, 0:ln_], lhsT=cm.ones_col[:, 0:1], rhs=sqblk[:, 0:ln_],
                start=(m == 0), stop=(m == KD - 1)),
                inc=("t" if m == KD - 1 else None))
        p.wait("vector", "t")
        p.op("vector", lambda e, ln_=ln_: e.tensor_scalar(
            out=msrow[:, 0:ln_], in0=msps[0:1, 0:ln_], scalar1=1.0 / D,
            scalar2=1e-6, op0=ALU.mult, op1=ALU.add), inc="v")
        p.wait("scalar", "v")
        p.op("scalar", lambda e, ln_=ln_: e.activation(
            out=msrow[:, 0:ln_], in_=msrow[:, 0:ln_], func=AF.Ln), inc="s")
        p.wait("scalar", "s")
        p.op("scalar", lambda e, ln_=ln_: e.activation(
            out=msrow[:, 0:ln_], in_=msrow[:, 0:ln_], func=AF.Exp,
            scale=-0.5), inc="s")
        p.wait("tensor", "s")
        p.wait("vector", "s")
        emit_bcast_row(p, cm, msrow, rstd_bc, ln_, pbb)
        for m in range(KD):
            p.wait("vector", "v")
            p.wait("vector", "st")
            p.op("vector", lambda e, m=m, ln_=ln_: e.scalar_tensor_tensor(
                out=sc512[:, 0:ln_], in0=yblk[:, m * 512:m * 512 + ln_],
                scalar=nwc[:, m:m + 1], in1=rstd_bc[:, 0:ln_],
                op0=ALU.mult, op1=ALU.mult), inc="v")
            p.wait("vector", "v")
            p.op("vector", lambda e, m=m, off=off, ln_=ln_: e.tensor_add(
                out=sc512[:, 0:ln_], in0=sc512[:, 0:ln_],
                in1=xt[:, m * TEA + 3 + off:m * TEA + 3 + off + ln_]), inc="v")
            p.wait("sync", "v")
            p.dma("sync", yd[m * P:(m + 1) * P, off:off + ln_], sc512[:, 0:ln_], inc="st")
    p.wait("sync", "st")


def run_l1_attn_a(x, inp):
    pr = _get_prog("l1aa", build_l1_attn_a)
    xs = []
    for c in range(NCORES):
        b, s0 = c // 4, (c % 4) * T
        ext = np.zeros((D, TEA), np.float32)
        lo = s0 - 3
        src_lo = max(lo, 0)
        ext[:, src_lo - lo:] = x[b, src_lo:s0 + T, :].T
        xs.append(ext)
    base = {
        "conv_w": np.asarray(inp["L1_attn_a_conv_weight"], np.float32).reshape(D, 4),
        "conv_b": np.asarray(inp["L1_attn_a_conv_bias"], np.float32).reshape(D, 1),
        "Wr_w": np.asarray(inp["L1_attn_a_Wr_w"], np.float32),
        "Wr_b": np.asarray(inp["L1_attn_a_Wr_b"], np.float32).reshape(D, 1),
        "Wi_w": np.asarray(inp["L1_attn_a_Wi_w"], np.float32),
        "Wi_b": np.asarray(inp["L1_attn_a_Wi_b"], np.float32).reshape(D, 1),
        "log_a": np.asarray(inp["L1_attn_a_log_a"], np.float32).reshape(D, 1),
        "out_w": np.asarray(inp["L1_attn_a_out_w"], np.float32),
        "norm_w": np.asarray(inp["L1_attn_a_norm_w"], np.float32).reshape(D, 1),
        **Common.host_inputs(),
    }
    in_maps = []
    for c in range(NCORES):
        pm = np.zeros((P, 8), np.float32)
        for r in range(8):
            if r // 4 == c // 4 and r < c:
                pm[:, r] = 1.0
        in_maps.append({**base, "x": xs[c], "prevmask": pm})
    res = run_bass_kernel_spmd(pr.nc, in_maps, core_ids=list(range(NCORES)))
    return _unshard([res.results[c]["y"] for c in range(NCORES)])




def build_l0_attn_b():
    p = Prog("l0ab")
    cm = Common(p)
    yd = p.dram("y", [D, T], F32, kind="ExternalOutput").ap()
    emit_l0ab(p, cm, yd)
    p.finish()
    return p


def emit_l0ab(p, cm, yd, xin=None):
    xd = (xin if xin is not None
          else p.dram("x", [D, T], F32, kind="ExternalInput").ap())
    wqd = p.dram("Wq", [D, D], F32, kind="ExternalInput").ap()
    wkd = p.dram("Wk", [D, D], F32, kind="ExternalInput").ap()
    wvd = p.dram("Wv", [D, D], F32, kind="ExternalInput").ap()
    wbd = p.dram("Wb", [D, 16], F32, kind="ExternalInput").ap()
    wod = p.dram("Wo", [D, D], F32, kind="ExternalInput").ap()
    ln1d = p.dram("ln1", [D, 1], F32, kind="ExternalInput").ap()
    ln2d = p.dram("ln2", [D, 1], F32, kind="ExternalInput").ap()
    wgd = p.dram("Wg", [D, 4096], F32, kind="ExternalInput").ap()
    wud = p.dram("Wu", [D, 4096], F32, kind="ExternalInput").ap()
    wdd = p.dram("Wd", [4096, D], F32, kind="ExternalInput").ap()
    trilSd = p.dram("trilS", [P, P], BF16, kind="ExternalInput").ap()
    triuSNd = p.dram("triuSN", [P, P], BF16, kind="ExternalInput").ap()
    triuId = p.dram("triuI", [P, P], BF16, kind="ExternalInput").ap()
    seld = p.dram("selk", [16, D], F32, kind="ExternalInput").ap()
    pmd = p.dram("prevmask", [P, 8], F32, kind="ExternalInput").ap()
    qspill = p.dram("qspill", [D, T], BF16)
    kspill = p.dram("kspill", [D, T], BF16)
    vspill = p.dram("vspill", [D, T], BF16)
    qptd = p.dram("qptd", [16 * 64, T], BF16)
    NC = T // P

    p.constcol(1e-6, 2)
    xt = p.sb("xt", [P, KD * T], F32)
    for k in range(KD):
        p.dma("sync", xt[:, k * T:(k + 1) * T], xd[k * P:(k + 1) * P, :])
    trilS = p.sb("c_trilS", [P, P], BF16)
    triuSN = p.sb("c_triuSN", [P, P], BF16)
    triuI = p.sb("c_triuI", [P, P], BF16)
    selk = p.sb("c_selk", [16, D], F32)
    pmc = p.sb("c_pmc", [P, 8], F32)
    ln1c = p.sb("ln1c", [P, KD], F32)
    ln2c = p.sb("ln2c", [P, KD], F32)
    p.dma("sync", trilS[:], trilSd)
    p.dma("sync", triuSN[:], triuSNd)
    p.dma("sync", triuI[:], triuId)
    p.dma("sync", selk[:], seld)
    p.dma("sync", pmc[:], pmd)
    for k in range(KD):
        p.dma("sync", ln1c[:, k:k + 1], ln1d[k * P:(k + 1) * P, :])
        p.dma("sync", ln2c[:, k:k + 1], ln2d[k * P:(k + 1) * P, :])
    p.wait("vector", "dma")
    p.wait("scalar", "dma")
    p.wait("tensor", "dma")

    rstd2_bc = p.sb("rstd2_bc", [P, T], F32)
    rho_bc, mrho_bc, ln_names = emit_ln(p, cm, xt, T, "ab")
    rhor = p._ln_rows["B"]
    pbps = p._ln_rows["pb"]
    rowR = p.sb("rowR", [1, T], F32)
    p.wait("vector", "v")
    p.op("vector", lambda e: e.tensor_mul(out=rowR[:], in0=rhor[0:1, :], in1=rhor[0:1, :]), inc="v")
    p.wait("vector", "v")
    p.op("vector", lambda e: e.tensor_scalar(
        out=rowR[:], in0=rowR[:], scalar1=-1e-5, scalar2=1.0 + 1e-6,
        op0=ALU.mult, op1=ALU.add), inc="v")
    p.wait("scalar", "v")
    p.op("scalar", lambda e: e.activation(out=rowR[:], in_=rowR[:], func=AF.Ln), inc="s")
    p.wait("scalar", "s")
    p.op("scalar", lambda e: e.activation(out=rowR[:], in_=rowR[:], func=AF.Exp,
                                          scale=-0.5), inc="s")
    p.wait("tensor", "s")
    emit_bcast_row(p, cm, rowR, rstd2_bc, T, pbps)
    p.free("rowR")
    p.free(*ln_names)

    Otm = p.sb("Otm", [P, NC * 16 * 64], BF16)
    btm = p.sb("btm", [P, NC * 16], F32)
    btmN = p.sb("btmN", [P, NC * 16], F32)
    x2b = p.sb("x2b", [P, KD * T], BF16)
    wb_ = p.sb("wb_", [P, KD * P], BF16)
    wstg = p.sb("wstg", [P, P], F32)
    prj = p.sb("prj", [P, T], F32)
    prb = p.sb("prb", [P, T], BF16)
    s2r = p.sb("s2r", [2, T], F32)
    rsqb = p.sb("rsqb", [P, T], F32)
    b16 = p.sb("b16", [16, T], F32)
    ps1 = p.psum("ps1", [P, 512], F32)
    ps2 = p.psum("ps2", [2, 512], F32)
    psb = p.psum("psb", [P, 512], F32)
    E2 = p.sb("E2", [P, 2], F32)
    e2t = p.sb("e2t", [2, P], F32)
    p.op("vector", lambda e: e.memset(E2[:, :], 0.0), inc=None)
    p.op("vector", lambda e: e.memset(e2t[:, :], 0.0), inc="v")
    p.wait("vector", "v")
    p.op("vector", lambda e: e.memset(E2[0:64, 0:1], 1.0), inc=None)
    p.op("vector", lambda e: e.memset(E2[64:128, 1:2], 1.0), inc=None)
    p.op("vector", lambda e: e.memset(e2t[0:1, 0:64], 1.0), inc="v")
    p.wait("sync", "v")
    p.dma("sync", e2t[1:2, 64:128], cm.ones_row[0:1, 0:64], inc="dma")
    p.wait("vector", "dma")
    p.wait("tensor", "dma")
    p.wait("vector", "v")

    for k in range(KD):
        p.wait("vector", "v")
        p.op("vector", lambda e, k=k: e.scalar_tensor_tensor(
            out=prj[:], in0=xt[:, k * T:(k + 1) * T], scalar=1.0,
            in1=rho_bc[:], op0=ALU.mult, op1=ALU.mult), inc=None)
        p.op("vector", lambda e, k=k: e.tensor_add(
            out=prj[:], in0=prj[:], in1=mrho_bc[:]), inc=None)
        p.op("vector", lambda e, k=k: e.scalar_tensor_tensor(
            out=prj[:], in0=prj[:], scalar=ln1c[:, k:k + 1],
            in1=rstd2_bc[:], op0=ALU.mult, op1=ALU.mult), inc=None)
        p.op("vector", lambda e, k=k: e.tensor_copy(
            out=x2b[:, k * T:(k + 1) * T], in_=prj[:]), inc="v")
    p.free("rstd2_bc") if False else None

    def load_w_chunk(Wd_, m):
        for k in range(KD):
            p.wait("sync", "v")
            p.wait("sync", "t")
            p.dma("sync", wstg[:, :], Wd_[k * P:(k + 1) * P, m * P:(m + 1) * P], inc="dma")
            p.wait("vector", "dma")
            p.op("vector", lambda e, k=k: e.tensor_copy(
                out=wb_[:, k * P:(k + 1) * P], in_=wstg[:, :]), inc="v")

    for which, Wd_, spill, do_l2 in (("q", wqd, qspill, True),
                                     ("k", wkd, kspill, True),
                                     ("v", wvd, vspill, False)):
        for m in range(KD):
            load_w_chunk(Wd_, m)
            for off, ln_ in tok_blocks(T, 512):
                p.wait("tensor", "v")
                for k in range(KD):
                    p.op("tensor", lambda e, m=m, k=k, off=off, ln_=ln_: e.matmul(
                        out=ps1[:, 0:ln_],
                        lhsT=wb_[:, k * P:(k + 1) * P],
                        rhs=x2b[:, k * T + off:k * T + off + ln_],
                        start=(k == 0), stop=(k == KD - 1)),
                        inc=("t" if k == KD - 1 else None))
                p.wait("scalar", "t")
                p.op("scalar", lambda e, off=off, ln_=ln_: e.activation(
                    out=prj[:, off:off + ln_], in_=ps1[:, 0:ln_], func=AF.Silu), inc="v")
            if do_l2:
                p.wait("scalar", "v")
                for off, ln_ in tok_blocks(T, 512):
                    p.op("scalar", lambda e, off=off, ln_=ln_: e.activation(
                        out=rsqb[:, off:off + ln_], in_=prj[:, off:off + ln_],
                        func=AF.Square), inc="s")
                p.wait("tensor", "s")
                for off, ln_ in tok_blocks(T, 512):
                    p.op("tensor", lambda e, off=off, ln_=ln_: e.matmul(
                        out=ps2[0:2, 0:ln_], lhsT=E2[:, 0:2],
                        rhs=rsqb[:, off:off + ln_], start=True, stop=True), inc="t")
                    p.wait("vector", "t")
                    p.op("vector", lambda e, off=off, ln_=ln_: e.tensor_copy(
                        out=s2r[0:2, off:off + ln_], in_=ps2[0:2, 0:ln_]), inc="v")
                    p.wait("tensor", "v")
                ec = p.constcol(1e-6, 2)
                p.wait("scalar", "v")
                p.op("scalar", lambda e, ec=ec: e.activation(
                    out=s2r[0:2, :], in_=s2r[0:2, :],
                    func=AF.Ln, bias=ec[0:2, 0:1]), inc="s")
                p.wait("scalar", "s")
                p.op("scalar", lambda e: e.activation(
                    out=s2r[0:2, :], in_=s2r[0:2, :],
                    func=AF.Exp, scale=-0.5), inc="s")
                p.wait("tensor", "s")
                for off, ln_ in tok_blocks(T, 512):
                    p.op("tensor", lambda e, off=off, ln_=ln_: e.matmul(
                        out=psb[:, 0:ln_], lhsT=e2t[0:2, :],
                        rhs=s2r[0:2, off:off + ln_], start=True, stop=True), inc="t")
                    p.wait("vector", "t")
                    p.op("vector", lambda e, off=off, ln_=ln_: e.tensor_mul(
                        out=prb[:, off:off + ln_], in0=prj[:, off:off + ln_],
                        in1=psb[:, 0:ln_]), inc="v")
                    p.wait("tensor", "v")
            else:
                p.wait("vector", "v")
                p.op("vector", lambda e: e.tensor_copy(out=prb[:], in_=prj[:]), inc="v")
            p.wait("sync", "v")
            p.dma("sync", spill[m * P:(m + 1) * P, :], prb[:], inc="sp")
            p.wait("vector", "sp")
            p.wait("scalar", "sp")

    wbb = p.sb("wbb", [P, KD * 16], BF16)
    for k in range(KD):
        p.wait("sync", "v")
        p.dma("sync", wstg[:, 0:16], wbd[k * P:(k + 1) * P, :], inc="dma")
        p.wait("vector", "dma")
        p.op("vector", lambda e, k=k: e.tensor_copy(
            out=wbb[:, k * 16:(k + 1) * 16], in_=wstg[:, 0:16]), inc="v")
    psb16 = p.psum("psb16", [16, 512], F32)
    for off, ln_ in tok_blocks(T, 512):
        p.wait("tensor", "v")
        p.wait("tensor", "s")
        for k in range(KD):
            p.op("tensor", lambda e, k=k, off=off, ln_=ln_: e.matmul(
                out=psb16[0:16, 0:ln_], lhsT=wbb[:, k * 16:(k + 1) * 16],
                rhs=x2b[:, k * T + off:k * T + off + ln_],
                start=(k == 0), stop=(k == KD - 1)),
                inc=("t" if k == KD - 1 else None))
        p.wait("scalar", "t")
        p.op("scalar", lambda e, off=off, ln_=ln_: e.activation(
            out=b16[0:16, off:off + ln_], in_=psb16[0:16, 0:ln_],
            func=AF.Sigmoid), inc="s")
    pstb = p.psum("pstb", [P, 16], F32)
    p.wait("tensor", "s")
    for c in range(NC):
        p.wait("tensor", "v")
        p.op("tensor", lambda e, c=c: e.transpose(
            out=pstb[0:P, 0:16], in_=b16[0:16, c * P:(c + 1) * P],
            identity=cm.ident_f[0:16, 0:16]), inc="t")
        p.wait("vector", "t")
        p.op("vector", lambda e, c=c: e.tensor_copy(
            out=btm[:, c * 16:(c + 1) * 16], in_=pstb[0:P, 0:16]), inc=None)
        p.op("vector", lambda e, c=c: e.tensor_scalar(
            out=btmN[:, c * 16:(c + 1) * 16], in0=pstb[0:P, 0:16],
            scalar1=-1.0, scalar2=None, op0=ALU.mult), inc="v")
        p.wait("tensor", "v")
    p.free("pstb", "psb16", "psb", "ps2")
    p.free("wbb")
    p.free("e2t", "E2")
    p.free("b16", "rsqb", "s2r", "prb", "prj", "wstg", "wb_", "x2b")
    p.fence_dma()

    Mf = p.sb("Mf", [64, 16 * 64], F32)
    Mb = p.sb("Mb", [64, 16 * 64], BF16)
    Pac = p.sb("Pac", [64, 16 * 64], F32)
    Pacb = p.sb("Pacb", [64, 16 * 64], BF16)
    Fac = p.sb("Fac", [64, 16 * 64], F32)
    identf64 = cm.ident_f
    p.op("vector", lambda e: e.memset(Mf[:], 0.0), inc=None)
    p.op("vector", lambda e: e.memset(Mb[:], 0.0), inc=None)
    for h in range(16):
        p.op("vector", lambda e, h=h: e.tensor_copy(
            out=Pac[0:64, h * 64:(h + 1) * 64], in_=identf64[0:64, 0:64]), inc=None)
        p.op("vector", lambda e, h=h: e.tensor_copy(
            out=Pacb[0:64, h * 64:(h + 1) * 64], in_=identf64[0:64, 0:64]), inc=None)
        p.op("vector", lambda e, h=h: e.tensor_copy(
            out=Fac[0:64, h * 64:(h + 1) * 64], in_=identf64[0:64, 0:64]), inc="v")
    p.wait("vector", "v")

    kslc = p.sb("kslc", [64, P], BF16)
    vslc = p.sb("vslc", [64, P], BF16)
    qslc = p.sb("qslc", [64, P], BF16)
    Gsb = p.sb("Gsb", [P, P], BF16)
    bfree = p.sb("bfree", [P, P], BF16)
    Zt = p.sb("Zt_t", [P, P], BF16)
    Zu = p.sb("Zu_t", [P, P], BF16)
    At = p.sb("At_t", [P, P], BF16)
    Xb = p.sb("Xb_t", [P, P], BF16)
    Ktm = p.sb("Ktm", [P, 64], BF16)
    WT = p.sb("WT_t", [64, P], BF16)
    Up = p.sb("Up_t", [P, 64], BF16)
    QeT = p.sb("QeT", [64, P], BF16)
    tmp64 = p.sb("tmp64", [P, 64], F32)
    brow = p.sb("brow", [1, P], F32)
    ft64 = p.sb("ft64", [64, 64], F32)
    pg = p.psum("pg", [P, P], F32)
    pz1 = p.psum("pz1", [P, P], F32)
    pz2 = p.psum("pz2", [P, P], F32)
    pm1 = p.psum("pm1", [P, P], F32)
    ps64 = p.psum("ps64", [64, P], F32)
    pzt = p.psum("pzt", [P, 64], BF16)
    ps64b = p.psum("ps64b", [64, P], BF16)

    for c in range(NC):
        for h in range(16):
            uc = c * 16 + h
            p.wait("sync", "v")
            p.wait("sync", "t")
            p.dma("sync", kslc[:], kspill[h * 64:(h + 1) * 64, c * P:(c + 1) * P], inc="sp")
            p.dma("sync", vslc[:], vspill[h * 64:(h + 1) * 64, c * P:(c + 1) * P], inc="sp")
            p.dma("sync", qslc[:], qspill[h * 64:(h + 1) * 64, c * P:(c + 1) * P], inc="sp")
            p.wait("tensor", "sp")
            p.op("tensor", lambda e: e.matmul(out=pg[:], lhsT=kslc[:], rhs=kslc[:],
                                              start=True, stop=True), inc="t")
            p.wait("vector", "t")
            p.op("vector", lambda e: e.tensor_copy(out=Gsb[:], in_=pg[:]), inc="v")
            p.wait("tensor", "v")
            p.op("tensor", lambda e, uc=uc: e.transpose(
                out=pm1[0:1, 0:P], in_=btm[:, uc:uc + 1], identity=cm.ident_f[:]), inc="t")
            p.wait("vector", "t")
            p.op("vector", lambda e: e.tensor_copy(out=brow[:], in_=pm1[0:1, :]), inc="v")
            p.wait("tensor", "v")
            p.op("tensor", lambda e: e.matmul(out=pm1[:], lhsT=cm.ones_row[0:1, :],
                                              rhs=brow[0:1, :], start=True, stop=True), inc="t")
            p.wait("vector", "t")
            p.op("vector", lambda e: e.tensor_copy(out=bfree[:], in_=pm1[:]), inc="v")
            p.wait("vector", "v")
            p.op("vector", lambda e: e.tensor_mul(out=Zt[:], in0=Gsb[:], in1=bfree[:]), inc="v")
            p.wait("vector", "v")
            p.op("vector", lambda e: e.tensor_mul(out=Zt[:], in0=Zt[:], in1=triuSN[:]), inc="v")
            p.op("vector", lambda e, uc=uc: e.scalar_tensor_tensor(
                out=Zu[:], in0=Gsb[:], scalar=btmN[:, uc:uc + 1], in1=trilS[:],
                op0=ALU.mult, op1=ALU.mult), inc="v")
            p.wait("vector", "v")
            p.op("tensor", lambda e: e.matmul(out=pg[:], lhsT=kslc[:], rhs=qslc[:],
                                              start=True, stop=True), inc="t")
            p.wait("vector", "t")
            p.op("vector", lambda e: e.tensor_mul(out=At[:], in0=pg[:], in1=triuI[:]), inc="v")
            p.wait("tensor", "v")
            p.op("tensor", lambda e: e.transpose(out=pzt[0:P, 0:64], in_=vslc[:],
                                                 identity=cm.ident_bf[0:64, 0:64]), inc="t")
            p.wait("vector", "t")
            p.op("vector", lambda e, uc=uc: e.tensor_scalar(
                out=Xb[:, 0:64], in0=pzt[0:P, 0:64], scalar1=btm[:, uc:uc + 1],
                scalar2=None, op0=ALU.mult), inc="v")
            p.wait("tensor", "v")
            p.op("tensor", lambda e: e.transpose(out=pzt[0:P, 0:64], in_=kslc[:],
                                                 identity=cm.ident_bf[0:64, 0:64]), inc="t")
            p.wait("vector", "t")
            p.op("vector", lambda e: e.tensor_copy(out=Ktm[:], in_=pzt[0:P, 0:64]), inc=None)
            p.op("vector", lambda e, uc=uc: e.tensor_scalar(
                out=Xb[:, 64:128], in0=pzt[0:P, 0:64], scalar1=btm[:, uc:uc + 1],
                scalar2=None, op0=ALU.mult), inc="v")
            p.wait("tensor", "v")
            for it in range(7):
                p.op("tensor", lambda e: e.matmul(out=pz1[:], lhsT=Zt[:], rhs=Xb[:],
                                                  start=True, stop=True), inc="t")
                p.wait("vector", "t")
                p.op("vector", lambda e: e.tensor_add(out=Xb[:], in0=Xb[:], in1=pz1[:]), inc="v")
                if it < 6:
                    p.wait("tensor", "v")
                    p.op("tensor", lambda e: e.matmul(out=pz1[:], lhsT=Zt[:], rhs=Zu[:],
                                                      start=True, stop=True), inc=None)
                    p.op("tensor", lambda e: e.matmul(out=pz2[:], lhsT=Zu[:], rhs=Zt[:],
                                                      start=True, stop=True), inc="t")
                    p.wait("vector", "t")
                    p.op("vector", lambda e: e.tensor_copy(out=Zu[:], in_=pz1[:]), inc=None)
                    p.op("vector", lambda e: e.tensor_copy(out=Zt[:], in_=pz2[:]), inc="v")
                p.wait("tensor", "v")
            p.op("tensor", lambda e: e.transpose(out=ps64b[0:64, 0:P], in_=Xb[:, 64:128],
                                                 identity=cm.ident_bf[:]), inc="t")
            p.wait("vector", "t")
            p.op("vector", lambda e: e.tensor_copy(out=WT[:], in_=ps64b[0:64, :]), inc="v")
            p.wait("tensor", "v")
            p.op("tensor", lambda e, h=h: e.matmul(
                out=pz1[0:P, 0:64], lhsT=WT[:], rhs=Mb[0:64, h * 64:(h + 1) * 64],
                start=True, stop=True), inc="t")
            p.wait("vector", "t")
            p.op("vector", lambda e: e.scalar_tensor_tensor(
                out=Up[:], in0=pz1[0:P, 0:64], scalar=-1.0, in1=Xb[:, 0:64],
                op0=ALU.mult, op1=ALU.add), inc="v")
            p.wait("tensor", "v")
            p.op("tensor", lambda e: e.matmul(out=ps64[0:64, 0:P], lhsT=Xb[:, 64:128],
                                              rhs=At[:], start=True, stop=True), inc="t")
            p.wait("vector", "t")
            p.op("vector", lambda e: e.scalar_tensor_tensor(
                out=QeT[:], in0=ps64[0:64, :], scalar=-1.0, in1=qslc[:],
                op0=ALU.mult, op1=ALU.add), inc="v")
            p.wait("tensor", "v")
            p.op("tensor", lambda e, h=h: e.matmul(
                out=ps64[0:64, 0:P], lhsT=Pacb[0:64, h * 64:(h + 1) * 64],
                rhs=QeT[:], start=True, stop=True), inc="t")
            p.wait("vector", "t")
            p.op("vector", lambda e: e.tensor_copy(out=QeT[:], in_=ps64[0:64, :]), inc="v")
            p.wait("sync", "v")
            p.dma("sync", qptd[h * 64:(h + 1) * 64, c * P:(c + 1) * P], QeT[:], inc="sp")
            p.wait("tensor", "v")
            p.op("tensor", lambda e: e.matmul(out=pz1[0:P, 0:64], lhsT=At[:], rhs=Up[:],
                                              start=True, stop=False), inc=None)
            p.op("tensor", lambda e, h=h: e.matmul(
                out=pz1[0:P, 0:64], lhsT=qslc[:], rhs=Mb[0:64, h * 64:(h + 1) * 64],
                start=False, stop=True), inc="t")
            p.wait("vector", "t")
            p.op("vector", lambda e, uc=uc: e.tensor_copy(
                out=Otm[:, uc * 64:(uc + 1) * 64], in_=pz1[0:P, 0:64]), inc="v")
            p.wait("tensor", "v")
            p.op("tensor", lambda e: e.matmul(out=pm1[0:64, 0:64], lhsT=Xb[:, 64:128],
                                              rhs=Ktm[:], start=True, stop=True), inc="t")
            p.wait("vector", "t")
            p.op("vector", lambda e: e.scalar_tensor_tensor(
                out=ft64[:], in0=pm1[0:64, 0:64], scalar=-1.0,
                in1=cm.ident_f[0:64, 0:64], op0=ALU.mult, op1=ALU.add), inc="v")
            p.wait("tensor", "v")
            p.op("tensor", lambda e: e.matmul(out=pm1[0:64, 0:64], lhsT=Ktm[:], rhs=Up[:],
                                              start=True, stop=True), inc="t")
            p.wait("vector", "t")
            p.op("vector", lambda e, h=h: e.tensor_add(
                out=Mf[0:64, h * 64:(h + 1) * 64], in0=Mf[0:64, h * 64:(h + 1) * 64],
                in1=pm1[0:64, 0:64]), inc="v")
            p.wait("vector", "v")
            p.op("vector", lambda e, h=h: e.tensor_copy(
                out=Mb[0:64, h * 64:(h + 1) * 64], in_=Mf[0:64, h * 64:(h + 1) * 64]), inc="v")
            p.wait("tensor", "v")
            p.op("tensor", lambda e, h=h: e.matmul(
                out=pm1[0:64, 0:64], lhsT=ft64[:, :], rhs=Pac[0:64, h * 64:(h + 1) * 64],
                start=True, stop=True), inc=None)
            p.op("tensor", lambda e, h=h: e.matmul(
                out=pm1[0:64, 64:128], lhsT=ft64[:, :], rhs=Fac[0:64, h * 64:(h + 1) * 64],
                start=True, stop=True), inc="t")
            p.wait("vector", "t")
            p.op("vector", lambda e, h=h: e.tensor_copy(
                out=Pac[0:64, h * 64:(h + 1) * 64], in_=pm1[0:64, 0:64]), inc=None)
            p.op("vector", lambda e, h=h: e.tensor_copy(
                out=Pacb[0:64, h * 64:(h + 1) * 64], in_=pm1[0:64, 0:64]), inc=None)
            p.op("vector", lambda e, h=h: e.tensor_copy(
                out=Fac[0:64, h * 64:(h + 1) * 64], in_=pm1[0:64, 64:128]), inc="v")
            p.wait("tensor", "v")

    FacT = p.sb("FacT", [64, 16 * 64], F32)
    for h in range(16):
        p.wait("tensor", "v")
        p.op("tensor", lambda e, h=h: e.transpose(
            out=pm1[0:64, 0:64], in_=Fac[0:64, h * 64:(h + 1) * 64],
            identity=cm.ident_f[0:64, 0:64]), inc="t")
        p.wait("vector", "t")
        p.op("vector", lambda e, h=h: e.tensor_copy(
            out=FacT[0:64, h * 64:(h + 1) * 64], in_=pm1[0:64, 0:64]), inc="v")
        p.wait("tensor", "v")
    _, dM = emit_allgather(p, Mf, 16 * 64, "dm", npart=64, readback=False)
    _, dF = emit_allgather(p, FacT, 16 * 64, "df", npart=64, readback=False)
    gMr = p.sb("gMr", [64, 16 * 64], F32)
    gFr = p.sb("gFr", [64, 16 * 64], F32)

    Min = p.sb("Min", [64, 16 * 64], F32)
    Minb = p.sb("Minb", [64, 16 * 64], BF16)
    fblend = p.sb("fblend", [64, 64], F32)
    iscaled = p.sb("iscaled", [64, 64], F32)
    p.op("vector", lambda e: e.memset(Min[:], 0.0), inc="v")
    p.wait("vector", "v")
    for r in range(7):
        p.wait("sync", "v")
        p.wait("sync", "cc")
        p.dma("sync", gMr[:], dM.ap()[r * 64:(r + 1) * 64, :], inc="dma")
        p.dma("sync", gFr[:], dF.ap()[r * 64:(r + 1) * 64, :], inc="dma")
        p.wait("vector", "dma")
        p.op("vector", lambda e, r=r: e.tensor_scalar(
            out=fblend[:], in0=pmc[0:64, r:r + 1], scalar1=-1.0, scalar2=1.0,
            op0=ALU.mult, op1=ALU.add), inc="v") if False else None
        p.op("vector", lambda e, r=r: e.scalar_tensor_tensor(
            out=iscaled[:], in0=cm.ident_f[0:64, 0:64], scalar=pmc[0:64, r:r + 1],
            in1=cm.ident_f[0:64, 0:64], op0=ALU.mult, op1=ALU.subtract), inc="v")
        p.wait("vector", "v")
        p.op("vector", lambda e: e.tensor_scalar(
            out=iscaled[:], in0=iscaled[:], scalar1=-1.0, scalar2=None,
            op0=ALU.mult), inc="v")
        p.wait("vector", "v")
        for h in range(16):
            p.op("vector", lambda e, r=r, h=h: e.scalar_tensor_tensor(
                out=fblend[:], in0=gFr[0:64, h * 64:(h + 1) * 64],
                scalar=pmc[0:64, r:r + 1], in1=iscaled[:],
                op0=ALU.mult, op1=ALU.add), inc="v")
            p.wait("tensor", "v")
            p.op("tensor", lambda e, h=h: e.matmul(
                out=pm1[0:64, 0:64], lhsT=fblend[:], rhs=Min[0:64, h * 64:(h + 1) * 64],
                start=True, stop=True), inc="t")
            p.wait("vector", "t")
            p.op("vector", lambda e, r=r, h=h: e.scalar_tensor_tensor(
                out=Min[0:64, h * 64:(h + 1) * 64],
                in0=gMr[0:64, h * 64:(h + 1) * 64],
                scalar=pmc[0:64, r:r + 1], in1=pm1[0:64, 0:64],
                op0=ALU.mult, op1=ALU.add), inc="v")
            p.wait("vector", "v")
    for h in range(16):
        p.op("vector", lambda e, h=h: e.tensor_copy(
            out=Minb[0:64, h * 64:(h + 1) * 64], in_=Min[0:64, h * 64:(h + 1) * 64]),
            inc=("v" if h == 15 else None))
    p.wait("vector", "v")

    for c in range(NC):
        for h in range(16):
            uc = c * 16 + h
            p.wait("sync", "v")
            p.wait("sync", "t")
            p.dma("sync", QeT[:], qptd[h * 64:(h + 1) * 64, c * P:(c + 1) * P], inc="sp")
            p.wait("tensor", "sp")
            p.op("tensor", lambda e, h=h: e.matmul(
                out=pz1[0:P, 0:64], lhsT=QeT[:], rhs=Minb[0:64, h * 64:(h + 1) * 64],
                start=True, stop=True), inc="t")
            p.wait("vector", "t")
            p.op("vector", lambda e, uc=uc: e.tensor_add(
                out=Otm[:, uc * 64:(uc + 1) * 64], in0=Otm[:, uc * 64:(uc + 1) * 64],
                in1=pz1[0:P, 0:64]), inc="v")
            p.wait("tensor", "v")

    p.free("iscaled", "fblend", "Minb", "Min", "gFr", "gMr", "FacT")
    p.free("ps64b", "pzt", "ps64", "pm1", "pz2", "pz1", "pg")
    p.free("ft64", "brow", "tmp64", "QeT", "Up_t", "WT_t", "Ktm", "Xb_t",
           "At_t", "Zu_t", "Zt_t", "bfree", "Gsb", "qslc", "vslc", "kslc")
    p.free("Fac", "Pacb", "Pac", "Mb", "Mf")
    p.fence_dma()

    otb64 = p.sb("otb64", [64, 512], BF16)
    wohb = p.sb("wohb", [64, 16 * P], BF16)
    wstg2 = p.sb("wstg2", [64, P], F32)
    hbb = p.sb("hbb", [P, KD * T], BF16)
    hch = p.sb("hch", [P, 512], F32)
    msr = p.sb("msr", [1, T], F32)
    sqb2 = p.sb("sqb2", [P, 512], F32)
    pst2 = p.psum("pst2", [64, P], BF16)
    msps2 = p.psum("msps2", [1, 512], F32)
    for off, ln_ in tok_blocks(T, 512):
        for m in range(KD):
            for h in range(16):
                p.wait("sync", "v")
                p.wait("sync", "t")
                p.dma("sync", wstg2[:, :], wod[h * 64:(h + 1) * 64, m * P:(m + 1) * P], inc="dma")
                p.wait("vector", "dma")
                p.op("vector", lambda e, h=h: e.tensor_copy(
                    out=wohb[:, h * P:(h + 1) * P], in_=wstg2[:, :]), inc="v")
            p.wait("tensor", "v")
            for h in range(16):
                for cc in range(off // P, (off + ln_) // P):
                    p.wait("tensor", "v")
                    p.op("tensor", lambda e, h=h, cc=cc: e.transpose(
                        out=pst2[0:64, 0:P],
                        in_=Otm[:, (cc * 16 + h) * 64:(cc * 16 + h + 1) * 64],
                        identity=cm.ident_bf[:]), inc="t")
                    p.wait("vector", "t")
                    p.op("vector", lambda e, cc=cc, off=off: e.tensor_copy(
                        out=otb64[:, (cc * P - off):(cc * P - off) + P],
                        in_=pst2[0:64, 0:P]), inc="v")
                p.wait("tensor", "v")
                p.op("tensor", lambda e, h=h, ln_=ln_: e.matmul(
                    out=ps1[:, 0:ln_], lhsT=wohb[:, h * P:(h + 1) * P],
                    rhs=otb64[:, 0:ln_], start=(h == 0), stop=(h == 15)),
                    inc=("t" if h == 15 else None))
            p.wait("vector", "t")
            p.op("vector", lambda e, m=m, off=off, ln_=ln_: e.scalar_tensor_tensor(
                out=hch[:, 0:ln_], in0=xt[:, m * T + off:m * T + off + ln_],
                scalar=1.0, in1=rho_bc[:, off:off + ln_], op0=ALU.mult, op1=ALU.mult), inc=None)
            p.op("vector", lambda e, off=off, ln_=ln_: e.tensor_add(
                out=hch[:, 0:ln_], in0=hch[:, 0:ln_],
                in1=mrho_bc[:, off:off + ln_]), inc=None)
            p.op("vector", lambda e, ln_=ln_: e.tensor_add(
                out=hch[:, 0:ln_], in0=hch[:, 0:ln_], in1=ps1[:, 0:ln_]), inc="v")
            p.wait("vector", "v")
            p.op("vector", lambda e, m=m, off=off, ln_=ln_: e.tensor_scalar(
                out=hbb[:, m * T + off:m * T + off + ln_], in0=hch[:, 0:ln_],
                scalar1=ln2c[:, m:m + 1], scalar2=None, op0=ALU.mult), inc=None)
            p.op("vector", lambda e, m=m, off=off, ln_=ln_: e.tensor_add(
                out=xt[:, m * T + off:m * T + off + ln_],
                in0=xt[:, m * T + off:m * T + off + ln_],
                in1=hch[:, 0:ln_]), inc="v")
            p.wait("scalar", "v")
            p.op("scalar", lambda e, ln_=ln_: e.activation(
                out=sqb2[:, 0:ln_], in_=hch[:, 0:ln_], func=AF.Square), inc="s")
            p.wait("tensor", "s")
            p.op("tensor", lambda e, m=m, ln_=ln_: e.matmul(
                out=msps2[0:1, 0:ln_], lhsT=cm.ones_col[:, 0:1], rhs=sqb2[:, 0:ln_],
                start=(m == 0), stop=(m == KD - 1)),
                inc=("t" if m == KD - 1 else None))
        p.wait("vector", "t")
        p.op("vector", lambda e, off=off, ln_=ln_: e.tensor_copy(
            out=msr[:, off:off + ln_], in_=msps2[0:1, 0:ln_]), inc="v")
        p.wait("vector", "v")
    rstdb = p.sb("rstdb", [P, T], F32)
    pbb2 = p.psum("pbb2", [P, 512], F32)
    p.op("vector", lambda e: e.tensor_scalar(
        out=msr[:], in0=msr[:], scalar1=1.0 / D, scalar2=1e-6,
        op0=ALU.mult, op1=ALU.add), inc="v")
    p.wait("scalar", "v")
    p.op("scalar", lambda e: e.activation(out=msr[:], in_=msr[:], func=AF.Ln), inc="s")
    p.wait("scalar", "s")
    p.op("scalar", lambda e: e.activation(out=msr[:], in_=msr[:], func=AF.Exp,
                                          scale=-0.5), inc="s")
    p.wait("tensor", "s")
    p.wait("vector", "s")
    emit_bcast_row(p, cm, msr, rstdb, T, pbb2)
    for m in range(KD):
        for off, ln_ in tok_blocks(T, 512):
            p.wait("vector", "v")
            p.op("vector", lambda e, m=m, off=off, ln_=ln_: e.tensor_copy(
                out=hch[:, 0:ln_], in_=hbb[:, m * T + off:m * T + off + ln_]), inc="v")
            p.wait("vector", "v")
            p.op("vector", lambda e, m=m, off=off, ln_=ln_: e.tensor_mul(
                out=hbb[:, m * T + off:m * T + off + ln_], in0=hch[:, 0:ln_],
                in1=rstdb[:, off:off + ln_]), inc="v")

    p.free("pbb2")
    p.free("rstdb", "sqb2", "msr", "hch")
    HQ = 1024
    NHQ = HQ // P
    wbuf = p.sb("wbufg", [P, KD * HQ], BF16)
    stg = p.sb("stgg", [P, HQ], F32)
    gsil_q = p.sb("gsil_qg", [P, NHQ * 512], BF16)
    mprod_q = p.sb("mprod_qg", [P, NHQ * 512], BF16)
    psg = p.psum("psg", [P, 512], F32)
    for off, ln_ in tok_blocks(T, 512):
        for quar in range(4):
            p.wait("tensor", "v")
            emit_load_wbf(p, wgd[:, quar * HQ:(quar + 1) * HQ], D, HQ, wbuf, stg)
            for m in range(NHQ):
                p.wait("tensor", "v")
                for k in range(KD):
                    p.op("tensor", lambda e, m=m, k=k, off=off, ln_=ln_: e.matmul(
                        out=psg[:, 0:ln_],
                        lhsT=wbuf[:, k * HQ + m * P:k * HQ + (m + 1) * P],
                        rhs=hbb[:, k * T + off:k * T + off + ln_],
                        start=(k == 0), stop=(k == KD - 1)),
                        inc=("t" if k == KD - 1 else None))
                p.wait("scalar", "t")
                p.op("scalar", lambda e, m=m, ln_=ln_: e.activation(
                    out=gsil_q[:, m * 512:m * 512 + ln_], in_=psg[:, 0:ln_],
                    func=AF.Silu), inc="v")
            p.wait("tensor", "v")
            emit_load_wbf(p, wud[:, quar * HQ:(quar + 1) * HQ], D, HQ, wbuf, stg)
            for m in range(NHQ):
                p.wait("tensor", "v")
                for k in range(KD):
                    p.op("tensor", lambda e, m=m, k=k, off=off, ln_=ln_: e.matmul(
                        out=psg[:, 0:ln_],
                        lhsT=wbuf[:, k * HQ + m * P:k * HQ + (m + 1) * P],
                        rhs=hbb[:, k * T + off:k * T + off + ln_],
                        start=(k == 0), stop=(k == KD - 1)),
                        inc=("t" if k == KD - 1 else None))
                p.wait("vector", "t")
                p.op("vector", lambda e, m=m, ln_=ln_: e.tensor_mul(
                    out=mprod_q[:, m * 512:m * 512 + ln_], in0=psg[:, 0:ln_],
                    in1=gsil_q[:, m * 512:m * 512 + ln_]), inc="v")
            p.wait("tensor", "v")
            emit_load_wbf(p, wdd[quar * HQ:(quar + 1) * HQ, :], HQ, D, wbuf, stg)
            for mo in range(KD):
                p.wait("tensor", "v")
                for k in range(NHQ):
                    p.op("tensor", lambda e, mo=mo, k=k, off=off, ln_=ln_: e.matmul(
                        out=psg[:, 0:ln_],
                        lhsT=wbuf[:, k * D + mo * P:k * D + (mo + 1) * P],
                        rhs=mprod_q[:, k * 512:k * 512 + ln_],
                        start=(k == 0), stop=(k == NHQ - 1)),
                        inc=("t" if k == NHQ - 1 else None))
                p.wait("vector", "t")
                p.op("vector", lambda e, mo=mo, off=off, ln_=ln_: e.tensor_add(
                    out=xt[:, mo * T + off:mo * T + off + ln_],
                    in0=xt[:, mo * T + off:mo * T + off + ln_],
                    in1=psg[:, 0:ln_]), inc="v")

    p.wait("sync", "v")
    for k in range(KD):
        p.dma("sync", yd[k * P:(k + 1) * P, :], xt[:, k * T:(k + 1) * T], inc="st")
    p.wait("sync", "st")


def run_l0_attn_b(x, inp):
    pr = _get_prog("l0ab", build_l0_attn_b)
    xs = _fm_slices(x)
    tril = np.tril(np.ones((P, P), np.float32), -1).astype(_bf16np())
    triuSN = (-np.triu(np.ones((P, P), np.float32), 1)).astype(_bf16np())
    triuI = np.triu(np.ones((P, P), np.float32), 0).astype(_bf16np())
    selk = np.zeros((16, D), np.float32)
    for g in range(D):
        selk[g // 64, g] = 1.0
    base = {
        "Wq": np.asarray(inp["L0_attn_b_Wq"], np.float32),
        "Wk": np.asarray(inp["L0_attn_b_Wk"], np.float32),
        "Wv": np.asarray(inp["L0_attn_b_Wv"], np.float32),
        "Wb": np.asarray(inp["L0_attn_b_Wb"], np.float32),
        "Wo": np.asarray(inp["L0_attn_b_Wo"], np.float32),
        "ln1": np.asarray(inp["L0_attn_b_ln1"], np.float32).reshape(D, 1),
        "ln2": np.asarray(inp["L0_attn_b_ln2"], np.float32).reshape(D, 1),
        "Wg": np.asarray(inp["L0_attn_b_Wg"], np.float32),
        "Wu": np.asarray(inp["L0_attn_b_Wu"], np.float32),
        "Wd": np.asarray(inp["L0_attn_b_Wd"], np.float32),
        "trilS": tril, "triuSN": triuSN, "triuI": triuI, "selk": selk,
        **Common.host_inputs(),
    }
    in_maps = []
    for c in range(NCORES):
        pm = np.zeros((P, 8), np.float32)
        for r in range(8):
            if r // 4 == c // 4 and r < c:
                pm[:, r] = 1.0
        in_maps.append({**base, "x": xs[c], "prevmask": pm})
    res = run_bass_kernel_spmd(pr.nc, in_maps, core_ids=list(range(NCORES)))
    return _unshard([res.results[c]["y"] for c in range(NCORES)])


def _bw_list(N=SEQ):
    out = []
    w = 1
    while True:
        den = 2 ** (w - 1)
        num = N - 1
        Bw = (num // den - 1 if num % den == 0 else num // den) - 1
        if Bw < 0:
            break
        out.append(Bw)
        w += 1
    return out


def build_l1_attn_b():
    p = Prog("l1ab")
    cm = Common(p)
    yd = p.dram("y", [D, T], F32, kind="ExternalOutput").ap()
    emit_l1ab(p, cm, yd)
    p.finish()
    return p


def emit_l1ab(p, cm, yd, xin=None):
    xd = (xin if xin is not None
          else p.dram("x", [D, T], F32, kind="ExternalInput").ap())
    wtd = {}
    for nm, dn in (("q", "WqT"), ("k", "WkT"), ("v", "WvT"), ("o", "WoT")):
        wtd[nm] = p.dram(dn, [D, D], F32, kind="ExternalInput").ap()
    coldef = [("bq", "bq"), ("bk", "bk"), ("bv", "bv"), ("bo", "bo"),
              ("gq", "gq"), ("b2q", "b2q"), ("gk", "gk"), ("b2k", "b2k"),
              ("gv", "gv"), ("b2v", "b2v"), ("go", "go"), ("b2o", "b2o")]
    cold = {nm: p.dram(dn, [D, 1], F32, kind="ExternalInput").ap()
            for nm, dn in coldef}
    scd = p.dram("scalec", [P, 1], F32, kind="ExternalInput").ap()
    bmd = p.dram("bmask", [P, 8], F32, kind="ExternalInput").ap()
    vixd = p.dram("vidx", [P, 8], U32, kind="ExternalInput").ap()
    bixd = p.dram("bidx", [P, 8], U32, kind="ExternalInput").ap()
    NC = T // P
    NT = SEQ

    p.constcol(1e-5)
    xnb = p.sb("xnb", [P, KD * T], BF16)
    xt = p.sb("xt", [P, KD * T], F32)
    for k in range(KD):
        p.dma("sync", xt[:, k * T:(k + 1) * T], xd[k * P:(k + 1) * P, :])
    p.wait("vector", "dma")
    p.wait("scalar", "dma")
    p.wait("tensor", "dma")

    rho_bc, mrho_bc, lnn = emit_ln(p, cm, xt, T, "ab2")
    xnf = p.sb("xnf_ab2", [P, T], F32)
    for k in range(KD):
        p.wait("vector", "v")
        p.op("vector", lambda e, k=k: e.scalar_tensor_tensor(
            out=xnf[:], in0=xt[:, k * T:(k + 1) * T], scalar=1.0,
            in1=rho_bc[:], op0=ALU.mult, op1=ALU.mult), inc=None)
        p.op("vector", lambda e, k=k: e.tensor_add(
            out=xnf[:], in0=xnf[:], in1=mrho_bc[:]), inc=None)
        p.op("vector", lambda e, k=k: e.tensor_copy(
            out=xnb[:, k * T:(k + 1) * T], in_=xnf[:]), inc="v")
    p.free("xnf_ab2")
    p.free(*lnn)
    p.free("ln_mrho_ab2", "ln_rho_ab2")
    xspill = p.dram("xspill", [D, T], F32)
    p.wait("sync", "v")
    for k in range(KD):
        p.dma("sync", xspill.ap()[k * P:(k + 1) * P, :],
              xt[:, k * T:(k + 1) * T], inc="sp")
    p.wait("sync", "sp")
    p.free("xt")
    x2b = xnb

    tabs = {}
    for nm, _dn in coldef:
        t = p.sb("tb_" + nm, [P, KD], F32)
        for k in range(KD):
            p.dma("sync", t[:, k:k + 1], cold[nm][k * P:(k + 1) * P, :])
        tabs[nm] = t
    scol = p.sb("tb_scale", [P, 1], F32)
    p.dma("sync", scol[:], scd)
    bmc = p.sb("tb_bmask", [P, 8], F32)
    p.dma("sync", bmc[:], bmd)
    vix = p.sb("tb_vidx", [P, 8], U32)
    p.dma("sync", vix[:], vixd)
    bix = p.sb("tb_bidx", [P, 8], U32)
    p.dma("sync", bix[:], bixd)
    p.wait("vector", "dma")
    p.wait("scalar", "dma")
    p.wait("tensor", "dma")

    Qf = p.sb("Qf", [P, KD * T], BF16)
    Ofm = p.sb("Ofm", [P, KD * T], BF16)
    KtVg = p.sb("KtVg", [P, KD * 64], F32)
    colVg = p.sb("colVg", [P, KD], F32)
    c3t = p.sb("c3t", [P, KD], F32)
    combT = p.sb("combT", [P, KD * 64], F32)
    combH = p.sb("combH", [P, KD * 64], BF16)
    combL = p.sb("combL", [P, KD * 64], BF16)
    Kb = p.sb("Kb", [P, KD * T], BF16)
    Vb = p.sb("Vb", [P, KD * T], BF16)

    ps1 = p.psum("ps1", [P, 512], F32)
    pb = p.psum("pb", [P, 512], F32)

    def mk_pw(sfx):
        pw = {"sfx": sfx}
        pw["wb"] = p.sb("pw_wb" + sfx, [P, KD * P], BF16)
        pw["stg"] = p.sb("pw_stg" + sfx, [P, P], F32)
        pw["prj"] = p.sb("pw_prj" + sfx, [P, 512], F32)
        pw["sq"] = p.sb("pw_sq" + sfx, [P, 512], F32)
        pw["srow"] = p.sb("pw_srow" + sfx, [1, T], F32)
        pw["qrow"] = p.sb("pw_qrow" + sfx, [1, T], F32)
        pw["m2r"] = p.sb("pw_m2r" + sfx, [1, T], F32)
        pw["rho2b"] = p.sb("pw_rho2" + sfx, [P, 512], F32)
        pw["mrho2b"] = p.sb("pw_mrho2" + sfx, [P, 512], F32)
        pw["pss"] = p.psum("pss" + sfx, [1, 512], F32)
        pw["psq"] = p.psum("psq" + sfx, [1, 512], F32)
        return pw

    def free_pw(pw):
        sfx = pw["sfx"]
        p.free("psq" + sfx, "pss" + sfx)
        p.free("pw_mrho2" + sfx, "pw_rho2" + sfx, "pw_m2r" + sfx,
               "pw_qrow" + sfx, "pw_srow" + sfx,
               "pw_sq" + sfx, "pw_prj" + sfx, "pw_stg" + sfx, "pw_wb" + sfx)

    def load_wT(pw, Wd_, m):
        for k in range(KD):
            p.wait("sync", "v")
            p.wait("sync", "t")
            p.dma("sync", pw["stg"][:, :],
                  Wd_[k * P:(k + 1) * P, m * P:(m + 1) * P], inc="dma")
            p.wait("vector", "dma")
            p.op("vector", lambda e, k=k: e.tensor_copy(
                out=pw["wb"][:, k * P:(k + 1) * P], in_=pw["stg"][:, :]),
                inc="v")

    def proj_mm(pw, src, m, off, ln_):
        p.wait("tensor", "v")
        for k in range(KD):
            p.op("tensor", lambda e, m=m, k=k, off=off, ln_=ln_: e.matmul(
                out=ps1[:, 0:ln_], lhsT=pw["wb"][:, k * P:(k + 1) * P],
                rhs=src[:, k * T + off:k * T + off + ln_],
                start=(k == 0), stop=(k == KD - 1)),
                inc=("t" if k == KD - 1 else None))

    def proj_stats(pw, Wd_, bt, src):
        for m in range(KD):
            load_wT(pw, Wd_, m)
            for off, ln_ in tok_blocks(T, 512):
                proj_mm(pw, src, m, off, ln_)
                p.wait("vector", "t")
                p.op("vector", lambda e, m=m, ln_=ln_: e.tensor_scalar(
                    out=pw["prj"][:, 0:ln_], in0=ps1[:, 0:ln_],
                    scalar1=bt[:, m:m + 1], scalar2=None, op0=ALU.add), inc="v")
                p.wait("vector", "v")
                p.wait("scalar", "v")
                p.op("scalar", lambda e, ln_=ln_: e.activation(
                    out=pw["sq"][:, 0:ln_], in_=pw["prj"][:, 0:ln_],
                    func=AF.Square), inc="s")
                p.wait("tensor", "s")
                p.op("tensor", lambda e, ln_=ln_: e.matmul(
                    out=pw["pss"][0:1, 0:ln_], lhsT=cm.ones_col[:, 0:1],
                    rhs=pw["prj"][:, 0:ln_], start=True, stop=True), inc=None)
                p.op("tensor", lambda e, ln_=ln_: e.matmul(
                    out=pw["psq"][0:1, 0:ln_], lhsT=cm.ones_col[:, 0:1],
                    rhs=pw["sq"][:, 0:ln_], start=True, stop=True), inc="t")
                p.wait("vector", "t")
                if m == 0:
                    p.op("vector", lambda e, off=off, ln_=ln_: e.tensor_copy(
                        out=pw["srow"][:, off:off + ln_],
                        in_=pw["pss"][0:1, 0:ln_]), inc=None)
                    p.op("vector", lambda e, off=off, ln_=ln_: e.tensor_copy(
                        out=pw["qrow"][:, off:off + ln_],
                        in_=pw["psq"][0:1, 0:ln_]), inc="v")
                else:
                    p.op("vector", lambda e, off=off, ln_=ln_: e.tensor_add(
                        out=pw["srow"][:, off:off + ln_],
                        in0=pw["srow"][:, off:off + ln_],
                        in1=pw["pss"][0:1, 0:ln_]), inc=None)
                    p.op("vector", lambda e, off=off, ln_=ln_: e.tensor_add(
                        out=pw["qrow"][:, off:off + ln_],
                        in0=pw["qrow"][:, off:off + ln_],
                        in1=pw["psq"][0:1, 0:ln_]), inc="v")
                p.wait("tensor", "v")

    def finalize_rows(pw):
        epsc = p.constcol(1e-5)
        p.wait("vector", "t")
        p.wait("vector", "s")
        p.op("vector", lambda e: e.tensor_scalar(
            out=pw["srow"][:], in0=pw["srow"][0:1, :], scalar1=1.0 / D,
            scalar2=None, op0=ALU.mult), inc="v")
        p.wait("scalar", "v")
        p.op("scalar", lambda e: e.activation(out=pw["m2r"][:],
                                              in_=pw["srow"][:],
                                              func=AF.Square), inc="s")
        p.wait("vector", "s")
        p.op("vector", lambda e: e.scalar_tensor_tensor(
            out=pw["qrow"][:], in0=pw["qrow"][0:1, :], scalar=1.0 / D,
            in1=pw["m2r"][:], op0=ALU.mult, op1=ALU.subtract), inc="v")
        p.wait("scalar", "v")
        p.op("scalar", lambda e, epsc=epsc: e.activation(
            out=pw["qrow"][:], in_=pw["qrow"][:], func=AF.Ln,
            bias=epsc[0:1, 0:1]), inc="s")
        p.wait("scalar", "s")
        p.op("scalar", lambda e: e.activation(out=pw["qrow"][:],
                                              in_=pw["qrow"][:],
                                              func=AF.Exp, scale=-0.5), inc="s")
        p.wait("vector", "s")
        p.op("vector", lambda e: e.tensor_mul(out=pw["m2r"][:],
                                              in0=pw["srow"][:],
                                              in1=pw["qrow"][:]), inc="v")
        p.wait("tensor", "v")

    def proj_apply(pw, Wd_, bt, src, gcol, b2col, writer):
        for m in range(KD):
            load_wT(pw, Wd_, m)
            for off, ln_ in tok_blocks(T, 512):
                proj_mm(pw, src, m, off, ln_)
                p.wait("vector", "t")
                p.op("vector", lambda e, m=m, ln_=ln_: e.tensor_scalar(
                    out=pw["prj"][:, 0:ln_], in0=ps1[:, 0:ln_],
                    scalar1=bt[:, m:m + 1], scalar2=None, op0=ALU.add), inc="v")
                p.wait("vector", "v")
                p.wait("tensor", "v")
                p.op("tensor", lambda e, off=off, ln_=ln_: e.matmul(
                    out=pb[:, 0:ln_], lhsT=cm.ones_row[0:1, :],
                    rhs=pw["qrow"][0:1, off:off + ln_],
                    start=True, stop=True), inc="t")
                p.wait("vector", "t")
                p.op("vector", lambda e, ln_=ln_: e.tensor_copy(
                    out=pw["rho2b"][:, 0:ln_], in_=pb[:, 0:ln_]), inc="v")
                p.wait("tensor", "v")
                p.op("tensor", lambda e, off=off, ln_=ln_: e.matmul(
                    out=pb[:, 0:ln_], lhsT=cm.ones_row[0:1, :],
                    rhs=pw["m2r"][0:1, off:off + ln_],
                    start=True, stop=True), inc="t")
                p.wait("vector", "t")
                p.op("vector", lambda e, ln_=ln_: e.tensor_scalar(
                    out=pw["mrho2b"][:, 0:ln_], in0=pb[:, 0:ln_],
                    scalar1=-1.0, scalar2=None, op0=ALU.mult), inc="v")
                p.wait("vector", "v")
                p.op("vector", lambda e, ln_=ln_: e.tensor_mul(
                    out=pw["prj"][:, 0:ln_], in0=pw["prj"][:, 0:ln_],
                    in1=pw["rho2b"][:, 0:ln_]), inc=None)
                p.op("vector", lambda e, ln_=ln_: e.tensor_add(
                    out=pw["prj"][:, 0:ln_], in0=pw["prj"][:, 0:ln_],
                    in1=pw["mrho2b"][:, 0:ln_]), inc=None)
                p.op("vector", lambda e, m=m, ln_=ln_: e.tensor_scalar(
                    out=pw["prj"][:, 0:ln_], in0=pw["prj"][:, 0:ln_],
                    scalar1=gcol[:, m:m + 1], scalar2=None, op0=ALU.mult),
                    inc=None)
                p.op("vector", lambda e, m=m, ln_=ln_: e.tensor_scalar(
                    out=pw["prj"][:, 0:ln_], in0=pw["prj"][:, 0:ln_],
                    scalar1=b2col[:, m:m + 1], scalar2=None, op0=ALU.add),
                    inc="v")
                p.wait("vector", "v")
                writer(m, off, ln_)

    def mk_bin_writer(pw, dst):
        def w(m, off, ln_):
            p.op("vector", lambda e, m=m, off=off, ln_=ln_: e.tensor_scalar(
                out=dst[:, m * T + off:m * T + off + ln_],
                in0=pw["prj"][:, 0:ln_], scalar1=1.0, scalar2=None,
                op0=ALU.is_ge), inc="v")
            p.wait("tensor", "v")
        return w

    pw1 = mk_pw("")
    proj_stats(pw1, wtd["q"], tabs["bq"], x2b)
    finalize_rows(pw1)
    proj_apply(pw1, wtd["q"], tabs["bq"], x2b, tabs["gq"], tabs["b2q"],
               mk_bin_writer(pw1, Qf))
    proj_stats(pw1, wtd["k"], tabs["bk"], x2b)
    finalize_rows(pw1)
    proj_apply(pw1, wtd["k"], tabs["bk"], x2b, tabs["gk"], tabs["b2k"],
               mk_bin_writer(pw1, Kb))
    proj_stats(pw1, wtd["v"], tabs["bv"], x2b)
    finalize_rows(pw1)
    proj_apply(pw1, wtd["v"], tabs["bv"], x2b, tabs["gv"], tabs["b2v"],
               mk_bin_writer(pw1, Vb))

    free_pw(pw1)
    p.fence_dma()

    _, dV = emit_allgather(p, Vb, KD * T, "vv", npart=P, dtype=BF16,
                           readback=False)

    colVl = p.sb("pt_colVl", [P, KD], F32)
    vscr = p.sb("pt_vscr", [P, T], BF16)
    KtVl = p.sb("pt_KtVl", [64, 16 * 64], F32)
    KtVs = p.sb("pt_KtVs", [64, 16 * 64], F32)
    gstg = p.sb("pt_gstg", [64, 16 * 64], F32)
    cstg = p.sb("pt_cstg", [P, KD], F32)
    ktm = p.sb("pt_ktm", [P, 64], BF16)
    vtm = p.sb("pt_vtm", [P, 64], BF16)
    rowc = p.sb("pt_rowc", [1, 16 * 64], F32)
    rbc = p.sb("pt_rbc", [P, 64], F32)
    ptA = p.psum("ptA", [P, 64], BF16)
    ptB = p.psum("ptB", [P, 64], BF16)
    pkv = p.psum("pkv", [P, 64], F32)
    pko = p.psum("pko", [P, 64], F32)
    psrA = p.psum("psrA", [1, 64], F32)
    psrB = p.psum("psrB", [1, 64], F32)

    for k in range(KD):
        p.wait("vector", "v")
        p.op("vector", lambda e, k=k: e.scalar_tensor_tensor(
            out=vscr[:], in0=Vb[:, k * T:(k + 1) * T], scalar=1.0,
            in1=Vb[:, k * T:(k + 1) * T], op0=ALU.mult, op1=ALU.mult,
            accum_out=colVl[:, k:k + 1]), inc="v")

    for h in range(16):
        base, k = (h % 2) * 64, h // 2
        ptX = ptA if h % 2 == 0 else ptB
        for cc in range(NC):
            p.wait("tensor", "v")
            p.op("tensor", lambda e, base=base, k=k, cc=cc, ptX=ptX: e.transpose(
                out=ptX[0:P, 0:64], in_=Kb[base:base + 64, k * T + cc * P:k * T + (cc + 1) * P],
                identity=cm.ident_bf[base:base + 64, base:base + 64]), inc="t")
            p.wait("vector", "t")
            p.op("vector", lambda e, ptX=ptX: e.tensor_copy(
                out=ktm[:, 0:64], in_=ptX[0:P, 0:64]), inc="v")
            p.wait("tensor", "v")
            p.op("tensor", lambda e, base=base, k=k, cc=cc, ptX=ptX: e.transpose(
                out=ptX[0:P, 0:64], in_=Vb[base:base + 64, k * T + cc * P:k * T + (cc + 1) * P],
                identity=cm.ident_bf[base:base + 64, base:base + 64]), inc="t")
            p.wait("vector", "t")
            p.op("vector", lambda e, ptX=ptX: e.tensor_copy(
                out=vtm[:, 0:64], in_=ptX[0:P, 0:64]), inc="v")
            p.wait("tensor", "v")
            p.op("tensor", lambda e, cc=cc: e.matmul(
                out=pkv[0:64, 0:64], lhsT=ktm[:, 0:64], rhs=vtm[:, 0:64],
                start=(cc == 0), stop=(cc == NC - 1)),
                inc=("t" if cc == NC - 1 else None))
        p.wait("vector", "t")
        p.op("vector", lambda e, h=h: e.tensor_copy(
            out=KtVl[0:64, h * 64:(h + 1) * 64], in_=pkv[0:64, 0:64]), inc="v")
        p.wait("tensor", "v")

    _, dC = emit_allgather(p, colVl, KD, "cv", npart=P, readback=False)
    _, dK = emit_allgather(p, KtVl, 16 * 64, "kv", npart=64, readback=False)
    p.op("vector", lambda e: e.memset(colVg[:], 0.0), inc="v")
    p.op("vector", lambda e: e.memset(KtVs[:], 0.0), inc="v")
    p.wait("vector", "v")
    for r in range(8):
        p.wait("sync", "v")
        p.wait("sync", "cc")
        p.dma("sync", cstg[:, :], dC.ap()[r * P:(r + 1) * P, :], inc="dma")
        p.dma("sync", gstg[:, :], dK.ap()[r * 64:(r + 1) * 64, :], inc="dma")
        p.wait("vector", "dma")
        p.op("vector", lambda e, r=r: e.scalar_tensor_tensor(
            out=colVg[:], in0=cstg[:], scalar=bmc[:, r:r + 1], in1=colVg[:],
            op0=ALU.mult, op1=ALU.add), inc=None)
        p.op("vector", lambda e, r=r: e.scalar_tensor_tensor(
            out=KtVs[:], in0=gstg[:], scalar=bmc[0:64, r:r + 1], in1=KtVs[:],
            op0=ALU.mult, op1=ALU.add), inc="v")
        p.wait("vector", "v")

    for h in range(16):
        base, k = (h % 2) * 64, h // 2
        if base == 0:
            p.op("vector", lambda e, h=h, k=k: e.tensor_copy(
                out=KtVg[0:64, k * 64:(k + 1) * 64],
                in_=KtVs[0:64, h * 64:(h + 1) * 64]), inc="v")
        else:
            p.wait("sync", "v")
            p.dma("sync", KtVg[64:128, k * 64:(k + 1) * 64],
                  KtVs[0:64, h * 64:(h + 1) * 64], inc="dma")
    p.wait("vector", "dma")
    p.wait("tensor", "dma")
    p.wait("tensor", "v")
    for h in range(16):
        base, k = (h % 2) * 64, h // 2
        psk = pkv if base == 0 else pko
        p.op("tensor", lambda e, base=base, k=k, psk=psk: e.matmul(
            out=psk[base:base + 64, 0:1],
            lhsT=KtVg[base:base + 64, k * 64:(k + 1) * 64],
            rhs=cm.ones_col[base:base + 64, 0:1], start=True, stop=True), inc="t")
        p.wait("vector", "t")
        p.op("vector", lambda e, base=base, k=k, psk=psk: e.tensor_copy(
            out=c3t[base:base + 64, k:k + 1], in_=psk[base:base + 64, 0:1]),
            inc="v")
        p.wait("tensor", "v")
    p.wait("vector", "v")
    p.op("vector", lambda e: e.scalar_tensor_tensor(
        out=c3t[:], in0=colVg[:], scalar=64.0, in1=c3t[:],
        op0=ALU.mult, op1=ALU.subtract), inc="v")
    p.wait("tensor", "v")
    for h in range(16):
        base, k = (h % 2) * 64, h // 2
        psr = psrA if base == 0 else psrB
        p.op("tensor", lambda e, base=base, k=k, psr=psr: e.transpose(
            out=psr[0:1, 0:64], in_=colVg[base:base + 64, k:k + 1],
            identity=cm.ident_f[base:base + 64, base:base + 64]), inc="t")
        p.wait("vector", "t")
        p.op("vector", lambda e, h=h, psr=psr: e.tensor_copy(
            out=rowc[0:1, h * 64:(h + 1) * 64], in_=psr[0:1, 0:64]), inc="v")
        p.wait("tensor", "v")
    for h in range(16):
        base, k = (h % 2) * 64, h // 2
        p.op("tensor", lambda e, h=h: e.matmul(
            out=pb[:, 0:64], lhsT=cm.ones_row[0:1, :],
            rhs=rowc[0:1, h * 64:(h + 1) * 64], start=True, stop=True), inc="t")
        p.wait("vector", "t")
        p.op("vector", lambda e: e.tensor_copy(out=rbc[:, 0:64],
                                               in_=pb[:, 0:64]), inc="v")
        p.wait("vector", "v")
        p.op("vector", lambda e, base=base, k=k: e.scalar_tensor_tensor(
            out=combT[base:base + 64, k * 64:(k + 1) * 64],
            in0=KtVg[base:base + 64, k * 64:(k + 1) * 64], scalar=2.0,
            in1=rbc[base:base + 64, 0:64], op0=ALU.mult, op1=ALU.subtract),
            inc="v")
        p.wait("tensor", "v")

    csp1 = p.sb("pt_csp1", [P, KD * 64], F32)
    p.wait("vector", "v")
    p.op("vector", lambda e: e.tensor_scalar(
        out=csp1[:], in0=combT[:], scalar1=1.0 / 256.0, scalar2=8388608.0,
        op0=ALU.mult, op1=ALU.add), inc="v")
    p.wait("vector", "v")
    p.op("vector", lambda e: e.tensor_scalar(
        out=csp1[:], in0=csp1[:], scalar1=-8388608.0, scalar2=256.0,
        op0=ALU.add, op1=ALU.mult), inc="v")
    p.wait("vector", "v")
    p.op("vector", lambda e: e.tensor_copy(out=combH[:], in_=csp1[:]), inc=None)
    p.op("vector", lambda e: e.tensor_sub(
        out=combL[:], in0=combT[:], in1=csp1[:]), inc="v")
    p.wait("vector", "v")

    p.free("psrB", "psrA", "pko", "pkv", "ptB", "ptA")
    p.free("pt_csp1")
    p.free("pt_rbc", "pt_rowc", "pt_vtm", "pt_ktm", "pt_cstg", "pt_gstg",
           "pt_KtVs", "pt_KtVl", "pt_vscr", "pt_colVl")
    p.free("Vb", "Kb")
    p.fence_dma()
    p.wait("vector", "cc")
    p.wait("scalar", "cc")
    p.wait("tensor", "cc")

    onesT = p.sb("bv_ones", [P, T], F32)
    Pfull = p.sb("bv_P", [P, NT], F32)
    biasVj = p.sb("bv_b", [P, NT], F32)
    vstg = p.sb("bv_vstg", [P, T], BF16)
    vf = p.sb("bv_vf", [P, T], F32)
    offc = p.sb("bv_off", [P, 1], F32)
    p.op("vector", lambda e: e.memset(onesT[:], 1.0), inc="v")
    p.wait("vector", "v")
    dB = []
    for j in range(2):
        if j == 1:
            p.wait("vector", "ag")
        for blk in range(4):
            col = j * 4 + blk
            p.wait("gpsimd", "cc")
            p.wait("gpsimd", "v")
            p.op("gpsimd", lambda e, col=col: e.indirect_dma_start(
                out=vstg[:, :], out_offset=None, in_=dV.ap(),
                in_offset=IndirectOffsetOnAxis(ap=vix[:, col:col + 1], axis=1)),
                inc="ig", amt=16)
            p.wait("vector", "ig")
            p.op("vector", lambda e: e.tensor_copy(out=vf[:], in_=vstg[:]),
                 inc="v")
            p.wait("vector", "v")
            p.op("vector", lambda e, blk=blk: e.tensor_tensor_scan(
                out=Pfull[:, blk * T:(blk + 1) * T], data0=onesT[:],
                data1=vf[:], initial=0.0, op0=ALU.mult, op1=ALU.add), inc="v")
            p.wait("vector", "v")
            if blk > 0:
                p.op("vector", lambda e, blk=blk: e.tensor_scalar(
                    out=Pfull[:, blk * T:(blk + 1) * T],
                    in0=Pfull[:, blk * T:(blk + 1) * T], scalar1=offc[:, 0:1],
                    scalar2=None, op0=ALU.add), inc="v")
                p.wait("vector", "v")
            if blk < 3:
                p.op("vector", lambda e, blk=blk: e.tensor_copy(
                    out=offc[:, 0:1],
                    in_=Pfull[:, (blk + 1) * T - 1:(blk + 1) * T]), inc="v")
                p.wait("vector", "v")
        p.op("vector", lambda e: e.memset(biasVj[:], 0.0), inc="v")
        p.wait("vector", "v")
        for Bw in _bw_list(NT):
            hi_len = NT - 1 - Bw
            if hi_len > 0:
                p.op("vector", lambda e, Bw=Bw, hi_len=hi_len: e.tensor_add(
                    out=biasVj[:, 0:hi_len], in0=biasVj[:, 0:hi_len],
                    in1=Pfull[:, Bw:Bw + hi_len]), inc="v")
                p.wait("vector", "v")
            p.op("vector", lambda e, hi_len=hi_len: e.tensor_scalar(
                out=biasVj[:, hi_len:NT], in0=biasVj[:, hi_len:NT],
                scalar1=Pfull[:, NT - 1:NT], scalar2=None, op0=ALU.add), inc="v")
            p.wait("vector", "v")
            lo = Bw + 1
            p.op("vector", lambda e, lo=lo: e.tensor_sub(
                out=biasVj[:, lo:NT], in0=biasVj[:, lo:NT],
                in1=Pfull[:, 0:NT - lo]), inc="v")
            p.wait("vector", "v")
        _, dj = emit_allgather(p, biasVj, NT, f"b{j}", npart=P, readback=False)
        dB.append(dj)

    p.free("bv_off", "bv_vf", "bv_vstg", "bv_b", "bv_P", "bv_ones")
    p.fence_dma()
    p.wait("vector", "cc")
    p.wait("tensor", "cc")

    bVk = p.sb("pa_bVk", [P, T], F32)
    otmp = p.sb("pa_otmp", [P, 512], F32)
    psE = p.psum("psE", [P, 512], F32)
    for k in range(KD):
        p.wait("gpsimd", "v")
        p.wait("gpsimd", "cc")
        p.op("gpsimd", lambda e, k=k: e.indirect_dma_start(
            out=bVk[:, :], out_offset=None, in_=dB[k % 2].ap(),
            in_offset=IndirectOffsetOnAxis(ap=bix[:, k:k + 1], axis=1)),
            inc="ig", amt=16)
        p.wait("vector", "ig")
        p.wait("tensor", "ig")
        for sub in (0, 1):
            base = sub * 64
            ps = ps1 if sub == 0 else psE
            for off, ln_ in tok_blocks(T, 512):
                p.wait("tensor", "v")
                p.op("tensor", lambda e, base=base, k=k, off=off, ln_=ln_, ps=ps: e.matmul(
                    out=ps[base:base + 64, 0:ln_],
                    lhsT=combH[base:base + 64, k * 64:(k + 1) * 64],
                    rhs=Qf[base:base + 64, k * T + off:k * T + off + ln_],
                    start=True, stop=False), inc=None)
                p.op("tensor", lambda e, base=base, k=k, off=off, ln_=ln_, ps=ps: e.matmul(
                    out=ps[base:base + 64, 0:ln_],
                    lhsT=combL[base:base + 64, k * 64:(k + 1) * 64],
                    rhs=Qf[base:base + 64, k * T + off:k * T + off + ln_],
                    start=False, stop=True), inc="t")
                p.wait("vector", "t")
                p.op("vector", lambda e, base=base, k=k, ln_=ln_, ps=ps: e.tensor_scalar(
                    out=otmp[base:base + 64, 0:ln_], in0=ps[base:base + 64, 0:ln_],
                    scalar1=c3t[base:base + 64, k:k + 1], scalar2=None,
                    op0=ALU.add), inc="v")
                p.wait("vector", "v")
                p.op("vector", lambda e, base=base, off=off, ln_=ln_: e.tensor_add(
                    out=otmp[base:base + 64, 0:ln_],
                    in0=otmp[base:base + 64, 0:ln_],
                    in1=bVk[base:base + 64, off:off + ln_]), inc="v")
                p.wait("vector", "v")
                p.op("vector", lambda e, base=base, k=k, off=off, ln_=ln_: e.tensor_scalar(
                    out=Ofm[base:base + 64, k * T + off:k * T + off + ln_],
                    in0=otmp[base:base + 64, 0:ln_],
                    scalar1=scol[base:base + 64, 0:1], scalar2=0.5,
                    op0=ALU.mult, op1=ALU.is_ge), inc="v")
                p.wait("tensor", "v")

    p.free("psE")
    p.free("pa_otmp", "pa_bVk")
    p.free("combL", "combH", "combT", "c3t", "colVg", "KtVg")

    Yb = p.sb("Yb", [P, KD * T], BF16)
    pw2 = mk_pw("2")

    def o_writer(m, off, ln_):
        p.op("vector", lambda e, m=m, off=off, ln_=ln_: e.tensor_scalar(
            out=Yb[:, m * T + off:m * T + off + ln_], in0=pw2["prj"][:, 0:ln_],
            scalar1=1.0, scalar2=None, op0=ALU.is_ge), inc="v")
        p.wait("tensor", "v")

    proj_stats(pw2, wtd["o"], tabs["bo"], Ofm)
    finalize_rows(pw2)
    proj_apply(pw2, wtd["o"], tabs["bo"], Ofm, tabs["go"], tabs["b2o"],
               o_writer)

    xld = p.sb("xld", [P, T], F32)
    yout = p.sb("yout", [P, T], F32)
    for k in range(KD):
        p.wait("sync", "v")
        p.dma("sync", xld[:, :], xspill.ap()[k * P:(k + 1) * P, :], inc="sp")
        p.wait("vector", "sp")
        p.wait("vector", "st")
        p.op("vector", lambda e, k=k: e.tensor_add(
            out=yout[:], in0=xld[:], in1=Yb[:, k * T:(k + 1) * T]), inc="v")
        p.wait("sync", "v")
        p.dma("sync", yd[k * P:(k + 1) * P, :], yout[:], inc="st")
    p.wait("sync", "st")


def _l1ab_base(inp):
    def wT(nm):
        return np.ascontiguousarray(
            np.asarray(inp[f"L1_attn_b_{nm}"], np.float32).T)

    def col(nm):
        return np.asarray(inp[f"L1_attn_b_{nm}"], np.float32).reshape(D, 1)
    base = {
        "WqT": wT("Wq"), "WkT": wT("Wk"), "WvT": wT("Wv"), "WoT": wT("Wo"),
        "bq": col("bq"), "bk": col("bk"), "bv": col("bv"), "bo": col("bo"),
        "gq": col("gq"), "b2q": col("betaq"), "gk": col("gk"),
        "b2k": col("betak"), "gv": col("gv"), "b2v": col("betav"),
        "go": col("go"), "b2o": col("betao"),
        "scalec": np.full((P, 1), np.float32(inp["L1_attn_b_scale"]),
                          np.float32),
        **Common.host_inputs(),
    }
    return base


def _l1ab_maps(inp, x):
    xs = _fm_slices(x)
    base = _l1ab_base(inp)
    maps = []
    pcol = np.arange(P, dtype=np.uint64)
    for c in range(NCORES):
        b, co = c // 4, c % 4
        bm = np.zeros((P, 8), np.float32)
        for r in range(8):
            if r // 4 == b:
                bm[:, r] = 1.0
        vidx = np.zeros((P, 8), np.uint32)
        for j in range(2):
            for rr in range(4):
                vidx[:, j * 4 + rr] = (((4 * b + rr) * 128 + pcol) * (KD * T)
                                       + (2 * co + j) * T).astype(np.uint32)
        bidx = np.zeros((P, 8), np.uint32)
        for k in range(KD):
            bidx[:, k] = (((4 * b + k // 2) * 128 + pcol) * SEQ
                          + co * T).astype(np.uint32)
        maps.append({**base, "x": xs[c], "bmask": bm, "vidx": vidx,
                     "bidx": bidx})
    return maps


def run_l1_attn_b(x, inp):
    pr = _get_prog("l1ab", build_l1_attn_b)
    in_maps = _l1ab_maps(inp, x)
    res = run_bass_kernel_spmd(pr.nc, in_maps, core_ids=list(range(NCORES)))
    return _unshard([res.results[c]["y"] for c in range(NCORES)])


def build_fused():
    p = Prog("fused")
    cm = Common(p)
    p.constcol(1e-5)
    p.constcol(1e-6, 2)
    yint = [p.dram(f"yint{i}", [D, T], F32) for i in range(7)]
    yfin = p.dram("y", [D, T], F32, kind="ExternalOutput")
    seq = [
        ("l0aa_", emit_l0aa, None, yint[0]),
        ("s1_", emit_l0ffn, yint[0], yint[1]),
        ("l0ab_", emit_l0ab, yint[1], yint[2]),
        ("s3_", emit_l0ffn, yint[2], yint[3]),
        ("l1aa_", emit_l1aa, yint[3], yint[4]),
        ("s5_", emit_l1ffn, yint[4], yint[5]),
        ("l1ab_", emit_l1ab, yint[5], yint[6]),
        ("s7_", emit_l1ffn, yint[6], yfin),
    ]
    for pre, fn, xin, yout in seq:
        before = p.begin_epoch(pre)
        for s in ("v", "t", "s", "cc", "ag", "ig", "sp", "rs", "st", "dma"):
            if s in p.cnt:
                p.wait("sync", s)
        fn(p, cm, yout.ap(), xin=(xin.ap() if xin is not None else None))
        p.end_epoch(before)
    p.finish()
    return p


def _prevmask(c):
    pm = np.zeros((P, 8), np.float32)
    for r in range(8):
        if r // 4 == c // 4 and r < c:
            pm[:, r] = 1.0
    return pm


def _fused_maps(inp):
    x = np.asarray(inp["L0_attn_a_u"], np.float32)
    xs_ext = _fm_slices_ext(x)
    keys = np.asarray(inp["L0_ffn_keys"], np.float32)
    keysT = np.ascontiguousarray(keys.transpose(2, 0, 3, 1).reshape(512, 64))
    tril = np.tril(np.ones((P, P), np.float32), -1).astype(_bf16np())
    triuSN = (-np.triu(np.ones((P, P), np.float32), 1)).astype(_bf16np())
    triuI = np.triu(np.ones((P, P), np.float32), 0).astype(_bf16np())
    selk = np.zeros((16, D), np.float32)
    for g in range(D):
        selk[g // 64, g] = 1.0
    base = {**Common.host_inputs()}
    base.update({"l0aa_" + k: v for k, v in {
        "Ain": np.asarray(inp["L0_attn_a_A"], np.float32).reshape(1, 64),
        "Bin": np.asarray(inp["L0_attn_a_B"], np.float32),
        "Cin": np.asarray(inp["L0_attn_a_C"], np.float32),
        "Dvec": np.asarray(inp["L0_attn_a_D"], np.float32).reshape(D, 1),
        "log_dt": np.asarray(inp["L0_attn_a_log_dt"], np.float32).reshape(D, 1),
    }.items()})
    ffn0 = {
        "Wg": np.asarray(inp["L0_ffn_Wg"], np.float32),
        "Wu": np.asarray(inp["L0_ffn_Wu"], np.float32),
        "Wd": np.asarray(inp["L0_ffn_Wd"], np.float32),
        "bg": np.asarray(inp["L0_ffn_bg"], np.float32).reshape(4096, 1),
        "bu": np.asarray(inp["L0_ffn_bu"], np.float32).reshape(4096, 1),
        "bd": np.asarray(inp["L0_ffn_bd"], np.float32).reshape(D, 1),
        "Wq": np.asarray(inp["L0_ffn_Wq"], np.float32),
        "keysT": keysT,
        "down_embed": np.asarray(inp["L0_ffn_down_embed"], np.float32),
        "up_embed": np.asarray(inp["L0_ffn_up_embed"], np.float32),
    }
    base.update({"s1_" + k: v for k, v in ffn0.items()})
    base.update({"s3_" + k: v for k, v in ffn0.items()})
    base.update({"l0ab_" + k: v for k, v in {
        "Wq": np.asarray(inp["L0_attn_b_Wq"], np.float32),
        "Wk": np.asarray(inp["L0_attn_b_Wk"], np.float32),
        "Wv": np.asarray(inp["L0_attn_b_Wv"], np.float32),
        "Wb": np.asarray(inp["L0_attn_b_Wb"], np.float32),
        "Wo": np.asarray(inp["L0_attn_b_Wo"], np.float32),
        "ln1": np.asarray(inp["L0_attn_b_ln1"], np.float32).reshape(D, 1),
        "ln2": np.asarray(inp["L0_attn_b_ln2"], np.float32).reshape(D, 1),
        "Wg": np.asarray(inp["L0_attn_b_Wg"], np.float32),
        "Wu": np.asarray(inp["L0_attn_b_Wu"], np.float32),
        "Wd": np.asarray(inp["L0_attn_b_Wd"], np.float32),
        "trilS": tril, "triuSN": triuSN, "triuI": triuI, "selk": selk,
    }.items()})
    base.update({"l1aa_" + k: v for k, v in {
        "conv_w": np.asarray(inp["L1_attn_a_conv_weight"], np.float32).reshape(D, 4),
        "conv_b": np.asarray(inp["L1_attn_a_conv_bias"], np.float32).reshape(D, 1),
        "Wr_w": np.asarray(inp["L1_attn_a_Wr_w"], np.float32),
        "Wr_b": np.asarray(inp["L1_attn_a_Wr_b"], np.float32).reshape(D, 1),
        "Wi_w": np.asarray(inp["L1_attn_a_Wi_w"], np.float32),
        "Wi_b": np.asarray(inp["L1_attn_a_Wi_b"], np.float32).reshape(D, 1),
        "log_a": np.asarray(inp["L1_attn_a_log_a"], np.float32).reshape(D, 1),
        "out_w": np.asarray(inp["L1_attn_a_out_w"], np.float32),
        "norm_w": np.asarray(inp["L1_attn_a_norm_w"], np.float32).reshape(D, 1),
    }.items()})
    ffn1 = {
        "router_w": np.asarray(inp["L1_ffn_router_w"], np.float32),
        "router_bias": np.asarray(inp["L1_ffn_router_bias"], np.float32).reshape(1, 8),
        "sw1": np.asarray(inp["L1_ffn_sw1"], np.float32),
        "sw3": np.asarray(inp["L1_ffn_sw3"], np.float32),
        "sw2": np.asarray(inp["L1_ffn_sw2"], np.float32),
        "ew1": np.asarray(inp["L1_ffn_ew1"], np.float32).reshape(8 * D, D),
        "ew2": np.asarray(inp["L1_ffn_ew2"], np.float32).reshape(8 * D, D),
        "eye8": np.eye(8, dtype=np.float32),
    }
    base.update({"s5_" + k: v for k, v in ffn1.items()})
    base.update({"s7_" + k: v for k, v in ffn1.items()})
    l1ab = _l1ab_base(inp)
    l1ab.pop("c_ident", None)
    base.update({"l1ab_" + k: v for k, v in l1ab.items()})

    maps = []
    pcol = np.arange(P, dtype=np.uint64)
    for c in range(NCORES):
        b, co = c // 4, c % 4
        m = dict(base)
        m["l0aa_x"] = xs_ext[c]
        pm = _prevmask(c)
        m["l0ab_prevmask"] = pm
        m["l1aa_prevmask"] = pm
        pm3 = np.zeros((P, 8), np.float32)
        if co != 0:
            pm3[:, c - 1] = 1.0
        m["l1aa_pm3"] = pm3
        bm = np.zeros((P, 8), np.float32)
        for r in range(8):
            if r // 4 == b:
                bm[:, r] = 1.0
        m["l1ab_bmask"] = bm
        vidx = np.zeros((P, 8), np.uint32)
        for j in range(2):
            for rr in range(4):
                vidx[:, j * 4 + rr] = (((4 * b + rr) * 128 + pcol) * (KD * T)
                                       + (2 * co + j) * T).astype(np.uint32)
        m["l1ab_vidx"] = vidx
        bidx = np.zeros((P, 8), np.uint32)
        for k in range(KD):
            bidx[:, k] = (((4 * b + k // 2) * 128 + pcol) * SEQ
                          + co * T).astype(np.uint32)
        m["l1ab_bidx"] = bidx
        maps.append(m)
    return maps


def kernel_unfused(**inputs):
    x = np.asarray(inputs["L0_attn_a_u"], np.float32)
    x = run_l0_attn_a(x, inputs)
    x = run_l0_ffn(x, inputs)
    x = run_l0_attn_b(x, inputs)
    x = run_l0_ffn(x, inputs)
    x = run_l1_attn_a(x, inputs)
    x = run_l1_ffn(x, inputs)
    x = run_l1_attn_b(x, inputs)
    x = run_l1_ffn(x, inputs)
    return x


def kernel_fused(**inputs):
    pr = _get_prog("fused", build_fused)
    maps = _fused_maps(inputs)
    res = run_bass_kernel_spmd(pr.nc, maps, core_ids=list(range(NCORES)))
    return _unshard([res.results[c]["y"] for c in range(NCORES)])
